# version 18
# speedup vs baseline: 1.1201x; 1.0147x over previous
from contextlib import ExitStack
import numpy as np
import ml_dtypes
import concourse.bass as bass
import concourse.mybir as mybir
from concourse.bass_utils import run_bass_kernel_spmd

F32 = mybir.dt.float32
BF16 = mybir.dt.bfloat16
AF = mybir.ActivationFunctionType
OP = mybir.AluOpType
AX = mybir.AxisListType

NCORE = 8
B, N, E = 4, 4096, 1024
TSH = 2048
H, DH, D1 = 8, 384, 3072
RW = 512
GAMMA, EPS_SRMS, EPS_LN = 0.99, 1e-8, 1e-5
M = 8192
NF1 = 65
NEXP, NGRP, EPG, TOPK = 16, 4, 4, 4
HS, HR = 2048, 256

ENGS = ("sync", "tensor", "vector", "scalar", "gpsimd")
NDMACH = 12
SEMS = (("tensor", "vector", "scalar", "gpsimd_e", "vb", "sb", "tb")
        + tuple(f"dmas_{i}" for i in range(NDMACH))
        + tuple(f"dmag_{i}" for i in range(NDMACH)))


def conv_consts():
    i_ = np.arange(64, dtype=np.float64)
    j_ = np.arange(64, dtype=np.float64)
    f1 = np.arange(NF1, dtype=np.float64)
    f2 = np.arange(64, dtype=np.float64)
    out = {}
    Dm = np.exp(-2j * np.pi * np.outer(i_, f1) / 128.0)
    out["c_d1mov"] = np.concatenate([Dm.real, Dm.imag], axis=1).astype(np.float32)
    ph = (-2 * np.pi / M) * np.einsum(
        "j,f->jf", j_, (128 * f2[None, :] + f1[:, None]).reshape(-1)
    ).reshape(64, NF1, 64)
    wf = np.stack([np.cos(ph), np.sin(ph), -np.sin(ph)], axis=2)
    out["c_wf"] = wf.reshape(64, NF1 * 3 * 64).astype(np.float32)
    phe = 2 * np.pi * np.outer(f2, j_) / 64.0
    out["c_e2"] = np.concatenate(
        [np.cos(phe), np.sin(phe), -np.sin(phe)], axis=1).astype(np.float32)
    pht = 2 * np.pi * np.outer(f1, j_) / M
    out["c_tinv"] = np.concatenate([np.cos(pht), np.sin(pht)], axis=1).astype(np.float32)
    w = np.full(NF1, 2.0); w[0] = 1.0; w[64] = 1.0
    ang = 2 * np.pi * np.outer(f1, i_) / 128.0
    Crs = (w[:, None] / M * np.cos(ang))
    Cis = (-w[:, None] / M * np.sin(ang))
    out["c_cstk"] = np.concatenate([Crs[0:64], Crs[64:65], Cis[1:64]],
                                   axis=0).astype(np.float32)
    out["c_ramp"] = np.arange(N, dtype=np.float64)[None, :].astype(np.float32)
    out["c_decay"] = (GAMMA ** np.arange(N, dtype=np.float64))[None, :].astype(np.float32)
    out["c_ones"] = np.ones((128, 128), np.float32)
    out["c_ident"] = np.eye(128, dtype=np.float32)
    sel = np.zeros((NEXP, NEXP * 128), np.float32)
    for e in range(NEXP):
        sel[e, e * 128:(e + 1) * 128] = 1.0
    out["c_sel"] = sel
    return out


class EV:

    def __init__(self, active, engine, sems):
        self.active = active
        self.eng = engine
        self.sems = sems
        self.cnt = {k: 0 for k in sems}
        self.events = {}
        self._chanidx = {}
        self._chanlast = {}

    def do(self, engine, emit=None, sig=None, sem=None, step=None, waits=()):
        for w in waits:
            if w is None:
                continue
            if engine == self.active:
                s, v = self.events[w]
                self.eng.wait_ge(self.sems[s], v)
        last = None
        if engine == self.active and emit is not None:
            last = emit(self.eng)
        if sig is not None:
            semn = sem if sem is not None else engine
            stp = step if step is not None else (16 if semn.startswith("dma") else 1)
            self.cnt[semn] += stp
            self.events[sig] = (semn, self.cnt[semn])
            if engine == self.active:
                assert last is not None, f"signal {sig} without instruction"
                last.then_inc(self.sems[semn], stp)
        return last

    def alias(self, new, old):
        self.events[new] = self.events[old]

    def dma(self, engine, chan, emit=None, sig=None, waits=()):
        q = "dmas" if engine in ("sync", "scalar") else "dmag"
        key = (q, chan)
        if key not in self._chanidx:
            self._chanidx[key] = len([k for k in self._chanidx if k[0] == q])
        idx = self._chanidx[key]
        assert idx < NDMACH, f"too many DMA channels on {q}"
        semn = f"{q}_{idx}"
        wl = list(waits)
        if key in self._chanlast:
            wl.append(self._chanlast[key])
        nm = sig or f"_dma_{q}_{chan}_{self.cnt[semn] // 16}"
        self.do(engine, emit, sig=nm, sem=semn, step=16, waits=tuple(wl))
        self._chanlast[key] = nm
        return nm


def build_program(body):
    nc = bass.Bass("TRN2", target_bir_lowering=False)
    ctx = ExitStack()
    d = {}
    body(nc, None, d, ctx, True)
    sems = {s: ctx.enter_context(nc.semaphore(s)) for s in SEMS}
    with nc.Block() as block:
        for engname in ENGS:
            def section(eng, engname=engname):
                ev = EV(engname, eng, sems)
                body(nc, ev, d, ctx, False)
            getattr(block, engname)(section)
    ctx.close()
    return nc


def sbuf_decl(nc, ctx, d, items):
    for nm, shp, dt, *rest in items:
        kw = rest[0] if rest else {}
        d[nm] = ctx.enter_context(nc.sbuf_tensor(nm, shp, dt, **kw))


def ddecl(nc, d, key, name, shape, dt, kind):
    if key not in d:
        d[key] = nc.dram_tensor(name, shape, dt, kind=kind)


def body_L1(nc, ev, d, ctx, declare):
    nt = TSH // 128
    if declare:
        ddecl(nc, d, "x", "x", [TSH, E], F32, "ExternalInput")
        ddecl(nc, d, "xn", "xn", [TSH, E], BF16, "ExternalOutput")
        sbuf_decl(nc, ctx, d, [(f"xt{i}", [128, E], F32) for i in range(nt)])
        sbuf_decl(nc, ctx, d, [(f"xnb{i}", [128, E], BF16) for i in range(2)])
        sbuf_decl(nc, ctx, d, [("sq", [128, E], F32)])
        sbuf_decl(nc, ctx, d, [(nm, [128, nt], F32) for nm in
                               ["rs", "rs2", "mcol", "vcol", "stdc", "rstd", "nmr"]])
        return
    for i in range(nt):
        ev.dma("sync", "x", lambda e, i=i: e.dma_start(
            out=d[f"xt{i}"][:, :], in_=d["x"].ap()[128 * i:128 * (i + 1), :]),
            sig=f"ld{i}")
    for i in range(nt):
        ev.do("vector", lambda e, i=i: e.reduce_sum(
            d["rs"][:, i:i + 1], d[f"xt{i}"][:, :], axis=AX.X), waits=(f"ld{i}",))
        ev.do("vector", lambda e, i=i: e.tensor_tensor(
            d["sq"][:, :], d[f"xt{i}"][:, :], d[f"xt{i}"][:, :], OP.mult),
            sig=f"sq{i}", sem="vector")
        ev.do("vector", lambda e, i=i: e.reduce_sum(
            d["rs2"][:, i:i + 1], d["sq"][:, :], axis=AX.X),
            waits=(f"sq{i}",), sig=f"red{i}", sem="vector")
    ev.do("vector", lambda e: e.tensor_scalar_mul(d["mcol"][:, :], d["rs"][:, :], 1.0 / E),
          sig="m", sem="vector", waits=(f"red{nt-1}",))
    ev.do("vector", lambda e: e.tensor_scalar_mul(d["rs2"][:, :], d["rs2"][:, :], 1.0 / E),
          sig="e2", sem="vector")
    ev.do("vector", lambda e: e.tensor_tensor(d["vcol"][:, :], d["mcol"][:, :], d["mcol"][:, :], OP.mult),
          sig="m2", sem="vector", waits=("m",))
    ev.do("vector", lambda e: e.tensor_tensor(d["vcol"][:, :], d["rs2"][:, :], d["vcol"][:, :], OP.subtract),
          sig="var", sem="vector", waits=("m2", "e2"))
    ev.do("vector", lambda e: e.tensor_scalar_add(d["vcol"][:, :], d["vcol"][:, :], EPS_LN),
          sig="vare", sem="vector", waits=("var",))
    ev.do("scalar", lambda e: e.activation(d["stdc"][:, :], d["vcol"][:, :], AF.Sqrt),
          sig="std", sem="scalar", waits=("vare",))
    ev.do("vector", lambda e: e.reciprocal(d["rstd"][:, :], d["stdc"][:, :]),
          sig="rstd", sem="vector", waits=("std",))
    ev.do("vector", lambda e: e.tensor_tensor(d["nmr"][:, :], d["mcol"][:, :], d["rstd"][:, :], OP.mult),
          sig="nmr0", sem="vector", waits=("rstd",))
    ev.do("vector", lambda e: e.tensor_scalar_mul(d["nmr"][:, :], d["nmr"][:, :], -1.0),
          sig="nmr", sem="vector", waits=("nmr0",))
    for i in range(nt):
        w = ("nmr",) if i < 2 else ("nmr", f"st{i-2}")
        ev.do("vector", lambda e, i=i: e.tensor_scalar(
            d[f"xnb{i % 2}"][:, :], d[f"xt{i}"][:, :],
            d["rstd"][:, i:i + 1], d["nmr"][:, i:i + 1], OP.mult, OP.add),
            sig=f"app{i}", sem="vector", waits=w)
        ev.dma("gpsimd", "xn", lambda e, i=i: e.dma_start(
            out=d["xn"].ap()[128 * i:128 * (i + 1), :], in_=d[f"xnb{i % 2}"][:, :]),
            sig=f"st{i}", waits=(f"app{i}",))
    ev.do("gpsimd", waits=(f"st{nt-1}",))


def body_L2(nc, ev, d, ctx, declare):
    if declare:
        ddecl(nc, d, "xnag", "xnag", [B * N, E], BF16, "ExternalInput")
        d["uw"] = nc.dram_tensor("uw", [E, DH], F32, kind="ExternalInput")
        d["vw"] = nc.dram_tensor("vw", [E, DH], F32, kind="ExternalInput")
        d["ub"] = nc.dram_tensor("ub", [DH, 1], F32, kind="ExternalInput")
        d["vb"] = nc.dram_tensor("vb", [DH, 1], F32, kind="ExternalInput")
        ddecl(nc, d, "vsig", "vsig", [B, 3, 128, M], BF16, "ExternalOutput")
        ddecl(nc, d, "usig", "usig", [B, 3, 128, M], BF16, "ExternalOutput")
        sbuf_decl(nc, ctx, d, [
            ("uwf", [128, 8 * DH], F32), ("uwb", [128, 8 * DH], BF16),
            ("vwf", [128, 8 * DH], F32), ("vwb", [128, 8 * DH], BF16),
            ("ubf", [128, 3], F32), ("vbf", [128, 3], F32)])
        sbuf_decl(nc, ctx, d, [(f"xk{q}_{i}", [128, 512], BF16, dict(is_transpose_dest=True))
                               for q in range(8) for i in range(2)])
        sbuf_decl(nc, ctx, d, [(f"sig{s}{oc}", [128, M], BF16)
                               for s in ("v", "u") for oc in range(3)])
        d["psA"] = ctx.enter_context(nc.psum_tensor("psA", [128, 512], F32))
        d["psB"] = ctx.enter_context(nc.psum_tensor("psB", [128, 512], F32))
        return

    d["_unit"] = [0]
    for nm, src in [("uwf", d["uw"]), ("vwf", d["vw"])]:
        ev.dma("sync", "wl", lambda e, nm=nm, src=src: e.dma_start(
            out=d[nm][:, :].rearrange("p (k c) -> p k c", k=8),
            in_=src.ap().rearrange("(k p) c -> p k c", p=128)),
            sig=f"ld_{nm}")
    ev.dma("sync", "wl", lambda e: e.dma_start(out=d["ubf"][:, :].rearrange("p (o x) -> p o x", x=1),
           in_=d["ub"].ap().rearrange("(o p) x -> p o x", p=128),
           allow_slow_non_contiguous=True), sig="ld_ub")
    ev.dma("sync", "wl", lambda e: e.dma_start(out=d["vbf"][:, :].rearrange("p (o x) -> p o x", x=1),
           in_=d["vb"].ap().rearrange("(o p) x -> p o x", p=128),
           allow_slow_non_contiguous=True), sig="ld_vb")
    ev.do("vector", lambda e: e.tensor_copy(d["uwb"][:, :], d["uwf"][:, :]),
          sig="cst_uw", sem="vector", waits=("ld_uwf",))
    ev.do("vector", lambda e: e.tensor_copy(d["vwb"][:, :], d["vwf"][:, :]),
          sig="cst_vw", sem="vector", waits=("ld_vwf",))
    last = None
    for s in ("v", "u"):
        for oc in range(3):
            last = ev.do("vector", lambda e, s=s, oc=oc: e.memset(
                d[f"sig{s}{oc}"][:, :].rearrange("p (j two i) -> p j two i", two=2, i=64)[:, :, 1, :], 0.0),
                sig=f"pad{s}{oc}", sem="vector")

    for b in range(B):
        for tt in range(8):
            buf = tt % 2
            for q in range(8):
                wl = []
                if b > 0 or tt > 1:
                    pb, ptt = (b, tt - 2) if tt > 1 else (b - 1, 6 + tt)
                    wl = [f"mmdone{pb}_{ptt}"]
                ev.dma("sync", f"xk{q}",
                       lambda e, b=b, tt=tt, q=q, buf=buf: e.dma_start(
                    out=d[f"xk{q}_{buf}"][:, :],
                    in_=d["xnag"].ap()[N * b + 512 * tt:N * b + 512 * (tt + 1),
                                       128 * q:128 * (q + 1)], transpose=True),
                    sig=f"xl{b}_{tt}_{q}", waits=tuple(wl))
            for si, (s, W, bias) in enumerate((("v", "vwb", "vbf"), ("u", "uwb", "ubf"))):
                for oc in range(3):
                    un = d["_unit"]
                    n = un[0]; un[0] += 1
                    ps = d["psA"] if n % 2 == 0 else d["psB"]
                    def mm(e, W=W, oc=oc, buf=buf, ps=ps):
                        lastl = None
                        for q in range(8):
                            lhs = d[W][:, q * DH + oc * 128:q * DH + oc * 128 + 128]
                            lastl = e.matmul(ps[:, :], lhs, d[f"xk{q}_{buf}"][:, :],
                                             start=(q == 0), stop=(q == 7))
                        return lastl
                    wl = [f"xl{b}_{tt}_3", f"xl{b}_{tt}_7", "cst_uw", "cst_vw"]
                    if n >= 2:
                        wl.append(f"uev{n - 2}")
                    ev.do("tensor", mm, sig=f"umm{n}", sem="tensor", waits=tuple(wl))
                    def evac(e, s=s, oc=oc, tt=tt, ps=ps, bias=bias):
                        dst = d[f"sig{s}{oc}"][:, :].rearrange(
                            "p (j i) -> p j i", i=128)[:, :, tt * 8:tt * 8 + 8]
                        src = ps[:, :].rearrange("p (i j) -> p i j", i=8)
                        return e.activation(dst.rearrange("p j i -> p i j"), src,
                                            AF.Silu, bias=d[bias][:, oc:oc + 1], scale=1.0)
                    ev.do("scalar", evac, sig=f"uev{n}", sem="scalar",
                          waits=(f"umm{n}", f"pad{s}{oc}",
                                 None if b == 0 else f"sw{b-1}_{s}{oc}"))
                    ev.alias(f"ev{b}_{tt}_{si * 3 + oc}", f"uev{n}")
            ev.alias(f"mmdone{b}_{tt}", f"umm{un[0] - 1}")
        for s in ("v", "u"):
            for oc in range(3):
                tgt = d["vsig"] if s == "v" else d["usig"]
                ev.dma("gpsimd", f"sw{s}{oc}", lambda e, tgt=tgt, b=b, oc=oc, s=s: e.dma_start(
                    out=tgt.ap()[b, oc, :, :], in_=d[f"sig{s}{oc}"][:, :]),
                    sig=f"sw{b}_{s}{oc}",
                    waits=(f"ev{b}_7_{(0 if s == 'v' else 1) * 3 + oc}",))
    ev.do("gpsimd", waits=tuple(f"sw{B-1}_{s}{oc}" for s in ("v", "u") for oc in range(3)))


def body_L2b(nc, ev, d, ctx, declare):
    if declare:
        d["posw"] = nc.dram_tensor("posw", [1, RW], F32, kind="ExternalInput")
        d["posb"] = nc.dram_tensor("posb", [RW, 1], F32, kind="ExternalInput")
        for l in range(3):
            d[f"lw{l}"] = nc.dram_tensor(f"lw{l}", [RW, RW], F32, kind="ExternalInput")
            d[f"lb{l}"] = nc.dram_tensor(f"lb{l}", [RW, 1], F32, kind="ExternalInput")
        d["outw"] = nc.dram_tensor("outw", [RW, DH], F32, kind="ExternalInput")
        d["outb"] = nc.dram_tensor("outb", [DH, 1], F32, kind="ExternalInput")
        d["cramp"] = nc.dram_tensor("c_ramp", [1, N], F32, kind="ExternalInput")
        d["cdecay"] = nc.dram_tensor("c_decay", [1, N], F32, kind="ExternalInput")
        ddecl(nc, d, "cones", "c_ones", [128, 128], F32, "ExternalInput")
        ddecl(nc, d, "asig", "asig", [3, 128, M], BF16, "ExternalOutput")
        sbuf_decl(nc, ctx, d, [
            ("poswf", [1, RW], F32), ("posbf", [128, 4], F32),
            ("rampf", [1, N], F32), ("decayf", [1, N], F32),
            ("decayt", [128, N], BF16), ("onesf", [128, 128], F32),
            ("onesb", [128, 1], BF16),
            ("hcur", [128, 4 * N], BF16), ("hnxt", [128, 4 * N], BF16),
            ("hsq", [128, 4 * N], BF16),
            ("s2sl", [1, 512], F32), ("rmssl", [1, 512], F32), ("rinvsl", [1, 512], F32),
            ("rbt", [128, N], BF16), ("ctmp", [128, 512], BF16),
            ("ast", [128, M], BF16)])
        for l in range(3):
            sbuf_decl(nc, ctx, d, [(f"lwb{l}", [128, 4 * RW], BF16),
                                   (f"lbf{l}", [128, 4], F32)])
        sbuf_decl(nc, ctx, d, [("wstage", [128, 4 * RW], F32),
                               ("outwb", [128, 4 * DH], BF16),
                               ("outbf", [128, 3], F32)])
        d["psA"] = ctx.enter_context(nc.psum_tensor("psA", [128, 512], F32))
        d["psB"] = ctx.enter_context(nc.psum_tensor("psB", [128, 512], F32))
        d["psD"] = ctx.enter_context(nc.psum_tensor("psD", [128, 512], F32))
        d["psE"] = ctx.enter_context(nc.psum_tensor("psE", [1, 512], F32))
        return

    ev.dma("sync", "wl", lambda e: e.dma_start(out=d["poswf"][:, :], in_=d["posw"].ap()),
           sig="ld_pw")
    ev.dma("sync", "wl", lambda e: e.dma_start(
        out=d["posbf"][:, :].rearrange("p (o x) -> p o x", x=1),
        in_=d["posb"].ap().rearrange("(o p) x -> p o x", p=128),
        allow_slow_non_contiguous=True), sig="ld_pb")
    ev.dma("sync", "wl", lambda e: e.dma_start(out=d["rampf"][:, :], in_=d["cramp"].ap()),
           sig="ld_rmp")
    ev.dma("sync", "wl", lambda e: e.dma_start(out=d["decayf"][:, :], in_=d["cdecay"].ap()),
           sig="ld_dc")
    ev.dma("sync", "wl", lambda e: e.dma_start(out=d["onesf"][:, :], in_=d["cones"].ap()),
           sig="ld_on")
    for l in range(3):
        ev.dma("sync", "wl", lambda e, l=l: e.dma_start(
            out=d["wstage"][:, :].rearrange("p (k c) -> p k c", k=4),
            in_=d[f"lw{l}"].ap().rearrange("(k p) c -> p k c", p=128)),
            sig=f"ld_lw{l}",
            waits=(f"cst_lw{l-1}",) if l > 0 else ())
        ev.dma("sync", "wl", lambda e, l=l: e.dma_start(
            out=d[f"lbf{l}"][:, :].rearrange("p (o x) -> p o x", x=1),
            in_=d[f"lb{l}"].ap().rearrange("(o p) x -> p o x", p=128),
            allow_slow_non_contiguous=True), sig=f"ld_lb{l}")
        ev.do("vector", lambda e, l=l: e.tensor_copy(d[f"lwb{l}"][:, :], d["wstage"][:, :]),
              sig=f"cst_lw{l}", sem="vector", waits=(f"ld_lw{l}",))
    ev.dma("sync", "wl", lambda e: e.dma_start(
        out=d["wstage"][:, 0:4 * DH].rearrange("p (k c) -> p k c", k=4),
        in_=d["outw"].ap().rearrange("(k p) c -> p k c", p=128)),
        sig="ld_ow", waits=("cst_lw2",))
    ev.dma("sync", "wl", lambda e: e.dma_start(
        out=d["outbf"][:, :].rearrange("p (o x) -> p o x", x=1),
        in_=d["outb"].ap().rearrange("(o p) x -> p o x", p=128),
        allow_slow_non_contiguous=True), sig="ld_ob")
    ev.do("vector", lambda e: e.tensor_copy(d["outwb"][:, :], d["wstage"][:, 0:4 * DH]),
          sig="cst_ow", sem="vector", waits=("ld_ow",))
    ev.do("vector", lambda e: e.memset(
        d["ast"][:, :].rearrange("p (j two i) -> p j two i", two=2, i=64)[:, :, 1, :], 0.0),
        sig="pad_ast", sem="vector")
    ev.do("vector", lambda e: e.tensor_copy(d["onesb"][:, :], d["onesf"][:, 0:1]),
          sig="cst_on", sem="vector", waits=("ld_on",))

    for t in range(8):
        ev.do("tensor", lambda e, t=t: e.matmul(
            d["psD"][:, :], d["onesf"][0:1, :], d["decayf"][0:1, 512 * t:512 * (t + 1)],
            start=True, stop=True), sig=f"dcmm{t}", sem="tensor",
            waits=("ld_on", "ld_dc") if t == 0 else (f"dcev{t-1}",))
        ev.do("vector", lambda e, t=t: e.tensor_copy(
            d["decayt"][:, 512 * t:512 * (t + 1)], d["psD"][:, :]),
            sig=f"dcev{t}", sem="vector", waits=(f"dcmm{t}",))

    un = [0]
    for c4 in range(4):
        for tt in range(8):
            n = un[0]; un[0] += 1
            ps = d["psA"] if n % 2 == 0 else d["psB"]
            wl = ["ld_pw", "ld_rmp"]
            if n >= 2:
                wl.append(f"h0ev{n-2}")
            ev.do("tensor", lambda e, c4=c4, tt=tt, ps=ps: e.matmul(
                ps[:, :], d["poswf"][0:1, 128 * c4:128 * (c4 + 1)],
                d["rampf"][0:1, 512 * tt:512 * (tt + 1)], start=True, stop=True),
                sig=f"h0mm{n}", sem="tensor", waits=tuple(wl))
            ev.do("vector", lambda e, c4=c4, tt=tt, ps=ps: e.tensor_scalar(
                d["hcur"][:, :].rearrange("p (k t) -> p k t", k=4)[:, c4, 512 * tt:512 * (tt + 1)],
                ps[:, :], d["posbf"][:, c4:c4 + 1], None, OP.add),
                sig=f"h0ev{n}", sem="vector", waits=(f"h0mm{n}", "ld_pb"))
    ev.alias("hlay-1", f"h0ev{un[0]-1}")

    for l in range(4):
        hin = d["hcur"] if l % 2 == 0 else d["hnxt"]
        hout = d["hnxt"] if l % 2 == 0 else d["hcur"]
        ev.do("vector", lambda e, hin=hin: e.tensor_tensor(
            d["hsq"][:, :], hin[:, :], hin[:, :], OP.mult),
            sig=f"sq{l}", sem="vector", waits=(f"hlay{l-1}",))
        for tt in range(8):
            def ssum(e, tt=tt):
                lastl = None
                for k in range(4):
                    lastl = e.matmul(
                        d["psE"][:, :], d["onesb"][:, :],
                        d["hsq"][:, :].rearrange("p (k t) -> p k t", k=4)[:, k, 512 * tt:512 * (tt + 1)],
                        start=(k == 0), stop=(k == 3))
                return lastl
            ev.do("tensor", ssum, sig=f"ss{l}_{tt}", sem="tensor",
                  waits=(f"sq{l}", "cst_on") if tt == 0 else (f"ssev{l}_{tt-1}",))
            ev.do("vector", lambda e: e.tensor_copy(d["s2sl"][:, :], d["psE"][:, :]),
                  sig=f"ssev{l}_{tt}", sem="vector",
                  waits=(f"ss{l}_{tt}",) + ((f"rms{l}_{tt-1}",) if tt > 0 else ()))
            ev.do("scalar", lambda e: e.activation(d["rmssl"][:, :], d["s2sl"][:, :],
                                                   AF.Sqrt, scale=1.0 / RW),
                  sig=f"rms{l}_{tt}", sem="scalar",
                  waits=(f"ssev{l}_{tt}",) + ((f"rinv{l}_{tt-1}",) if tt > 0 else ()))
            ev.do("vector", lambda e: e.tensor_scalar_add(d["rmssl"][:, :], d["rmssl"][:, :], EPS_SRMS),
                  sig=f"rmse{l}_{tt}", sem="vector", waits=(f"rms{l}_{tt}",))
            ev.do("vector", lambda e: e.reciprocal(d["rinvsl"][:, :], d["rmssl"][:, :]),
                  sig=f"rinv{l}_{tt}", sem="vector",
                  waits=(f"rmse{l}_{tt}",) + ((f"rb{l}_{tt-1}",) if tt > 0 else ()))
            ev.do("tensor", lambda e, tt=tt: e.matmul(
                d["psD"][:, :], d["onesf"][0:1, :], d["rinvsl"][0:1, :],
                start=True, stop=True), sig=f"rb{l}_{tt}", sem="tensor",
                waits=(f"rinv{l}_{tt}",) + ((f"rbev{l}_{tt-1}",) if tt > 0 else ()))
            ev.do("vector", lambda e, tt=tt: e.tensor_copy(
                d["rbt"][:, 512 * tt:512 * (tt + 1)], d["psD"][:, :]),
                sig=f"rbev{l}_{tt}", sem="vector", waits=(f"rb{l}_{tt}",))
        ev.do("vector", lambda e, hin=hin: e.tensor_tensor(
            d["hsq"][:, :].rearrange("p (k t) -> p k t", k=4), hin[:, :].rearrange("p (k t) -> p k t", k=4),
            d["rbt"][:, :].rearrange("p (o t) -> p o t", o=1).broadcast_to([128, 4, N]),
            OP.mult), sig=f"hm{l}", sem="vector", waits=(f"rbev{l}_7", f"ssev{l}_7"))
        ev.do("vector", lambda e: e.tensor_scalar_max(d["hsq"][:, :], d["hsq"][:, :], 0.0),
              sig=f"hr{l}", sem="vector", waits=(f"hm{l}",))
        nco = 4 if l < 3 else 3
        ncols = RW if l < 3 else DH
        Wb = d[f"lwb{l}"] if l < 3 else d["outwb"]
        un2 = [0]
        for oc in range(nco):
            for tt in range(8):
                n = un2[0]; un2[0] += 1
                ps = d["psA"] if n % 2 == 0 else d["psB"]
                def mmh(e, Wb=Wb, oc=oc, tt=tt, ps=ps, ncols=ncols):
                    lastl = None
                    for k in range(4):
                        lhs = Wb[:, k * ncols + oc * 128:k * ncols + oc * 128 + 128]
                        mov = d["hsq"][:, :].rearrange("p (k t) -> p k t", k=4)[:, k, 512 * tt:512 * (tt + 1)]
                        lastl = e.matmul(ps[:, :], lhs, mov, start=(k == 0), stop=(k == 3))
                    return lastl
                wl = [f"hr{l}", "cst_ow" if l == 3 else f"cst_lw{l}"]
                if n >= 2:
                    wl.append(f"lev{l}_{n-2}")
                ev.do("tensor", mmh, sig=f"lmm{l}_{n}", sem="tensor", waits=tuple(wl))
                if l < 3:
                    ev.do("vector", lambda e, hout=hout, oc=oc, tt=tt, ps=ps, l=l: e.tensor_scalar(
                        hout[:, :].rearrange("p (k t) -> p k t", k=4)[:, oc, 512 * tt:512 * (tt + 1)],
                        ps[:, :], d[f"lbf{l}"][:, oc:oc + 1], None, OP.add),
                        sig=f"lev{l}_{n}", sem="vector", waits=(f"lmm{l}_{n}", f"ld_lb{l}"))
                else:
                    ev.do("vector", lambda e, oc=oc, tt=tt, ps=ps: e.tensor_scalar(
                        d["ctmp"][:, :], ps[:, :], d["outbf"][:, oc:oc + 1], None, OP.add),
                        sig=f"cb{n}", sem="vector", waits=(f"lmm{l}_{n}", "ld_ob",
                                                           f"dmul{n-1}" if n > 0 else None))
                    def dmul(e, oc=oc, tt=tt):
                        dst = d["ast"][:, :].rearrange("p (j i) -> p j i", i=128)[:, :, tt * 8:tt * 8 + 8]
                        return e.tensor_tensor(
                            dst.rearrange("p j i -> p i j"),
                            d["ctmp"][:, :].rearrange("p (i j) -> p i j", i=8),
                            d["decayt"][:, 512 * tt:512 * (tt + 1)].rearrange("p (i j) -> p i j", i=8),
                            OP.mult)
                    ev.do("vector", dmul, sig=f"dmul{n}", sem="vector",
                          waits=(f"cb{n}", "dcev7", "pad_ast",
                                 f"awr{oc-1}" if (tt == 0 and oc > 0) else None))
                    ev.alias(f"lev{l}_{n}", f"dmul{n}")
            if l == 3:
                ev.dma("gpsimd", "aw", lambda e, oc=oc: e.dma_start(
                    out=d["asig"].ap()[oc, :, :], in_=d["ast"][:, :]),
                    sig=f"awr{oc}", waits=(f"dmul{oc * 8 + 7}",))
        if l < 3:
            ev.alias(f"hlay{l}", f"lev{l}_{un2[0]-1}")
    ev.do("gpsimd", waits=("awr0", "awr1", "awr2"))


def body_L3(nc, ev, d, ctx, declare):
    CC = 128
    if declare:
        ddecl(nc, d, "vsig", "vsig", [B, 3, 128, M], BF16, "ExternalInput")
        ddecl(nc, d, "usig", "usig", [B, 3, 128, M], BF16, "ExternalInput")
        ddecl(nc, d, "asig", "asig", [3, 128, M], BF16, "ExternalInput")
        for knm, shp in [("c_d1mov", [64, 130]), ("c_wf", [64, NF1 * 3 * 64]),
                         ("c_e2", [64, 192]), ("c_tinv", [NF1, 128]),
                         ("c_cstk", [128, 64])]:
            d[knm] = nc.dram_tensor(knm, shp, F32, kind="ExternalInput")
        ddecl(nc, d, "w", "w", [B, 3, N, CC], BF16, "ExternalOutput")
        sbuf_decl(nc, ctx, d, [
            ("Xv", [128, CC * 64], BF16, dict(is_transpose_dest=True)),
            ("Xu", [128, CC * 64], BF16, dict(is_transpose_dest=True)),
            ("sA", [NF1, CC * 130], BF16),
            ("sB", [128, CC * 130], BF16),
            ("Ah", [64, NF1 * 2 * CC], BF16),
            ("Pd", [64, NF1 * 2 * CC], BF16),
            ("t1", [NF1, NF1 * CC], BF16),
            ("kd1", [64, 130], BF16), ("kwf", [64, NF1 * 3 * 64], BF16),
            ("ke2", [64, 192], BF16), ("ktinv", [NF1, 128], BF16),
            ("kcstk", [128, 64], BF16)])
        d["bk"] = [ctx.enter_context(nc.psum_tensor(f"bk{i}", [128, 512], F32))
                   for i in range(8)]
        return

    for knm, dst in [("c_d1mov", "kd1"), ("c_wf", "kwf"), ("c_e2", "ke2"),
                     ("c_tinv", "ktinv"), ("c_cstk", "kcstk")]:
        ev.dma("gpsimd", "kc", lambda e, knm=knm, dst=dst: e.dma_start(
            out=d[dst][:, :], in_=d[knm].ap()), sig=f"ld_{dst}")
    KW = ("ld_kd1", "ld_kwf", "ld_ke2", "ld_ktinv", "ld_kcstk")

    bk = d["bk"]
    d["_Vh"] = None
    S1 = d["sA"][0:64, :]
    Qe = d["sA"][0:65, 0:CC * 2 * 64]
    Vh = d["sB"][0:64, 0:NF1 * 2 * CC]
    Rst = d["sB"][0:128, 0:CC * 64]
    Wt = d["sB"][0:64, CC * 64:CC * 64 + 64 * CC]

    def emit_f1(pfx, X, waits_extra):
        for wv in range(16):
            def f1(e, wv=wv, X=X):
                lastl = None
                for s in range(4):
                    c = wv * 8 + 2 * s
                    lastl = e.matmul(bk[s][0:128, 0:130], X[0:64, c * 64:(c + 2) * 64],
                                     d["kd1"][:, :], start=True, stop=True)
                return lastl
            wl = list(waits_extra) + ["ld_kd1"]
            if wv > 0:
                wl.append(f"{pfx}s1e{wv-1}")
            ev.do("tensor", f1, sig=f"{pfx}f1m{wv}", sem="tensor", waits=tuple(wl))
            def s1e(e, wv=wv):
                lastl = None
                for s in range(4):
                    c = wv * 8 + 2 * s
                    e.copy(S1[:, c * 130:(c + 1) * 130], bk[s][0:64, 0:130])
                    lastl = e.copy(S1[:, (c + 1) * 130:(c + 2) * 130],
                                   bk[s][64:128, 0:130])
                return lastl
            ev.do("scalar", s1e, sig=f"{pfx}s1e{wv}", sem="scalar",
                  waits=(f"{pfx}f1m{wv}",))

    def emit_f2(pfx, Dst, first_evac_waits=()):
        S1v = S1[:, :].rearrange("p (c x) -> p c x", x=130)
        for g in range(17):
            lo, hi = g * 4, min((g + 1) * 4, NF1)
            def f2(e, lo=lo, hi=hi):
                lastl = None
                for f1 in range(lo, hi):
                    o = (f1 - lo) * CC
                    wr = d["kwf"][:, (f1 * 3 + 0) * 64:(f1 * 3 + 1) * 64]
                    wi = d["kwf"][:, (f1 * 3 + 1) * 64:(f1 * 3 + 2) * 64]
                    wni = d["kwf"][:, (f1 * 3 + 2) * 64:(f1 * 3 + 3) * 64]
                    s1r = S1v[:, :, 0 * 65 + f1]
                    s1i = S1v[:, :, 1 * 65 + f1]
                    e.matmul(bk[0][0:64, o:o + CC], wr, s1r, start=True, stop=False)
                    e.matmul(bk[1][0:64, o:o + CC], wr, s1i, start=True, stop=False)
                    e.matmul(bk[1][0:64, o:o + CC], wi, s1r, start=False, stop=True)
                    lastl = e.matmul(bk[0][0:64, o:o + CC], wni, s1i, start=False, stop=True)
                return lastl
            wl = [f"{pfx}s1e15", "ld_kwf"]
            if g > 0:
                wl.append(f"{pfx}f2e{g-1}")
            ev.do("tensor", f2, sig=f"{pfx}f2m{g}", sem="tensor", waits=tuple(wl))
            def f2e(e, lo=lo, hi=hi, Dst=Dst):
                n = hi - lo
                dv = Dst[:, :].rearrange("p (f pl c) -> p f pl c", f=NF1, pl=2)
                e.tensor_copy(dv[:, lo:hi, 0, :],
                              bk[0][0:64, 0:n * CC].rearrange("p (f c) -> p f c", c=CC))
                return e.tensor_copy(dv[:, lo:hi, 1, :],
                                     bk[1][0:64, 0:n * CC].rearrange("p (f c) -> p f c", c=CC))
            def f2e_s(e, lo=lo, hi=hi, Dst=Dst):
                n = hi - lo
                dv = Dst[:, :].rearrange("p (f pl c) -> p f pl c", f=NF1, pl=2)
                e.copy(dv[:, lo:hi, 0, :],
                       bk[0][0:64, 0:n * CC].rearrange("p (f c) -> p f c", c=CC))
                return e.copy(dv[:, lo:hi, 1, :],
                              bk[1][0:64, 0:n * CC].rearrange("p (f c) -> p f c", c=CC))
            few = (f"{pfx}f2m{g}",) + (tuple(first_evac_waits) if g == 0 else ())
            if g % 2 == 0:
                ev.do("vector", f2e, sig=f"{pfx}f2e{g}", sem="vector", waits=few)
            else:
                ev.do("scalar", f2e_s, sig=f"{pfx}f2e{g}", sem="scalar", waits=few)

    for oc in range(3):
        prevlast = None if oc == 0 else f"{oc-1}_{B-1}"
        ev.dma("sync", "xv", lambda e, oc=oc: e.dma_start(
            out=d["Xv"][:, :], in_=d["asig"].ap()[oc, :, :].rearrange("c (j i) -> (c j) i", i=128),
            transpose=True), sig=f"axl{oc}",
            waits=() if oc == 0 else (f"vf1_done{prevlast}",))
        emit_f1(f"a{oc}", d["Xv"],
                [f"axl{oc}"] + ([] if oc == 0 else [f"wcomb{prevlast}_15"]))
        ev.alias(f"af1_done{oc}", f"a{oc}f1m15")
        emit_f2(f"a{oc}", d["Ah"],
                first_evac_waits=() if oc == 0 else (f"prod{prevlast}", f"prodG{prevlast}"))
        for b in range(B):
            pb = f"{oc}_{b}"
            prev = None if (oc == 0 and b == 0) else (f"{oc}_{b-1}" if b > 0 else f"{oc-1}_{B-1}")
            xw = [f"af1_done{oc}"] if b == 0 else [f"vf1_done{prev}"]
            uw_ = [] if prev is None else [f"wcomb{prev}_15"]
            ev.dma("sync", "xv", lambda e, oc=oc, b=b: e.dma_start(
                out=d["Xv"][:, :], in_=d["vsig"].ap()[b, oc, :, :].rearrange("c (j i) -> (c j) i", i=128),
                transpose=True), sig=f"vxl{pb}", waits=tuple(xw))
            ev.dma("sync", "xu", lambda e, oc=oc, b=b: e.dma_start(
                out=d["Xu"][:, :], in_=d["usig"].ap()[b, oc, :, :].rearrange("c (j i) -> (c j) i", i=128),
                transpose=True), sig=f"uxl{pb}", waits=tuple(uw_))
            vw = [f"vxl{pb}", f"a{oc}f2e16"]
            if prev is not None:
                vw.append(f"itw{prev}")
                vw.append(f"itwG{prev}")
                vw.append(f"wcomb{prev}_15")
            emit_f1(f"v{pb}", d["Xv"], vw)
            ev.alias(f"vf1_done{pb}", f"v{pb}f1m15")
            fe = () if prev is None else (f"i2m{prev}_15", f"wwr{prev}")
            emit_f2(f"v{pb}", Vh, first_evac_waits=fe)
            def PV(w):
                return d[w][:, :].rearrange("p (f pl c) -> p f pl c", f=NF1, pl=2)
            def pstage(which, c0, c1):
                def fa(e):
                    V, A, P = PV("_Vh"), PV("Ah"), PV("Pd")
                    T1 = d["t1"][0:64, :].rearrange("p (f c) -> p f c", f=NF1)
                    if which == 0:
                        e.tensor_tensor(P[:, :, 0, c0:c1], V[:, :, 0, c0:c1],
                                        A[:, :, 0, c0:c1], OP.mult)
                        return e.tensor_tensor(T1[:, :, c0:c1], V[:, :, 1, c0:c1],
                                               A[:, :, 1, c0:c1], OP.mult)
                    if which == 1:
                        e.tensor_tensor(P[:, :, 0, c0:c1], P[:, :, 0, c0:c1],
                                        T1[:, :, c0:c1], OP.subtract)
                        return e.tensor_tensor(P[:, :, 1, c0:c1], V[:, :, 0, c0:c1],
                                               A[:, :, 1, c0:c1], OP.mult)
                    if which == 2:
                        return e.tensor_tensor(T1[:, :, c0:c1], V[:, :, 1, c0:c1],
                                               A[:, :, 0, c0:c1], OP.mult)
                    return e.tensor_tensor(P[:, :, 1, c0:c1], P[:, :, 1, c0:c1],
                                           T1[:, :, c0:c1], OP.add)
                return fa
            pw = [f"v{pb}f2e16", f"a{oc}f2e16"]
            if prev is not None:
                pw.append(f"i1m_last{prev}")
                pw.append(f"itw{prev}")
                pw.append(f"itwG{prev}")
            d["_Vh"] = Vh
            ev.do("vector", pstage(0, 0, CC), sig=f"prodA{pb}", sem="vector",
                  waits=tuple(pw))
            ev.do("vector", pstage(1, 0, CC), sig=f"prodB{pb}", sem="vector",
                  waits=(f"prodA{pb}",))
            ev.do("vector", pstage(2, 0, CC), sig=f"prodC{pb}", sem="vector",
                  waits=(f"prodB{pb}",))
            ev.do("vector", pstage(3, 0, CC), sig=f"prod{pb}", sem="vector",
                  waits=(f"prodC{pb}",))
            ev.alias(f"prodG{pb}", f"prod{pb}")
            Pv = d["Pd"][:, :].rearrange("p (f pl c) -> p f pl c", f=NF1, pl=2)
            for g in range(32):
                pair = ((bk[2][0:65, 0:256], bk[3][0:65, 0:256]),
                        (bk[4][0:65, 0:256], bk[5][0:65, 0:256]))[g % 2]
                def i1(e, g=g, pair=pair):
                    psr, psi = pair
                    e2r = d["ke2"][:, 0:64]
                    e2i = d["ke2"][:, 64:128]
                    e2ni = d["ke2"][:, 128:192]
                    lastl = None
                    for s in range(4):
                        c = g * 4 + s
                        pr = Pv[:, :, 0, c]
                        pi = Pv[:, :, 1, c]
                        qr = psr[:, s * 64:(s + 1) * 64]
                        qi = psi[:, s * 64:(s + 1) * 64]
                        e.matmul(qr, pr, e2r, start=True, stop=False)
                        e.matmul(qi, pr, e2i, start=True, stop=False)
                        e.matmul(qr, pi, e2ni, start=False, stop=True)
                        lastl = e.matmul(qi, pi, e2r, start=False, stop=True)
                    return lastl
                wl = [f"prod{pb}", f"prodG{pb}", "ld_ke2"]
                if g > 1:
                    wl.append(f"qev{pb}_{g-2}")
                if g == 0 and prev is not None:
                    wl.append(f"itw{prev}")
                    wl.append(f"itwG{prev}")
                ev.do("tensor", i1, sig=f"i1m{pb}_{g}", sem="tensor", waits=tuple(wl))
                def qev(e, g=g, pair=pair):
                    psr, psi = pair
                    dst = Qe[:, :].rearrange("p (c pl j) -> p c pl j", pl=2, j=64)
                    e.copy(dst[:, g * 4:(g + 1) * 4, 0, :],
                           psr[:, :].rearrange("p (c j) -> p c j", j=64))
                    return e.copy(dst[:, g * 4:(g + 1) * 4, 1, :],
                                  psi[:, :].rearrange("p (c j) -> p c j", j=64))
                ev.do("scalar", qev, sig=f"qev{pb}_{g}", sem="scalar",
                      waits=(f"i1m{pb}_{g}",))
            ev.alias(f"i1m_last{pb}", f"i1m{pb}_31")
            def qv():
                return Qe[:, :].rearrange("p (c pl j) -> p c pl j", pl=2, j=64)
            def tib(rows, plane, c0, c1):
                lo, hi = rows
                return d["ktinv"][lo:hi, 64 * plane:64 * plane + 64].rearrange(
                    "p (o j) -> p o j", o=1).broadcast_to([hi - lo, c1 - c0, 64])
            T1A = d["t1"][0:64, 0:CC * 64].rearrange("p (c j) -> p c j", j=64)
            T1N = d["t1"][64:65, 0:CC * 64].rearrange("p (c j) -> p c j", j=64)
            RrV = Rst[0:64, :].rearrange("p (c j) -> p c j", j=64)
            RiV = Rst[64:128, :].rearrange("p (c j) -> p c j", j=64)
            RnV = Rst[64:65, :].rearrange("p (c j) -> p c j", j=64)
            TB = d["Pd"][0:64, 0:CC * 64].rearrange("p (c j) -> p c j", j=64)
            def istage(which, c0, c1):
                def fa(e):
                    Q = qv()
                    if which == 0:
                        e.tensor_tensor(RrV[:, c0:c1], Q[0:64, c0:c1, 0, :],
                                        tib((0, 64), 0, c0, c1), OP.mult)
                        e.tensor_tensor(T1A[:, c0:c1], Q[0:64, c0:c1, 1, :],
                                        tib((0, 64), 1, c0, c1), OP.mult)
                        return e.tensor_tensor(TB[:, c0:c1], Q[0:64, c0:c1, 0, :],
                                               tib((0, 64), 1, c0, c1), OP.mult)
                    if which == 1:
                        return e.tensor_tensor(RrV[:, c0:c1], RrV[:, c0:c1],
                                               T1A[:, c0:c1], OP.subtract)
                    if which == 2:
                        return e.tensor_tensor(T1A[:, c0:c1], Q[0:64, c0:c1, 1, :],
                                               tib((0, 64), 0, c0, c1), OP.mult)
                    return e.tensor_tensor(RiV[:, c0:c1], TB[:, c0:c1],
                                           T1A[:, c0:c1], OP.add)
                return fa
            iw = [f"qev{pb}_31", "ld_ktinv"]
            if prev is not None:
                iw.append(f"i2m{prev}_15")
                iw.append(f"wwr{prev}")
            def nyq_a(e):
                return e.tensor_tensor(T1N, qv()[64:65, :, 1, :],
                                       tib((64, 65), 1, 0, CC), OP.mult)
            def nyq_e(e):
                return e.tensor_tensor(RnV, qv()[64:65, :, 0, :],
                                       tib((64, 65), 0, 0, CC), OP.mult)
            def nyq_f(e):
                return e.tensor_tensor(RnV, RnV, T1N, OP.subtract)
            ev.do("vector", istage(0, 0, CC), sig=f"itwA{pb}", sem="vector",
                  waits=tuple(iw))
            ev.do("vector", istage(1, 0, CC), sig=f"itwB{pb}", sem="vector",
                  waits=(f"itwA{pb}",))
            ev.do("vector", istage(2, 0, CC), sig=f"itwC{pb}", sem="vector",
                  waits=(f"itwB{pb}",))
            ev.do("vector", istage(3, 0, CC), sig=f"itwD{pb}", sem="vector",
                  waits=(f"itwC{pb}",))
            ev.do("vector", nyq_a, sig=f"nyqA{pb}", sem="vector",
                  waits=(f"itwD{pb}",))
            ev.do("vector", nyq_e, sig=f"nyqE{pb}", sem="vector",
                  waits=(f"nyqA{pb}",))
            ev.do("vector", nyq_f, sig=f"itw{pb}", sem="vector",
                  waits=(f"nyqE{pb}",))
            ev.alias(f"itwG{pb}", f"itw{pb}")
            for g in range(16):
                ps = bk[6 + g % 2][0:64, 0:512]
                def i2(e, g=g, ps=ps):
                    return e.matmul(ps, d["kcstk"][:, :],
                                    Rst[:, g * 512:(g + 1) * 512],
                                    start=True, stop=True)
                wl = [f"itw{pb}", f"itwG{pb}", "ld_kcstk"]
                if g > 1:
                    wl.append(f"wcomb{pb}_{g-2}")
                ev.do("tensor", i2, sig=f"i2m{pb}_{g}", sem="tensor", waits=tuple(wl))
                def wcomb(e, g=g, ps=ps):
                    dst = Wt[:, :].rearrange("p (j c) -> p c j", c=CC)[:, g * 8:(g + 1) * 8, :]
                    src = ps[:, :].rearrange("p (c j) -> p c j", j=64)
                    uop = d["Xu"][0:64, :].rearrange("p (c j) -> p c j", j=64)[:, g * 8:(g + 1) * 8, :]
                    return e.tensor_tensor(dst, src, uop, OP.mult)
                ev.do("vector", wcomb, sig=f"wcomb{pb}_{g}", sem="vector",
                      waits=(f"i2m{pb}_{g}", f"uxl{pb}") +
                            ((f"wwr{prev}",) if (g == 0 and prev is not None) else ()))
            ev.dma("gpsimd", "ww", lambda e, oc=oc, b=b: e.dma_start(
                out=d["w"].ap()[b, oc, :, :].rearrange("(i j) c -> i j c", i=64),
                in_=Wt[:, :].rearrange("p (j c) -> p j c", c=CC)),
                sig=f"wwr{pb}", waits=(f"wcomb{pb}_15",))
    ev.do("gpsimd", waits=tuple(f"wwr{oc}_{b}" for oc in range(3) for b in range(B)))


def body_L4a(nc, ev, d, ctx, declare):
    import os
    TRUNC = int(os.environ.get("L4A_TRUNC", "99"))
    NT = TSH // 512
    NCK = TSH // 128
    if declare:
        ddecl(nc, d, "xn", "xn", [TSH, E], BF16, "ExternalInput")
        d["rwT"] = nc.dram_tensor("rwT", [E, NEXP], F32, kind="ExternalInput")
        d["corr"] = nc.dram_tensor("corr", [NEXP, 1], F32, kind="ExternalInput")
        ddecl(nc, d, "gw", "gw", [NEXP, E, HR], F32, "ExternalInput")
        ddecl(nc, d, "uw", "uw", [NEXP, E, HR], F32, "ExternalInput")
        ddecl(nc, d, "dw", "dw", [NEXP, HR, E], F32, "ExternalInput")
        d["sgwT"] = nc.dram_tensor("sgwT", [E, HS], F32, kind="ExternalInput")
        d["suwT"] = nc.dram_tensor("suwT", [E, HS], F32, kind="ExternalInput")
        d["sdwT"] = nc.dram_tensor("sdwT", [HS, E], F32, kind="ExternalInput")
        ddecl(nc, d, "cident", "c_ident", [128, 128], F32, "ExternalInput")
        ddecl(nc, d, "cones", "c_ones", [128, 128], F32, "ExternalInput")
        d["csel"] = nc.dram_tensor("c_sel", [NEXP, NEXP * 128], F32, kind="ExternalInput")
        ddecl(nc, d, "gout", "gout", [E, TSH], BF16, "ExternalOutput")
        sbuf_decl(nc, ctx, d, [(f"xnf{q}", [128, TSH], BF16, dict(is_transpose_dest=True))
                               for q in range(8)])
        sbuf_decl(nc, ctx, d, [
            ("rwb", [128, 8 * NEXP], BF16), ("corrf", [NEXP, 1], F32),
            ("self_", [NEXP, NEXP * 128], F32),
            ("identf", [128, 128], F32), ("onesf", [128, 128], F32),
            ("scf", [NEXP, TSH], F32), ("sbf", [NEXP, TSH], F32),
            ("sctok", [128, NCK * NEXP], F32), ("sbtok", [128, NCK * NEXP], F32),
            ("ps6", [128, NCK * 6 * NGRP], F32),
            ("mA", [128, NCK * NGRP], F32), ("mB", [128, NCK * NGRP], F32),
            ("ggt", [128, NCK * NGRP * NGRP], F32),
            ("gmask", [128, NCK * NGRP], F32),
            ("emask", [128, NCK * NEXP], F32),
            ("msk", [128, NCK * NEXP], F32), ("msk2", [128, NCK * NEXP], F32),
            ("egt", [128, NCK * NEXP * NEXP], F32),
            ("erank", [128, NCK * NEXP], F32),
            ("wts", [128, NCK * NEXP], F32), ("wsum", [128, NCK], F32),
            ("cfm", [NEXP, TSH], F32),
            ("cbt", [128, NEXP * 512], BF16),
            ("gwe", [128, 8 * HR], BF16), ("uwe", [128, 8 * HR], BF16),
            ("dwe", [128, 2 * 512], BF16),
            ("sgw", [128, 8 * 512], BF16), ("suw", [128, 8 * 512], BF16),
            ("sdw", [128, 512], BF16),
            ("sgt", [128, 512], BF16), ("sut", [128, 512], BF16),
            ("hall", [128, NEXP * 2 * 512], BF16),
            ("shh", [128, 16 * 512], BF16),
            ("gst", [128, 512], BF16)])
        d["bk"] = [ctx.enter_context(nc.psum_tensor(f"bk{i}", [128, 512], F32))
                   for i in range(8)]
        return

    bk = d["bk"]
    for q in range(8):
        ev.dma("sync", "xn", lambda e, q=q: e.dma_start(
            out=d[f"xnf{q}"][:, :],
            in_=d["xn"].ap()[:, 128 * q:128 * (q + 1)], transpose=True),
            sig=f"xl{q}")
    XL = tuple(f"xl{q}" for q in range(8))
    ev.dma("gpsimd", "rw", lambda e: e.dma_start(
        out=d["rwb"][:, :].rearrange("p (k x) -> p k x", k=8),
        in_=d["rwT"].ap().rearrange("(k p) x -> p k x", p=128)), sig="ld_rw")
    ev.dma("sync", "kc", lambda e: e.dma_start(out=d["corrf"][:, :], in_=d["corr"].ap()),
           sig="ld_corr")
    ev.dma("sync", "kc", lambda e: e.dma_start(out=d["identf"][:, :], in_=d["cident"].ap()),
           sig="ld_id")
    ev.dma("sync", "kc", lambda e: e.dma_start(out=d["onesf"][:, :], in_=d["cones"].ap()),
           sig="ld_on")
    ev.dma("sync", "kc", lambda e: e.dma_start(out=d["self_"][:, :], in_=d["csel"].ap()),
           sig="ld_sel")

    for t4 in range(NT):
        def rmm(e, t4=t4):
            lastl = None
            for q in range(8):
                lastl = e.matmul(bk[6][0:NEXP, 0:512],
                                 d["rwb"][:, :].rearrange("p (k x) -> p k x", k=8)[:, q, :],
                                 d[f"xnf{q}"][:, 512 * t4:512 * (t4 + 1)],
                                 start=(q == 0), stop=(q == 7))
            return lastl
        ev.do("tensor", rmm, sig=f"rmm{t4}", sem="tensor",
              waits=XL + ("ld_rw",) + ((f"rsev{t4-1}",) if t4 > 0 else ()))
        ev.do("scalar", lambda e, t4=t4: e.activation(
            d["scf"][:, 512 * t4:512 * (t4 + 1)], bk[6][0:NEXP, 0:512], AF.Sigmoid),
            sig=f"rsev{t4}", sem="scalar", waits=(f"rmm{t4}",))
    ev.do("vector", lambda e: e.tensor_scalar(
        d["sbf"][:, :], d["scf"][:, :], d["corrf"][:, :], None, OP.add),
        sig="sbf", sem="vector", waits=("rsev3", "ld_corr"))

    def trunc_exit(lastev):
        for nm in ["gw", "uw", "dw", "sgwT", "suwT", "sdwT", "cident", "csel"]:
            if ("d_" + nm) not in ev.events:
                ev.dma("sync", "dummy", lambda e, nm=nm: e.dma_start(
                    out=d["identf"][0:1, 0:2], in_=d[nm].ap().rearrange(
                        *(("a b c -> (a b) c",) if nm in ("gw", "uw", "dw") else ("a b -> a b",))
                    )[0:1, 0:2]), sig="d_" + nm)
        ev.do("vector", lambda e: e.memset(d["gst"][:, :], 0.0), sig="z_gst", sem="vector",
              waits=(lastev,))
        for oc in range(8):
            for t4 in range(NT):
                ev.dma("gpsimd", "go", lambda e, oc=oc, t4=t4: e.dma_start(
                    out=d["gout"].ap()[128 * oc:128 * (oc + 1), 512 * t4:512 * (t4 + 1)],
                    in_=d["gst"][:, :]), sig=f"zw{oc}_{t4}", waits=("z_gst",))
        ev.do("gpsimd", waits=("d_csel", f"zw7_{NT-1}"))

    if TRUNC <= 1:
        trunc_exit("rsev3")
        return
    for src, dst, sig0 in (("scf", "sctok", "tsc"), ("sbf", "sbtok", "tsb")):
        for ck in range(NCK):
            ev.do("tensor", lambda e, src=src, ck=ck: e.transpose(
                bk[7][0:128, 0:NEXP], d[src][:, 128 * ck:128 * (ck + 1)],
                d["identf"][0:NEXP, 0:NEXP]),
                sig=f"{sig0}m{ck}", sem="tensor",
                waits=(("sbf", "ld_id") if (ck == 0 and src == "scf") else
                       ((f"{sig0}e{ck-1}",) if ck > 0 else ("tsce15",))))
            ev.do("vector", lambda e, dst=dst, ck=ck: e.tensor_copy(
                d[dst][:, NEXP * ck:NEXP * (ck + 1)], bk[7][0:128, 0:NEXP]),
                sig=f"{sig0}e{ck}", sem="vector", waits=(f"{sig0}m{ck}",))

    if TRUNC <= 2:
        trunc_exit("tsbe15")
        return
    def V4(nm):
        return d[nm][:, :].rearrange("p (c g) -> p c g", g=NGRP)
    def V16(nm):
        return d[nm][:, :].rearrange("p (c x) -> p c x", x=NEXP)
    sbg = d["sbtok"][:, :].rearrange("p (c g e) -> p c g e", g=NGRP, e=EPG)
    P6 = d["ps6"][:, :].rearrange("p (c s g) -> p c s g", s=6, g=NGRP)
    pairs = [(0, 1), (0, 2), (0, 3), (1, 2), (1, 3), (2, 3)]
    def step_ps(e):
        lastl = None
        for pi, (i1, i2) in enumerate(pairs):
            lastl = e.tensor_tensor(P6[:, :, pi, :], sbg[:, :, :, i1], sbg[:, :, :, i2], OP.add)
        return lastl
    ev.do("vector", step_ps, sig="ps6", sem="vector", waits=("tsbe15",))
    ev.do("vector", lambda e: (e.tensor_tensor(V4("mA"), P6[:, :, 0, :], P6[:, :, 1, :], OP.max),
                               e.tensor_tensor(V4("mB"), P6[:, :, 2, :], P6[:, :, 3, :], OP.max))[-1],
          sig="mx1", sem="vector", waits=("ps6",))
    ev.do("vector", lambda e: e.tensor_tensor(V4("mA"), V4("mA"), V4("mB"), OP.max),
          sig="mx2", sem="vector", waits=("mx1",))
    ev.do("vector", lambda e: e.tensor_tensor(V4("mB"), P6[:, :, 4, :], P6[:, :, 5, :], OP.max),
          sig="mx3", sem="vector", waits=("mx2",))
    ev.do("vector", lambda e: e.tensor_tensor(V4("mA"), V4("mA"), V4("mB"), OP.max),
          sig="gsc", sem="vector", waits=("mx3",))
    ev.do("vector", lambda e: e.memset(d["ggt"][:, :], 0.0), sig="ggt0", sem="vector")
    def step_ggt(e):
        gsc = V4("mA")
        gf = d["ggt"][:, :].rearrange("p (c x) -> p c x", x=NGRP * NGRP)
        lastl = None
        for dl in range(1, NGRP):
            n = NGRP - dl
            lastl = e.tensor_tensor(gf[:, :, dl:dl + 5 * (n - 1) + 1:5],
                                    gsc[:, :, dl:], gsc[:, :, 0:n], OP.is_gt)
            lastl = e.tensor_tensor(gf[:, :, 4 * dl:4 * dl + 5 * (n - 1) + 1:5],
                                    gsc[:, :, 0:n], gsc[:, :, dl:], OP.is_gt)
        return lastl
    ev.do("vector", step_ggt, sig="ggt", sem="vector", waits=("gsc", "ggt0"))
    ev.do("vector", lambda e: e.tensor_reduce(
        V4("gmask"), d["ggt"][:, :].rearrange("p (c g x) -> p c g x", g=NGRP, x=NGRP),
        AX.X, OP.add), sig="grank", sem="vector", waits=("ggt",))
    ev.do("vector", lambda e: e.tensor_scalar(V4("gmask"), V4("gmask"), 2.0, None, OP.is_lt),
          sig="gmask", sem="vector", waits=("grank",))
    ev.do("vector", lambda e: e.tensor_copy(
        d["emask"][:, :].rearrange("p (c g e) -> p c g e", g=NGRP, e=EPG),
        V4("gmask").rearrange("p c g -> p c g ()").broadcast_to([128, NCK, NGRP, EPG])
        if False else
        V4("gmask").rearrange("p c (g o) -> p c g o", o=1).broadcast_to([128, NCK, NGRP, EPG])),
        sig="emask", sem="vector", waits=("gmask",))
    ev.do("vector", lambda e: e.tensor_tensor(
        d["msk"][:, :], d["sbtok"][:, :], d["emask"][:, :], OP.mult),
        sig="mska", sem="vector", waits=("emask",))
    ev.do("vector", lambda e: e.tensor_scalar(
        d["msk2"][:, :], d["emask"][:, :], 1.0, 1e30, OP.subtract, OP.mult),
        sig="mskb", sem="vector", waits=("emask",))
    ev.do("vector", lambda e: e.tensor_tensor(
        d["msk"][:, :], d["msk"][:, :], d["msk2"][:, :], OP.add),
        sig="msk", sem="vector", waits=("mska", "mskb"))
    ev.do("vector", lambda e: e.memset(d["egt"][:, :], 0.0), sig="egt0", sem="vector")
    def step_egt(e):
        mk = V16("msk")
        ef = d["egt"][:, :].rearrange("p (c x) -> p c x", x=NEXP * NEXP)
        lastl = None
        for dl in range(1, NEXP):
            n = NEXP - dl
            lastl = e.tensor_tensor(ef[:, :, dl:dl + 17 * (n - 1) + 1:17],
                                    mk[:, :, dl:], mk[:, :, 0:n], OP.is_gt)
            lastl = e.tensor_tensor(ef[:, :, 16 * dl:16 * dl + 17 * (n - 1) + 1:17],
                                    mk[:, :, 0:n], mk[:, :, dl:], OP.is_gt)
        return lastl
    ev.do("vector", step_egt, sig="egt", sem="vector", waits=("msk", "egt0"))
    ev.do("vector", lambda e: e.tensor_reduce(
        V16("erank"), d["egt"][:, :].rearrange("p (c a x) -> p c a x", a=NEXP, x=NEXP),
        AX.X, OP.add), sig="erank", sem="vector", waits=("egt",))
    ev.do("vector", lambda e: e.tensor_scalar(V16("erank"), V16("erank"), float(TOPK), None, OP.is_lt),
          sig="kmask", sem="vector", waits=("erank",))
    ev.do("vector", lambda e: e.tensor_tensor(
        d["wts"][:, :], d["sctok"][:, :], d["erank"][:, :], OP.mult),
        sig="wts", sem="vector", waits=("kmask",))
    ev.do("vector", lambda e: e.tensor_reduce(
        d["wsum"][:, :].rearrange("p (c o) -> p c o", o=1),
        V16("wts"), AX.X, OP.add), sig="wsum", sem="vector", waits=("wts",))
    ev.do("vector", lambda e: e.tensor_scalar_add(d["wsum"][:, :], d["wsum"][:, :], 1e-20),
          sig="wsum2", sem="vector", waits=("wsum",))
    ev.do("vector", lambda e: e.reciprocal(d["wsum"][:, :], d["wsum"][:, :]),
          sig="winv", sem="vector", waits=("wsum2",))
    ev.do("vector", lambda e: e.tensor_tensor(
        V16("wts"), V16("wts"),
        d["wsum"][:, :].rearrange("p (c o) -> p c o", o=1).broadcast_to([128, NCK, NEXP]),
        OP.mult), sig="comb", sem="vector", waits=("winv",))
    if TRUNC <= 3:
        trunc_exit("comb")
        return
    for ck in range(NCK):
        ev.do("tensor", lambda e, ck=ck: e.transpose(
            bk[7][0:NEXP, 0:128], d["wts"][:, NEXP * ck:NEXP * (ck + 1)], d["identf"][:, :]),
            sig=f"cbm{ck}", sem="tensor",
            waits=("comb", "ld_id") if ck == 0 else (f"cbe{ck-1}",))
        ev.do("vector", lambda e, ck=ck: e.tensor_copy(
            d["cfm"][:, 128 * ck:128 * (ck + 1)], bk[7][0:NEXP, 0:128]),
            sig=f"cbe{ck}", sem="vector", waits=(f"cbm{ck}",))

    if TRUNC <= 4:
        trunc_exit("cbe15")
        return
    NTeff = 1 if TRUNC == 5 else NT
    for t4 in range(NTeff):
        tsl = slice(512 * t4, 512 * (t4 + 1))
        for e_ in range(NEXP):
            cw = (("cbe15", "ld_sel") if (t4 == 0 and e_ == 0) else
                  ((f"cbte{t4}_{e_-1}",) if e_ > 0 else
                   (f"cbte{t4-1}_{NEXP-1}", f"gse{t4-1}_1_3")))
            ev.do("tensor", lambda e, e_=e_, t4=t4: e.matmul(
                bk[6][0:128, 0:512], d["self_"][:, 128 * e_:128 * (e_ + 1)],
                d["cfm"][:, tsl], start=True, stop=True),
                sig=f"cbtm{t4}_{e_}", sem="tensor", waits=cw)
            ev.do("vector", lambda e, e_=e_: e.tensor_copy(
                d["cbt"][:, 512 * e_:512 * (e_ + 1)], bk[6][0:128, 0:512]),
                sig=f"cbte{t4}_{e_}", sem="vector",
                waits=(f"cbtm{t4}_{e_}",) + ((f"hw{t4-1}_{e_}_1",) if t4 > 0 else ()))
        for e_ in range(NEXP):
            ev.dma("gpsimd", "gw", lambda e, e_=e_: e.dma_start(
                out=d["gwe"][:, :].rearrange("p (k x) -> p k x", k=8),
                in_=d["gw"].ap()[e_].rearrange("(k p) x -> p k x", p=128)),
                sig=f"gwl{t4}_{e_}",
                waits=() if (t4 == 0 and e_ == 0) else (f"gmm{t4}_{e_-1}" if e_ > 0 else f"gmm{t4-1}_{NEXP-1}",))
            ev.dma("gpsimd", "uw", lambda e, e_=e_: e.dma_start(
                out=d["uwe"][:, :].rearrange("p (k x) -> p k x", k=8),
                in_=d["uw"].ap()[e_].rearrange("(k p) x -> p k x", p=128)),
                sig=f"uwl{t4}_{e_}",
                waits=() if (t4 == 0 and e_ == 0) else (f"umm{t4}_{e_-1}" if e_ > 0 else f"umm{t4-1}_{NEXP-1}",))
            for hc in range(2):
                def gmm(e, e_=e_, hc=hc, t4=t4):
                    lastl = None
                    for q in range(8):
                        lhs = d["gwe"][:, :].rearrange("p (k x) -> p k x", k=8)[:, q, 128 * hc:128 * (hc + 1)]
                        lastl = e.matmul(bk[0][0:128, 0:512], lhs,
                                         d[f"xnf{q}"][:, tsl], start=(q == 0), stop=(q == 7))
                    return lastl
                def umm(e, e_=e_, hc=hc, t4=t4):
                    lastl = None
                    for q in range(8):
                        lhs = d["uwe"][:, :].rearrange("p (k x) -> p k x", k=8)[:, q, 128 * hc:128 * (hc + 1)]
                        lastl = e.matmul(bk[1][0:128, 0:512], lhs,
                                         d[f"xnf{q}"][:, tsl], start=(q == 0), stop=(q == 7))
                    return lastl
                prevh = None
                if not (t4 == 0 and e_ == 0 and hc == 0):
                    if hc == 1:
                        prevh = f"hw{t4}_{e_}_0"
                    elif e_ > 0:
                        prevh = f"hw{t4}_{e_-1}_1"
                    else:
                        prevh = f"hw{t4-1}_{NEXP-1}_1"
                ev.do("tensor", gmm, sig=f"gmm{t4}_{e_}" if hc == 1 else f"gmmh{t4}_{e_}_{hc}",
                      sem="tensor", waits=(f"gwl{t4}_{e_}",) + ((prevh,) if prevh else ()))
                ev.do("tensor", umm, sig=f"umm{t4}_{e_}" if hc == 1 else f"ummh{t4}_{e_}_{hc}",
                      sem="tensor", waits=(f"uwl{t4}_{e_}",))
                gev = f"gmm{t4}_{e_}" if hc == 1 else f"gmmh{t4}_{e_}_{hc}"
                uev = f"umm{t4}_{e_}" if hc == 1 else f"ummh{t4}_{e_}_{hc}"
                ev.do("scalar", lambda e: e.activation(d["sgt"][:, :], bk[0][0:128, 0:512], AF.Silu),
                      sig=f"sge{t4}_{e_}_{hc}", sem="scalar",
                      waits=(gev,) + ((prevh,) if prevh else ()))
                ev.do("scalar", lambda e: e.copy(d["sut"][:, :], bk[1][0:128, 0:512]),
                      sig=f"sue{t4}_{e_}_{hc}", sem="scalar",
                      waits=(uev,) + ((f"hm1{t4}_{e_}_{hc-1}" if hc > 0 else
                                       (f"hm1{t4}_{e_-1}_1" if e_ > 0 else
                                        (f"hm1{t4-1}_{NEXP-1}_1" if t4 > 0 else None)),)))
                def hmul1(e):
                    return e.tensor_tensor(d["sgt"][:, :], d["sgt"][:, :], d["sut"][:, :], OP.mult)
                def hmul2(e, e_=e_, hc=hc):
                    hv = d["hall"][:, :].rearrange("p (x s) -> p x s", s=512)
                    return e.tensor_tensor(hv[:, e_ * 2 + hc, :], d["sgt"][:, :],
                                           d["cbt"][:, 512 * e_:512 * (e_ + 1)], OP.mult)
                ev.do("vector", hmul1, sig=f"hm1{t4}_{e_}_{hc}", sem="vector",
                      waits=(f"sge{t4}_{e_}_{hc}", f"sue{t4}_{e_}_{hc}"))
                ev.do("vector", hmul2, sig=f"hw{t4}_{e_}_{hc}", sem="vector",
                      waits=(f"hm1{t4}_{e_}_{hc}", f"cbte{t4}_{e_}") +
                            ((f"dmm{t4-1}_1_{e_}",) if t4 > 0 else ()))
        for sh in range(4):
            ev.dma("gpsimd", "sg", lambda e, sh=sh: e.dma_start(
                out=d["sgw"][:, :].rearrange("p (k x) -> p k x", k=8),
                in_=d["sgwT"].ap()[:, 512 * sh:512 * (sh + 1)].rearrange("(k p) x -> p k x", p=128)),
                sig=f"sgl{t4}_{sh}",
                waits=() if (t4 == 0 and sh == 0) else ((f"shmm{t4}_{sh-1}_3g",) if sh > 0 else (f"shmm{t4-1}_3_3g",)))
            ev.dma("gpsimd", "su", lambda e, sh=sh: e.dma_start(
                out=d["suw"][:, :].rearrange("p (k x) -> p k x", k=8),
                in_=d["suwT"].ap()[:, 512 * sh:512 * (sh + 1)].rearrange("(k p) x -> p k x", p=128)),
                sig=f"sul{t4}_{sh}",
                waits=() if (t4 == 0 and sh == 0) else ((f"shmm{t4}_{sh-1}_3u",) if sh > 0 else (f"shmm{t4-1}_3_3u",)))
            for hc in range(4):
                def sgmm(e, sh=sh, hc=hc, t4=t4):
                    lastl = None
                    for q in range(8):
                        lhs = d["sgw"][:, :].rearrange("p (k x) -> p k x", k=8)[:, q, 128 * hc:128 * (hc + 1)]
                        lastl = e.matmul(bk[2][0:128, 0:512], lhs, d[f"xnf{q}"][:, tsl],
                                         start=(q == 0), stop=(q == 7))
                    return lastl
                def summ(e, sh=sh, hc=hc, t4=t4):
                    lastl = None
                    for q in range(8):
                        lhs = d["suw"][:, :].rearrange("p (k x) -> p k x", k=8)[:, q, 128 * hc:128 * (hc + 1)]
                        lastl = e.matmul(bk[3][0:128, 0:512], lhs, d[f"xnf{q}"][:, tsl],
                                         start=(q == 0), stop=(q == 7))
                    return lastl
                prevs = None
                if not (t4 == 0 and sh == 0 and hc == 0):
                    if hc > 0:
                        prevs = f"shw{t4}_{sh}_{hc-1}"
                    elif sh > 0:
                        prevs = f"shw{t4}_{sh-1}_3"
                    else:
                        prevs = f"shw{t4-1}_3_3"
                gs = f"shmm{t4}_{sh}_{hc}g" if hc < 3 else f"shmm{t4}_{sh}_3g"
                us = f"shmm{t4}_{sh}_{hc}u" if hc < 3 else f"shmm{t4}_{sh}_3u"
                ev.do("tensor", sgmm, sig=gs, sem="tensor",
                      waits=(f"sgl{t4}_{sh}",) + ((prevs,) if prevs else ()))
                ev.do("tensor", summ, sig=us, sem="tensor", waits=(f"sul{t4}_{sh}",))
                pssg = (f"shw{t4}_{sh}_{hc-1}" if hc > 0 else
                        (f"shw{t4}_{sh-1}_3" if sh > 0 else f"hw{t4}_{NEXP-1}_1"))
                pssu = (f"shm1{t4}_{sh}_{hc-1}" if hc > 0 else
                        (f"shm1{t4}_{sh-1}_3" if sh > 0 else f"hm1{t4}_{NEXP-1}_1"))
                ev.do("scalar", lambda e: e.activation(d["sgt"][:, :], bk[2][0:128, 0:512], AF.Silu),
                      sig=f"ssge{t4}_{sh}_{hc}", sem="scalar", waits=(gs, pssg))
                ev.do("scalar", lambda e: e.copy(d["sut"][:, :], bk[3][0:128, 0:512]),
                      sig=f"ssue{t4}_{sh}_{hc}", sem="scalar", waits=(us, pssu))
                ev.do("vector", lambda e: e.tensor_tensor(d["sgt"][:, :], d["sgt"][:, :], d["sut"][:, :], OP.mult),
                      sig=f"shm1{t4}_{sh}_{hc}", sem="vector",
                      waits=(f"ssge{t4}_{sh}_{hc}", f"ssue{t4}_{sh}_{hc}"))
                ev.do("vector", lambda e, sh=sh, hc=hc: e.tensor_copy(
                    d["shh"][:, :].rearrange("p (x s) -> p x s", s=512)[:, sh * 4 + hc, :],
                    d["sgt"][:, :]), sig=f"shw{t4}_{sh}_{hc}", sem="vector",
                    waits=(f"shm1{t4}_{sh}_{hc}",) +
                          ((f"sdmm{t4-1}_1_{sh}",) if t4 > 0 else ()))
        for half in range(2):
            for e_ in range(NEXP):
                ev.dma("gpsimd", "dw", lambda e, e_=e_, half=half: e.dma_start(
                    out=d["dwe"][:, :].rearrange("p (k x) -> p k x", k=2),
                    in_=d["dw"].ap()[e_][:, 512 * half:512 * (half + 1)]
                        .rearrange("(k p) x -> p k x", p=128)),
                    sig=f"dwl{t4}_{half}_{e_}",
                    waits=() if (t4 == 0 and half == 0 and e_ == 0) else
                          ((f"dmm{t4}_{half}_{e_-1}",) if e_ > 0 else
                           ((f"dmm{t4}_{half-1}_{NEXP-1}",) if half > 0 else (f"sdmm{t4-1}_1_15",))))
                def dmm(e, e_=e_, half=half):
                    hv = d["hall"][:, :].rearrange("p (x s) -> p x s", s=512)
                    lastl = None
                    for hc in range(2):
                        for ocq in range(4):
                            lhs = d["dwe"][:, :].rearrange("p (k x) -> p k x", k=2)[:, hc, 128 * ocq:128 * (ocq + 1)]
                            lastl = e.matmul(bk[4 + ocq][0:128, 0:512], lhs,
                                             hv[:, e_ * 2 + hc, :],
                                             start=(e_ == 0 and hc == 0), stop=False)
                    return lastl
                ev.do("tensor", dmm, sig=f"dmm{t4}_{half}_{e_}", sem="tensor",
                      waits=(f"dwl{t4}_{half}_{e_}", f"hw{t4}_{e_}_1") +
                            ((f"gde{t4}_{half-1}_3",) if (e_ == 0 and half == 1) else ()) +
                            ((f"gde{t4-1}_1_3",) if (e_ == 0 and half == 0 and t4 > 0) else ()))
            for sh in range(16):
                ev.dma("gpsimd", "sd", lambda e, sh=sh, half=half: e.dma_start(
                    out=d["sdw"][:, :],
                    in_=d["sdwT"].ap()[128 * sh:128 * (sh + 1), 512 * half:512 * (half + 1)]),
                    sig=f"sdl{t4}_{half}_{sh}",
                    waits=(f"sdmm{t4}_{half}_{sh-1}",) if sh > 0 else (f"dmm{t4}_{half}_{NEXP-1}",))
                def sdmm(e, sh=sh, half=half, t4=t4):
                    lastl = None
                    for ocq in range(4):
                        lhs = d["sdw"][:, 128 * ocq:128 * (ocq + 1)]
                        lastl = e.matmul(bk[4 + ocq][0:128, 0:512], lhs,
                                         d["shh"][:, :].rearrange("p (x s) -> p x s", s=512)[:, sh, :],
                                         start=False, stop=(sh == 15))
                    return lastl
                ev.do("tensor", sdmm, sig=f"sdmm{t4}_{half}_{sh}", sem="tensor",
                      waits=(f"sdl{t4}_{half}_{sh}", f"shw{t4}_3_3"))
            for ocq in range(4):
                oc = half * 4 + ocq
                pgd = (f"gde{t4}_{half}_{ocq-1}" if ocq > 0 else
                       (f"gde{t4}_{half-1}_3" if half > 0 else
                        (f"gde{t4-1}_1_3" if t4 > 0 else None)))
                ev.do("scalar", lambda e, ocq=ocq: e.activation(
                    d["gst"][:, :], bk[4 + ocq][0:128, 0:512], AF.Silu),
                    sig=f"gse{t4}_{half}_{ocq}", sem="scalar",
                    waits=(f"sdmm{t4}_{half}_15",) + ((pgd,) if pgd else ()))
                ev.dma("gpsimd", "go", lambda e, oc=oc, t4=t4: e.dma_start(
                    out=d["gout"].ap()[128 * oc:128 * (oc + 1), tsl], in_=d["gst"][:, :]),
                    sig=f"gde{t4}_{half}_{ocq}", waits=(f"gse{t4}_{half}_{ocq}",))
    if TRUNC == 5:
        trunc_exit(f"gde0_1_3")
        return
    ev.do("gpsimd", waits=tuple(f"gde{NT-1}_{h}_{q}" for h in range(2) for q in range(4)))


def body_L4r(nc, ev, d, ctx, declare):
    NCK = TSH // 128
    if declare:
        ddecl(nc, d, "xn", "xn", [TSH, E], BF16, "ExternalInput")
        d["rwT"] = nc.dram_tensor("rwT", [E, NEXP], F32, kind="ExternalInput")
        d["corr"] = nc.dram_tensor("corr", [NEXP, 1], F32, kind="ExternalInput")
        ddecl(nc, d, "cident", "c_ident", [128, 128], F32, "ExternalInput")
        ddecl(nc, d, "cfml", "cfml", [NEXP, TSH], BF16, "ExternalOutput")
        sbuf_decl(nc, ctx, d, [(f"xnf{q}", [128, TSH], BF16, dict(is_transpose_dest=True))
                               for q in range(8)])
        sbuf_decl(nc, ctx, d, [
            ("rwb", [128, 8 * NEXP], BF16), ("corrf", [NEXP, 1], F32),
            ("identf", [128, 128], F32),
            ("scf", [NEXP, TSH], F32), ("sbf", [NEXP, TSH], F32),
            ("sctok", [128, NCK * NEXP], F32), ("sbtok", [128, NCK * NEXP], F32),
            ("ps6", [128, NCK * 6 * NGRP], F32),
            ("mA", [128, NCK * NGRP], F32), ("mB", [128, NCK * NGRP], F32),
            ("ggt", [128, NCK * NGRP * NGRP], F32),
            ("gmask", [128, NCK * NGRP], F32),
            ("emask", [128, NCK * NEXP], F32),
            ("msk", [128, NCK * NEXP], F32), ("msk2", [128, NCK * NEXP], F32),
            ("egt", [128, NCK * NEXP * NEXP], F32),
            ("erank", [128, NCK * NEXP], F32),
            ("wts", [128, NCK * NEXP], F32), ("wsum", [128, NCK], F32),
            ("cfm", [NEXP, TSH], F32), ("cfmb", [NEXP, TSH], BF16)])
        d["bk"] = [ctx.enter_context(nc.psum_tensor(f"bk{i}", [128, 512], F32))
                   for i in range(8)]
        return

    bk = d["bk"]
    for q in range(8):
        ev.dma("sync", "xn", lambda e, q=q: e.dma_start(
            out=d[f"xnf{q}"][:, :],
            in_=d["xn"].ap()[:, 128 * q:128 * (q + 1)], transpose=True),
            sig=f"xl{q}")
    XL = tuple(f"xl{q}" for q in range(8))
    ev.dma("gpsimd", "rw", lambda e: e.dma_start(
        out=d["rwb"][:, :].rearrange("p (k x) -> p k x", k=8),
        in_=d["rwT"].ap().rearrange("(k p) x -> p k x", p=128)), sig="ld_rw")
    ev.dma("sync", "kc", lambda e: e.dma_start(out=d["corrf"][:, :], in_=d["corr"].ap()),
           sig="ld_corr")
    ev.dma("sync", "kc", lambda e: e.dma_start(out=d["identf"][:, :], in_=d["cident"].ap()),
           sig="ld_id")
    for t4 in range(4):
        def rmm(e, t4=t4):
            lastl = None
            for q in range(8):
                lastl = e.matmul(bk[6][0:NEXP, 0:512],
                                 d["rwb"][:, :].rearrange("p (k x) -> p k x", k=8)[:, q, :],
                                 d[f"xnf{q}"][:, 512 * t4:512 * (t4 + 1)],
                                 start=(q == 0), stop=(q == 7))
            return lastl
        ev.do("tensor", rmm, sig=f"rmm{t4}", sem="tensor",
              waits=XL + ("ld_rw",) + ((f"rsev{t4-1}",) if t4 > 0 else ()))
        ev.do("scalar", lambda e, t4=t4: e.activation(
            d["scf"][:, 512 * t4:512 * (t4 + 1)], bk[6][0:NEXP, 0:512], AF.Sigmoid),
            sig=f"rsev{t4}", sem="scalar", waits=(f"rmm{t4}",))
    ev.do("vector", lambda e: e.tensor_scalar(
        d["sbf"][:, :], d["scf"][:, :], d["corrf"][:, :], None, OP.add),
        sig="sbf", sem="vector", waits=("rsev3", "ld_corr"))
    for src, dst, sig0 in (("scf", "sctok", "tsc"), ("sbf", "sbtok", "tsb")):
        for ck in range(NCK):
            ev.do("tensor", lambda e, src=src, ck=ck: e.transpose(
                bk[7][0:128, 0:NEXP], d[src][:, 128 * ck:128 * (ck + 1)],
                d["identf"][0:NEXP, 0:NEXP]),
                sig=f"{sig0}m{ck}", sem="tensor",
                waits=(("sbf", "ld_id") if (ck == 0 and src == "scf") else
                       ((f"{sig0}e{ck-1}",) if ck > 0 else ("tsce15",))))
            ev.do("vector", lambda e, dst=dst, ck=ck: e.tensor_copy(
                d[dst][:, NEXP * ck:NEXP * (ck + 1)], bk[7][0:128, 0:NEXP]),
                sig=f"{sig0}e{ck}", sem="vector", waits=(f"{sig0}m{ck}",))
    def V4(nm):
        return d[nm][:, :].rearrange("p (c g) -> p c g", g=NGRP)
    def V16(nm):
        return d[nm][:, :].rearrange("p (c x) -> p c x", x=NEXP)
    sbg = d["sbtok"][:, :].rearrange("p (c g e) -> p c g e", g=NGRP, e=EPG)
    P6 = d["ps6"][:, :].rearrange("p (c s g) -> p c s g", s=6, g=NGRP)
    pairs = [(0, 1), (0, 2), (0, 3), (1, 2), (1, 3), (2, 3)]
    def step_ps(e):
        lastl = None
        for pi, (i1, i2) in enumerate(pairs):
            lastl = e.tensor_tensor(P6[:, :, pi, :], sbg[:, :, :, i1], sbg[:, :, :, i2], OP.add)
        return lastl
    ev.do("vector", step_ps, sig="ps6", sem="vector", waits=("tsbe15",))
    ev.do("vector", lambda e: (e.tensor_tensor(V4("mA"), P6[:, :, 0, :], P6[:, :, 1, :], OP.max),
                               e.tensor_tensor(V4("mB"), P6[:, :, 2, :], P6[:, :, 3, :], OP.max))[-1],
          sig="mx1", sem="vector", waits=("ps6",))
    ev.do("vector", lambda e: e.tensor_tensor(V4("mA"), V4("mA"), V4("mB"), OP.max),
          sig="mx2", sem="vector", waits=("mx1",))
    ev.do("vector", lambda e: e.tensor_tensor(V4("mB"), P6[:, :, 4, :], P6[:, :, 5, :], OP.max),
          sig="mx3", sem="vector", waits=("mx2",))
    ev.do("vector", lambda e: e.tensor_tensor(V4("mA"), V4("mA"), V4("mB"), OP.max),
          sig="gsc", sem="vector", waits=("mx3",))
    ev.do("vector", lambda e: e.memset(d["ggt"][:, :], 0.0), sig="ggt0", sem="vector")
    def step_ggt(e):
        gsc = V4("mA")
        gf = d["ggt"][:, :].rearrange("p (c x) -> p c x", x=NGRP * NGRP)
        lastl = None
        for dl in range(1, NGRP):
            n = NGRP - dl
            lastl = e.tensor_tensor(gf[:, :, dl:dl + 5 * (n - 1) + 1:5],
                                    gsc[:, :, dl:], gsc[:, :, 0:n], OP.is_gt)
            lastl = e.tensor_tensor(gf[:, :, 4 * dl:4 * dl + 5 * (n - 1) + 1:5],
                                    gsc[:, :, 0:n], gsc[:, :, dl:], OP.is_gt)
        return lastl
    ev.do("vector", step_ggt, sig="ggt", sem="vector", waits=("gsc", "ggt0"))
    ev.do("vector", lambda e: e.tensor_reduce(
        V4("gmask"), d["ggt"][:, :].rearrange("p (c g x) -> p c g x", g=NGRP, x=NGRP),
        AX.X, OP.add), sig="grank", sem="vector", waits=("ggt",))
    ev.do("vector", lambda e: e.tensor_scalar(V4("gmask"), V4("gmask"), 2.0, None, OP.is_lt),
          sig="gmask", sem="vector", waits=("grank",))
    ev.do("vector", lambda e: e.tensor_copy(
        d["emask"][:, :].rearrange("p (c g e) -> p c g e", g=NGRP, e=EPG),
        V4("gmask").rearrange("p c (g o) -> p c g o", o=1).broadcast_to([128, NCK, NGRP, EPG])),
        sig="emask", sem="vector", waits=("gmask",))
    ev.do("vector", lambda e: e.tensor_tensor(
        d["msk"][:, :], d["sbtok"][:, :], d["emask"][:, :], OP.mult),
        sig="mska", sem="vector", waits=("emask",))
    ev.do("vector", lambda e: e.tensor_scalar(
        d["msk2"][:, :], d["emask"][:, :], 1.0, 1e30, OP.subtract, OP.mult),
        sig="mskb", sem="vector", waits=("emask",))
    ev.do("vector", lambda e: e.tensor_tensor(
        d["msk"][:, :], d["msk"][:, :], d["msk2"][:, :], OP.add),
        sig="msk", sem="vector", waits=("mska", "mskb"))
    ev.do("vector", lambda e: e.memset(d["egt"][:, :], 0.0), sig="egt0", sem="vector")
    def step_egt(e):
        mk = V16("msk")
        ef = d["egt"][:, :].rearrange("p (c x) -> p c x", x=NEXP * NEXP)
        lastl = None
        for dl in range(1, NEXP):
            n = NEXP - dl
            lastl = e.tensor_tensor(ef[:, :, dl:dl + 17 * (n - 1) + 1:17],
                                    mk[:, :, dl:], mk[:, :, 0:n], OP.is_gt)
            lastl = e.tensor_tensor(ef[:, :, 16 * dl:16 * dl + 17 * (n - 1) + 1:17],
                                    mk[:, :, 0:n], mk[:, :, dl:], OP.is_gt)
        return lastl
    ev.do("vector", step_egt, sig="egt", sem="vector", waits=("msk", "egt0"))
    ev.do("vector", lambda e: e.tensor_reduce(
        V16("erank"), d["egt"][:, :].rearrange("p (c a x) -> p c a x", a=NEXP, x=NEXP),
        AX.X, OP.add), sig="erank", sem="vector", waits=("egt",))
    ev.do("vector", lambda e: e.tensor_scalar(V16("erank"), V16("erank"), float(TOPK), None, OP.is_lt),
          sig="kmask", sem="vector", waits=("erank",))
    ev.do("vector", lambda e: e.tensor_tensor(
        d["wts"][:, :], d["sctok"][:, :], d["erank"][:, :], OP.mult),
        sig="wts", sem="vector", waits=("kmask",))
    ev.do("vector", lambda e: e.tensor_reduce(
        d["wsum"][:, :].rearrange("p (c o) -> p c o", o=1),
        V16("wts"), AX.X, OP.add), sig="wsum", sem="vector", waits=("wts",))
    ev.do("vector", lambda e: e.tensor_scalar_add(d["wsum"][:, :], d["wsum"][:, :], 1e-20),
          sig="wsum2", sem="vector", waits=("wsum",))
    ev.do("vector", lambda e: e.reciprocal(d["wsum"][:, :], d["wsum"][:, :]),
          sig="winv", sem="vector", waits=("wsum2",))
    ev.do("vector", lambda e: e.tensor_tensor(
        V16("wts"), V16("wts"),
        d["wsum"][:, :].rearrange("p (c o) -> p c o", o=1).broadcast_to([128, NCK, NEXP]),
        OP.mult), sig="comb", sem="vector", waits=("winv",))
    for ck in range(NCK):
        ev.do("tensor", lambda e, ck=ck: e.transpose(
            bk[7][0:NEXP, 0:128], d["wts"][:, NEXP * ck:NEXP * (ck + 1)], d["identf"][:, :]),
            sig=f"cbm{ck}", sem="tensor",
            waits=("comb", "ld_id") if ck == 0 else (f"cbe{ck-1}",))
        ev.do("vector", lambda e, ck=ck: e.tensor_copy(
            d["cfm"][:, 128 * ck:128 * (ck + 1)], bk[7][0:NEXP, 0:128]),
            sig=f"cbe{ck}", sem="vector", waits=(f"cbm{ck}",))
    ev.do("vector", lambda e: e.tensor_copy(d["cfmb"][:, :], d["cfm"][:, :]),
          sig="cfmc", sem="vector", waits=("cbe15",))
    ev.dma("gpsimd", "cf", lambda e: e.dma_start(
        out=d["cfml"].ap(), in_=d["cfmb"][:, :]), sig="cfst", waits=("cfmc",))
    ev.do("gpsimd", waits=("cfst",))


def body_L4e(nc, ev, d, ctx, declare):
    NST = (B * N) // 512
    if declare:
        ddecl(nc, d, "xnag", "xnag", [B * N, E], BF16, "ExternalInput")
        ddecl(nc, d, "cfma", "cfma", [128, TSH], BF16, "ExternalInput")
        ddecl(nc, d, "spf", "spf", [B * N, E], BF16, "ExternalOutput")
        d["gwh"] = nc.dram_tensor("gwh", [2, E, HR], F32, kind="ExternalInput")
        d["uwh"] = nc.dram_tensor("uwh", [2, E, HR], F32, kind="ExternalInput")
        d["dwh"] = nc.dram_tensor("dwh", [2, HR, E], F32, kind="ExternalInput")
        d["sgwh"] = nc.dram_tensor("sgwh", [E, HR], F32, kind="ExternalInput")
        d["suwh"] = nc.dram_tensor("suwh", [E, HR], F32, kind="ExternalInput")
        d["sdwh"] = nc.dram_tensor("sdwh", [HR, E], F32, kind="ExternalInput")
        d["fselb"] = nc.dram_tensor("fselb", [8, 2, 128, 128], BF16, kind="ExternalInput")
        sbuf_decl(nc, ctx, d, [(f"xnt{p}", [128, 8 * 512], BF16,
                                dict(is_transpose_dest=True)) for p in range(2)])
        sbuf_decl(nc, ctx, d, [
            ("cfms", [128, TSH], BF16), ("fsb", [128, 16 * 128], BF16),
            ("gwb", [128, 2 * 8 * HR], BF16), ("uwb", [128, 2 * 8 * HR], BF16),
            ("dwb", [128, 2 * 2 * E], BF16),
            ("sgwb", [128, 8 * HR], BF16), ("suwb", [128, 8 * HR], BF16),
            ("sdwb", [128, 2 * E], BF16),
            ("cbts0", [128, 2 * 512], BF16), ("cbts1", [128, 2 * 512], BF16),
            ("sgt0", [128, 512], BF16), ("sgt1", [128, 512], BF16),
            ("sut0", [128, 512], BF16), ("sut1", [128, 512], BF16),
            ("hv0", [128, 6 * 512], BF16), ("hv1", [128, 6 * 512], BF16),
            ("fstg0", [128, 4 * E], BF16), ("fstg1", [128, 4 * E], BF16)])
        d["bk"] = [ctx.enter_context(nc.psum_tensor(f"bk{i}", [128, 512], F32))
                   for i in range(8)]
        return

    bk = d["bk"]
    ev.dma("sync", "cf", lambda e: e.dma_start(
        out=d["cfms"][:, :], in_=d["cfma"].ap()), sig="ld_cf")
    ev.dma("sync", "cf", lambda e: e.dma_start(
        out=d["fsb"][:, :].rearrange("p (a b) -> p a b", a=16),
        in_=d["fselb"].ap().rearrange("j l p c -> p (j l) c")), sig="ld_fs")
    for nm, src, k in (("gwb", "gwh", 2), ("uwb", "uwh", 2)):
        ev.dma("gpsimd", "wl", lambda e, nm=nm, src=src: e.dma_start(
            out=d[nm][:, :].rearrange("p (l k h) -> p (l k) h", l=2, k=8),
            in_=d[src].ap().rearrange("l (k p) h -> p (l k) h", p=128)),
            sig=f"ld_{nm}")
    ev.dma("gpsimd", "wl", lambda e: e.dma_start(
        out=d["dwb"][:, :].rearrange("p (l k h) -> p (l k) h", l=2, k=2),
        in_=d["dwh"].ap().rearrange("l (k p) h -> p (l k) h", p=128)),
        sig="ld_dwb")
    for nm, src in (("sgwb", "sgwh"), ("suwb", "suwh")):
        ev.dma("gpsimd", "wl", lambda e, nm=nm, src=src: e.dma_start(
            out=d[nm][:, :].rearrange("p (k h) -> p k h", k=8),
            in_=d[src].ap().rearrange("(k p) h -> p k h", p=128)),
            sig=f"ld_{nm}")
    ev.dma("gpsimd", "wl", lambda e: e.dma_start(
        out=d["sdwb"][:, :].rearrange("p (k h) -> p k h", k=2),
        in_=d["sdwh"].ap().rearrange("(k p) h -> p k h", p=128)),
        sig="ld_sdwb")

    GWB = lambda: d["gwb"][:, :].rearrange("p (l k h) -> p l k h", l=2, k=8)
    UWB = lambda: d["uwb"][:, :].rearrange("p (l k h) -> p l k h", l=2, k=8)
    DWB = lambda: d["dwb"][:, :].rearrange("p (l k h) -> p l k h", l=2, k=2)
    SGWB = lambda: d["sgwb"][:, :].rearrange("p (k h) -> p k h", k=8)
    SUWB = lambda: d["suwb"][:, :].rearrange("p (k h) -> p k h", k=8)
    SDWB = lambda: d["sdwb"][:, :].rearrange("p (k h) -> p k h", k=2)

    for st in range(NST):
        p = st % 2
        j = st // 4
        xnt, cbts = d[f"xnt{p}"], d[f"cbts{p}"]
        hv = d[f"hv{p}"][:, :].rearrange("p (u s) -> p u s", s=512)
        fstg = d[f"fstg{p}"]
        for q in range(8):
            ev.dma("sync", "xn", lambda e, st=st, q=q, xnt=xnt: e.dma_start(
                out=xnt[:, 512 * q:512 * (q + 1)],
                in_=d["xnag"].ap()[512 * st:512 * (st + 1), 128 * q:128 * (q + 1)],
                transpose=True), sig=f"xl{st}_{q}",
                waits=(f"gu{st-2}_5u",) if (st >= 2 and q == 0) else ())
        for el in range(2):
            sw = ["ld_cf", "ld_fs"] if st == 0 and el == 0 else []
            if st >= 1 and el == 0:
                sw.append(f"cbe{st-1}_1")
            ev.do("tensor", lambda e, j=j, el=el, st=st: e.matmul(
                bk[6 + el][0:128, 0:512],
                d["fsb"][:, :].rearrange("p (a b) -> p a b", a=16)[:, 2 * j + el, :],
                d["cfms"][:, 512 * (st % 4):512 * (st % 4 + 1)],
                start=True, stop=True),
                sig=f"cbm{st}_{el}", sem="tensor", waits=tuple(sw))
            cw = [f"cbm{st}_{el}"]
            if st >= 2:
                cw.append(f"hm2{st-2}_{2*el+1}")
            ev.do("vector", lambda e, el=el, cbts=cbts: e.tensor_copy(
                cbts[:, 512 * el:512 * (el + 1)], bk[6 + el][0:128, 0:512]),
                sig=f"cbe{st}_{el}", sem="vector", waits=tuple(cw))
        for u in range(6):
            up_ = u % 2
            bkg, bku = bk[2 * up_], bk[2 * up_ + 1]
            sgt, sut = d[f"sgt{up_}"], d[f"sut{up_}"]
            if u < 4:
                el, hc = u // 2, u % 2
                glhs = lambda q, el=el, hc=hc: GWB()[:, el, q, 128 * hc:128 * (hc + 1)]
                ulhs = lambda q, el=el, hc=hc: UWB()[:, el, q, 128 * hc:128 * (hc + 1)]
            else:
                hc = u - 4
                glhs = lambda q, hc=hc: SGWB()[:, q, 128 * hc:128 * (hc + 1)]
                ulhs = lambda q, hc=hc: SUWB()[:, q, 128 * hc:128 * (hc + 1)]
            def gmm(e, glhs=glhs, st=st, xnt=xnt, bkg=bkg):
                lastl = None
                for q in range(8):
                    lastl = e.matmul(bkg[0:128, 0:512], glhs(q),
                                     xnt[:, 512 * q:512 * (q + 1)],
                                     start=(q == 0), stop=(q == 7))
                return lastl
            def umm(e, ulhs=ulhs, st=st, xnt=xnt, bku=bku):
                lastl = None
                for q in range(8):
                    lastl = e.matmul(bku[0:128, 0:512], ulhs(q),
                                     xnt[:, 512 * q:512 * (q + 1)],
                                     start=(q == 0), stop=(q == 7))
                return lastl
            mw = []
            if u == 0:
                mw = [f"xl{st}_7"] + (["ld_gwb", "ld_uwb", "ld_sgwb", "ld_suwb"]
                                      if st == 0 else [])
            if u >= 2:
                mw.append(f"sge{st}_{u-2}")
                mw.append(f"sue{st}_{u-2}")
            elif st >= 1:
                mw.append(f"sge{st-1}_{u+4}")
                mw.append(f"sue{st-1}_{u+4}")
            ev.do("tensor", gmm, sig=f"gu{st}_{u}g", sem="tensor", waits=tuple(mw))
            ev.do("tensor", umm, sig=f"gu{st}_{u}u" if u < 5 else f"gu{st}_5u",
                  sem="tensor")
            usig = f"gu{st}_{u}u" if u < 5 else f"gu{st}_5u"
            ew = [f"gu{st}_{u}g"]
            uw_ = [usig]
            if u >= 2:
                ew.append(f"hm2{st}_{u-2}")
                uw_.append(f"hm1{st}_{u-2}")
            elif st >= 1:
                ew.append(f"hm2{st-1}_{u+4}")
                uw_.append(f"hm1{st-1}_{u+4}")
            ev.do("scalar", lambda e, bkg=bkg, sgt=sgt: e.activation(
                sgt[:, :], bkg[0:128, 0:512], AF.Silu),
                sig=f"sge{st}_{u}", sem="scalar", waits=tuple(ew))
            ev.do("scalar", lambda e, bku=bku, sut=sut: e.copy(
                sut[:, :], bku[0:128, 0:512]),
                sig=f"sue{st}_{u}", sem="scalar", waits=tuple(uw_))
            if u < 4:
                ev.do("vector", lambda e, sgt=sgt, sut=sut: e.tensor_tensor(
                    sgt[:, :], sgt[:, :], sut[:, :], OP.mult),
                    sig=f"hm1{st}_{u}", sem="vector",
                    waits=(f"sge{st}_{u}", f"sue{st}_{u}"))
                hw = [f"hm1{st}_{u}", f"cbe{st}_{u//2}"]
                if st >= 2:
                    hw.append(f"dmm{st-2}_7")
                ev.do("vector", lambda e, u=u, sgt=sgt, cbts=cbts, hv=hv:
                      e.tensor_tensor(hv[:, u, :], sgt[:, :],
                                      cbts[:, 512 * (u // 2):512 * (u // 2 + 1)],
                                      OP.mult),
                      sig=f"hm2{st}_{u}", sem="vector", waits=tuple(hw))
            else:
                hw = [f"sge{st}_{u}", f"sue{st}_{u}"]
                if st >= 2:
                    hw.append(f"dmm{st-2}_7")
                ev.do("vector", lambda e, u=u, sgt=sgt, sut=sut, hv=hv:
                      e.tensor_tensor(hv[:, u, :], sgt[:, :], sut[:, :], OP.mult),
                      sig=f"hm1{st}_{u}", sem="vector", waits=tuple(hw))
                ev.alias(f"hm2{st}_{u}", f"hm1{st}_{u}")
        HUNITS = [f"hm2{st}_{u}" for u in range(4)] + [f"hm1{st}_{u}" for u in (4, 5)]
        for g in range(8):
            tc, half = g // 2, g % 2
            bkd = bk[4 + g % 2]
            def dmm(e, tc=tc, half=half, bkd=bkd, hv=hv):
                lastl = None
                for u in range(6):
                    if u < 4:
                        el, hc = u // 2, u % 2
                        mov = DWB()[:, el, hc, 512 * half:512 * (half + 1)]
                    else:
                        mov = SDWB()[:, u - 4, 512 * half:512 * (half + 1)]
                    lastl = e.matmul(bkd[0:128, 0:512],
                                     hv[:, u, 128 * tc:128 * (tc + 1)], mov,
                                     start=(u == 0), stop=(u == 5))
                return lastl
            dw_ = list(HUNITS) if g == 0 else []
            if g == 0 and st == 0:
                dw_ += ["ld_dwb", "ld_sdwb"]
            if g >= 2:
                dw_.append(f"fev{st}_{g-2}")
            elif st >= 1:
                dw_.append(f"fev{st-1}_{g+6}")
            ev.do("tensor", dmm, sig=f"dmm{st}_{g}", sem="tensor", waits=tuple(dw_))
            fw = [f"dmm{st}_{g}"]
            if st >= 2 and g == 0:
                fw.append(f"fst{st-2}_3")
            ev.do("scalar", lambda e, tc=tc, half=half, bkd=bkd, fstg=fstg: e.copy(
                fstg[:, E * tc + 512 * half:E * tc + 512 * (half + 1)],
                bkd[0:128, 0:512]),
                sig=f"fev{st}_{g}", sem="scalar", waits=tuple(fw))
        for tc in range(4):
            ev.dma("gpsimd", "fo", lambda e, st=st, tc=tc, fstg=fstg: e.dma_start(
                out=d["spf"].ap()[512 * st + 128 * tc:512 * st + 128 * (tc + 1), 0:E],
                in_=fstg[:, E * tc:E * (tc + 1)]),
                sig=f"fst{st}_{tc}", waits=(f"fev{st}_{2*tc+1}",))
    ev.do("gpsimd", waits=(f"fst{NST-1}_3",))


def body_L4b(nc, ev, d, ctx, declare):
    NK = 24
    NE = 8
    if declare:
        d["wsl"] = nc.dram_tensor("wsl", [H, 3, TSH, 128], BF16, kind="ExternalInput")
        d["g"] = nc.dram_tensor("g", [E, TSH], BF16, kind="ExternalInput")
        d["x"] = nc.dram_tensor("x", [TSH, E], F32, kind="ExternalInput")
        d["ow"] = nc.dram_tensor("ow", [D1, E], F32, kind="ExternalInput")
        d["ob"] = nc.dram_tensor("ob", [E, 1], F32, kind="ExternalInput")
        d["cident"] = nc.dram_tensor("c_ident", [128, 128], F32, kind="ExternalInput")
        d["out"] = nc.dram_tensor("out", [TSH, E], F32, kind="ExternalOutput")
        sbuf_decl(nc, ctx, d, [(f"Xw{p}", [128, NK * 512], BF16,
                                dict(is_transpose_dest=True)) for p in range(2)])
        sbuf_decl(nc, ctx, d, [
            ("gb0", [128, NE * 512], BF16), ("gb1", [128, NE * 512], BF16),
            ("xb0", [128, 4 * E], F32), ("xb1", [128, 4 * E], F32),
            ("outb0", [128, 4 * E], F32), ("outb1", [128, 4 * E], F32),
            ("owb", [128, NK * E], BF16),
            ("obf", [128, NE], F32), ("identf", [128, 128], F32),
            ("Pt", [128, NE * 512], F32)])
        d["bk"] = [ctx.enter_context(nc.psum_tensor(f"bk{i}", [128, 512], F32))
                   for i in range(4)]
        return

    bk = d["bk"]
    ev.dma("gpsimd", "ow", lambda e: e.dma_start(
        out=d["owb"][:, :].rearrange("p (j c) -> p j c", j=NK),
        in_=d["ow"].ap().rearrange("(j p) c -> p j c", p=128)), sig="ld_ow")
    ev.dma("sync", "misc", lambda e: e.dma_start(
        out=d["obf"][:, :].rearrange("p (o x) -> p o x", x=1),
        in_=d["ob"].ap().rearrange("(o p) x -> p o x", p=128),
        allow_slow_non_contiguous=True), sig="ld_ob")
    ev.dma("sync", "misc", lambda e: e.dma_start(
        out=d["identf"][:, :], in_=d["cident"].ap()), sig="ld_id")

    for t4 in range(4):
        p = t4 % 2
        Xw, gb, xb, outb = d[f"Xw{p}"], d[f"gb{p}"], d[f"xb{p}"], d[f"outb{p}"]
        for h in range(H):
            for oc in range(3):
                kk = 3 * h + oc
                ev.dma("sync", "xw", lambda e, h=h, oc=oc, kk=kk, t4=t4: e.dma_start(
                    out=Xw[:, 512 * kk:512 * (kk + 1)],
                    in_=d["wsl"].ap()[h, oc, 512 * t4:512 * (t4 + 1), :],
                    transpose=True), sig=f"xw{t4}_{kk}",
                    waits=(f"pmm{t4-2}_{NE-1}",) if (t4 >= 2 and kk == 0) else ())
        for e_ in range(NE):
            ev.dma("sync", "g", lambda e, e_=e_, t4=t4: e.dma_start(
                out=gb[:, 512 * e_:512 * (e_ + 1)],
                in_=d["g"].ap()[128 * e_:128 * (e_ + 1), 512 * t4:512 * (t4 + 1)]),
                sig=f"g{t4}_{e_}",
                waits=(f"prod{t4-2}_{NE-1}",) if (t4 >= 2 and e_ == 0) else ())
        for tc in range(4):
            ev.dma("gpsimd", "x", lambda e, tc=tc, t4=t4: e.dma_start(
                out=xb[:, E * tc:E * (tc + 1)],
                in_=d["x"].ap()[512 * t4 + 128 * tc:512 * t4 + 128 * (tc + 1), :]),
                sig=f"x{t4}_{tc}",
                waits=(f"radd{t4-2}_31",) if (t4 >= 2 and tc == 0) else ())
        for e_ in range(NE):
            def pmm(e, e_=e_, t4=t4, Xw=Xw):
                lastl = None
                for kk in range(NK):
                    lastl = e.matmul(
                        bk[e_ % 2][0:128, 0:512],
                        d["owb"][:, :].rearrange("p (j c) -> p j c", j=NK)
                        [:, kk, 128 * e_:128 * (e_ + 1)],
                        Xw[:, 512 * kk:512 * (kk + 1)],
                        start=(kk == 0), stop=(kk == NK - 1))
                return lastl
            bw = [f"xw{t4}_{NK-1}", "ld_ow"]
            if e_ >= 2:
                bw.append(f"sadd{t4}_{e_-2}")
            elif t4 >= 1:
                bw.append(f"sadd{t4-1}_{e_+6}")
            ev.do("tensor", pmm, sig=f"pmm{t4}_{e_}", sem="tensor", waits=tuple(bw))
            sw = [f"pmm{t4}_{e_}", "ld_ob"]
            if t4 >= 1:
                sw.append(f"tmm{t4-1}_{24+e_}")
            ev.do("vector", lambda e, e_=e_: e.tensor_scalar(
                d["Pt"][:, 512 * e_:512 * (e_ + 1)], bk[e_ % 2][0:128, 0:512],
                d["obf"][:, e_:e_ + 1], None, OP.add),
                sig=f"sadd{t4}_{e_}", sem="vector", waits=tuple(sw))
            ev.do("vector", lambda e, e_=e_, gb=gb: e.tensor_tensor(
                d["Pt"][:, 512 * e_:512 * (e_ + 1)], d["Pt"][:, 512 * e_:512 * (e_ + 1)],
                gb[:, 512 * e_:512 * (e_ + 1)], OP.mult),
                sig=f"prod{t4}_{e_}", sem="vector",
                waits=(f"sadd{t4}_{e_}", f"g{t4}_{NE-1}"))
        for tc in range(4):
            for e_ in range(NE):
                idx = tc * NE + e_
                tw = [f"prod{t4}_{NE-1}"] if idx == 0 else []
                if t4 == 0 and idx == 0:
                    tw.append("ld_id")
                if idx >= 2:
                    tw.append(f"radd{t4}_{idx-2}")
                elif t4 >= 1:
                    tw.append(f"radd{t4-1}_{idx+30}")
                ev.do("tensor", lambda e, tc=tc, e_=e_, idx=idx: e.transpose(
                    bk[2 + idx % 2][0:128, 0:128],
                    d["Pt"][:, 512 * e_ + 128 * tc:512 * e_ + 128 * (tc + 1)],
                    d["identf"][:, :]),
                    sig=f"tmm{t4}_{idx}", sem="tensor", waits=tuple(tw))
                rw = [f"tmm{t4}_{idx}", f"x{t4}_3"]
                if t4 >= 2:
                    rw.append(f"ost{t4-2}_{tc}")
                ev.do("vector", lambda e, tc=tc, e_=e_, idx=idx, outb=outb, xb=xb:
                      e.tensor_tensor(
                          outb[:, E * tc + 128 * e_:E * tc + 128 * (e_ + 1)],
                          bk[2 + idx % 2][0:128, 0:128],
                          xb[:, E * tc + 128 * e_:E * tc + 128 * (e_ + 1)], OP.add),
                      sig=f"radd{t4}_{idx}", sem="vector", waits=tuple(rw))
            ev.dma("gpsimd", "out", lambda e, tc=tc, t4=t4, outb=outb: e.dma_start(
                out=d["out"].ap()[512 * t4 + 128 * tc:512 * t4 + 128 * (tc + 1), :],
                in_=outb[:, E * tc:E * (tc + 1)]),
                sig=f"ost{t4}_{tc}", waits=(f"radd{t4}_{tc * NE + NE - 1}",))
    ev.do("gpsimd", waits=("ost3_3",))


def body_L5o(nc, ev, d, ctx, declare):
    if declare:
        if "w" not in d:
            d["w"] = nc.dram_tensor("w", [B, 3, N, 128], BF16, kind="ExternalInput")
        if "owh" not in d:
            d["owh"] = nc.dram_tensor("owh", [DH, E], F32, kind="ExternalInput")
        if "spart" not in d:
            d["spart"] = nc.dram_tensor("spart", [B * N, E], BF16, kind="ExternalOutput")
        sbuf_decl(nc, ctx, d, [(f"Xo{p}", [128, 3 * 512], BF16,
                                dict(is_transpose_dest=True)) for p in range(2)])
        sbuf_decl(nc, ctx, d, [("owb", [128, 3 * E], BF16),
                               ("stg0", [128, 4 * E], BF16), ("stg1", [128, 4 * E], BF16)])
        d["bk"] = [ctx.enter_context(nc.psum_tensor(f"bk{i}", [128, 512], F32))
                   for i in range(4)]
        return
    bk = d["bk"]
    ev.dma("gpsimd", "ow", lambda e: e.dma_start(
        out=d["owb"][:, :].rearrange("p (o c) -> p o c", o=3),
        in_=d["owh"].ap().rearrange("(o p) c -> p o c", p=128)), sig="ld_ow")
    for st in range(32):
        b, s = st // 8, st % 8
        p = st % 2
        Xo, stg = d[f"Xo{p}"], d[f"stg{p}"]
        for oc in range(3):
            ev.dma("sync", "xw", lambda e, b=b, s=s, oc=oc, Xo=Xo: e.dma_start(
                out=Xo[:, 512 * oc:512 * (oc + 1)],
                in_=d["w"].ap()[b, oc, 512 * s:512 * (s + 1), :], transpose=True),
                sig=f"xw{st}_{oc}",
                waits=(f"omm{st-2}_3_1",) if (st >= 2 and oc == 0) else ())
        for tc in range(4):
            for hf in range(2):
                bki = 2 * (tc % 2) + hf
                def omm(e, tc=tc, hf=hf, bki=bki, Xo=Xo):
                    lastl = None
                    for oc in range(3):
                        lastl = e.matmul(
                            bk[bki][0:128, 0:512],
                            Xo[:, 512 * oc + 128 * tc:512 * oc + 128 * (tc + 1)],
                            d["owb"][:, :].rearrange("p (o c) -> p o c", o=3)
                            [:, oc, 512 * hf:512 * (hf + 1)],
                            start=(oc == 0), stop=(oc == 2))
                    return lastl
                mw = [f"xw{st}_2", "ld_ow"]
                if tc >= 2:
                    mw.append(f"oev{st}_{tc-2}_{hf}")
                elif st >= 1:
                    mw.append(f"oev{st-1}_{tc+2}_{hf}")
                ev.do("tensor", omm, sig=f"omm{st}_{tc}_{hf}", sem="tensor",
                      waits=tuple(mw))
                sw = [f"omm{st}_{tc}_{hf}"]
                if st >= 2 and tc == 0 and hf == 0:
                    sw.append(f"sst{st-2}_3")
                ev.do("scalar", lambda e, tc=tc, hf=hf, bki=bki, stg=stg: e.copy(
                    stg[:, E * tc + 512 * hf:E * tc + 512 * (hf + 1)],
                    bk[bki][0:128, 0:512]),
                    sig=f"oev{st}_{tc}_{hf}", sem="scalar", waits=tuple(sw))
            ev.dma("gpsimd", "sp", lambda e, st=st, tc=tc, stg=stg: e.dma_start(
                out=d["spart"].ap()[512 * st + 128 * tc:512 * st + 128 * (tc + 1), 0:E],
                in_=stg[:, E * tc:E * (tc + 1)]),
                sig=f"sst{st}_{tc}", waits=(f"oev{st}_{tc}_1",))
        if st % 4 == 3 and d.get("_rs_hook") is not None:
            d["_rs_hook"](st)
    ev.do("gpsimd", waits=(f"sst31_3",))


def body_L6(nc, ev, d, ctx, declare):
    if declare:
        if "sred" not in d:
            d["sred"] = nc.dram_tensor("sred", [TSH, E], BF16, kind="ExternalInput")
        if "gout" not in d:
            d["gout"] = nc.dram_tensor("gout", [E, TSH], BF16, kind="ExternalInput")
        if "x" not in d:
            d["x"] = nc.dram_tensor("x", [TSH, E], F32, kind="ExternalInput")
        if "obT" not in d:
            d["obT"] = nc.dram_tensor("obT", [1, E], F32, kind="ExternalInput")
        if "cones" not in d:
            d["cones"] = nc.dram_tensor("c_ones", [128, 128], F32, kind="ExternalInput")
        d["out"] = nc.dram_tensor("out", [TSH, E], F32, kind="ExternalOutput")
        sbuf_decl(nc, ctx, d, [(f"gt{p}", [128, E], BF16,
                                dict(is_transpose_dest=True)) for p in range(2)])
        sbuf_decl(nc, ctx, d, [
            ("st0", [128, E], BF16), ("st1", [128, E], BF16),
            ("xt0", [128, E], F32), ("xt1", [128, E], F32),
            ("ot0", [128, E], F32), ("ot1", [128, E], F32),
            ("sf0", [128, E], F32), ("sf1", [128, E], F32),
            ("obr", [1, E], F32), ("onesf", [128, 128], F32), ("obts", [128, E], F32)])
        d["bk"] = [ctx.enter_context(nc.psum_tensor(f"bk{i}", [128, 512], F32))
                   for i in range(2)]
        return
    bk = d["bk"]
    ev.dma("sync", "misc", lambda e: e.dma_start(out=d["obr"][:, :], in_=d["obT"].ap()),
           sig="ld_ob")
    ev.dma("sync", "misc", lambda e: e.dma_start(out=d["onesf"][:, :], in_=d["cones"].ap()),
           sig="ld_on")
    for hf in range(2):
        ev.do("tensor", lambda e, hf=hf: e.matmul(
            bk[hf][0:128, 0:512], d["onesf"][0:1, 0:128],
            d["obr"][0:1, 512 * hf:512 * (hf + 1)], start=True, stop=True),
            sig=f"obm{hf}", sem="tensor", waits=("ld_ob", "ld_on"))
        ev.do("scalar", lambda e, hf=hf: e.copy(
            d["obts"][:, 512 * hf:512 * (hf + 1)], bk[hf][0:128, 0:512]),
            sig=f"obe{hf}", sem="scalar", waits=(f"obm{hf}",))
    for tc in range(16):
        p = tc % 2
        gt, stt, xt, ot, sf = (d[f"gt{p}"], d[f"st{p}"], d[f"xt{p}"],
                               d[f"ot{p}"], d[f"sf{p}"])
        lw = (f"cmb{tc-2}",) if tc >= 2 else ()
        ev.dma("sync", "s", lambda e, tc=tc, stt=stt: e.dma_start(
            out=stt[:, :], in_=d["sred"].ap()[128 * tc:128 * (tc + 1), :]),
            sig=f"ls{tc}", waits=(f"sadd{tc-2}",) if tc >= 2 else ())
        for hf in range(2):
            ev.dma("sync", "g", lambda e, tc=tc, hf=hf, gt=gt: e.dma_start(
                out=gt[:, 512 * hf:512 * (hf + 1)],
                in_=d["gout"].ap()[512 * hf:512 * (hf + 1),
                                   128 * tc:128 * (tc + 1)], transpose=True),
                sig=f"lg{tc}_{hf}", waits=lw if hf == 0 else ())
        ev.dma("gpsimd", "x", lambda e, tc=tc, xt=xt: e.dma_start(
            out=xt[:, :], in_=d["x"].ap()[128 * tc:128 * (tc + 1), :]),
            sig=f"lx{tc}", waits=(f"cmb{tc-2}",) if tc >= 2 else ())
        ev.do("vector", lambda e, stt=stt, sf=sf: e.tensor_tensor(
            sf[:, :], stt[:, :], d["obts"][:, :], OP.add),
            sig=f"sadd{tc}", sem="vector",
            waits=(f"ls{tc}", "obe1") + ((f"mul{tc-2}",) if tc >= 2 else ()))
        ev.do("vector", lambda e, sf=sf, gt=gt: e.tensor_tensor(
            sf[:, :], sf[:, :], gt[:, :], OP.mult),
            sig=f"mul{tc}", sem="vector", waits=(f"sadd{tc}", f"lg{tc}_1"))
        ev.do("vector", lambda e, sf=sf, xt=xt, ot=ot: e.tensor_tensor(
            ot[:, :], sf[:, :], xt[:, :], OP.add),
            sig=f"cmb{tc}", sem="vector",
            waits=(f"mul{tc}", f"lx{tc}") + ((f"ost{tc-2}",) if tc >= 2 else ()))
        ev.dma("gpsimd", "o", lambda e, tc=tc, ot=ot: e.dma_start(
            out=d["out"].ap()[128 * tc:128 * (tc + 1), :], in_=ot[:, :]),
            sig=f"ost{tc}", waits=(f"cmb{tc}",))
    ev.do("gpsimd", waits=("ost15",))


class PEV:

    def __init__(self, ev, pfx, state, exclude=()):
        self.ev = ev
        self.pfx = pfx
        self.state = state
        self.pend = dict(ev.cnt)
        self.exclude = exclude
        self.fired = False

    @property
    def events(self):
        return self.ev.events

    def _pw(self, waits):
        return tuple((w if w is None else self.pfx + w) for w in waits)

    def _barrier(self, engine):
        if self.fired or engine != self.ev.active:
            return
        self.fired = True
        own = {"tensor": "tensor", "vector": "vector", "scalar": "scalar"}.get(engine)
        lw = self.state["lastwait"]
        for s, v in self.pend.items():
            if v > lw.get(s, 0) and s != own and s not in self.exclude:
                self.ev.eng.wait_ge(self.ev.sems[s], v)
                lw[s] = v

    def do(self, engine, emit=None, sig=None, sem=None, step=None, waits=()):
        self._barrier(engine)
        return self.ev.do(engine, emit,
                          sig=None if sig is None else self.pfx + sig,
                          sem=sem, step=step, waits=self._pw(waits))

    def dma(self, engine, chan, emit=None, sig=None, waits=()):
        self._barrier(engine)
        q = "dmas" if engine in ("sync", "scalar") else "dmag"
        key = (q, self.pfx + chan)
        cm = self.state["chanmap"]
        if key not in cm:
            cm[key] = str(self.state["chancnt"][q] % NDMACH)
            self.state["chancnt"][q] += 1
        return self.ev.dma(engine, cm[key], emit,
                           sig=None if sig is None else self.pfx + sig,
                           waits=self._pw(waits))

    def alias(self, new, old):
        self.ev.alias(self.pfx + new, self.pfx + old)


RG8 = [list(range(8))]


class NCNames:

    def __init__(self, nc, pfx):
        self._nc = nc
        self._pfx = pfx

    def sbuf_tensor(self, name, shape, dt, **kw):
        return self._nc.sbuf_tensor(self._pfx + name, shape, dt, **kw)

    def psum_tensor(self, name, shape, dt, **kw):
        return self._nc.psum_tensor(self._pfx + name, shape, dt, **kw)

    def __getattr__(self, a):
        return getattr(self._nc, a)


def body_fused(nc, ev, d, ctx, declare):
    if declare:
        share = {}
        share["x"] = nc.dram_tensor("x", [TSH, E], F32, kind="ExternalInput")
        share["xn"] = nc.dram_tensor("xn", [TSH, E], BF16, kind="Internal")
        share["xnag"] = nc.dram_tensor("xnag", [B * N, E], BF16, kind="Internal")
        share["vsig"] = nc.dram_tensor("vsig", [B, 3, 128, M], BF16, kind="Internal")
        share["usig"] = nc.dram_tensor("usig", [B, 3, 128, M], BF16, kind="Internal")
        share["asig"] = nc.dram_tensor("asig", [3, 128, M], BF16, kind="Internal")
        share["w"] = nc.dram_tensor("w", [B, 3, N, 128], BF16, kind="Internal")
        share["cfml"] = nc.dram_tensor("cfml", [NEXP, TSH], BF16, kind="Internal")
        share["cfma"] = nc.dram_tensor("cfma", [128, TSH], BF16, kind="Internal")
        share["spfg"] = nc.dram_tensor("spfg", [B * N, E], BF16, kind="Internal")
        share["spfs"] = nc.dram_tensor("spfs", [B * N, E], BF16, kind="Internal")
        share["spart"] = share["spfs"]
        share["spfrg"] = nc.dram_tensor("spfrg", [TSH, E], BF16, kind="Internal")
        share["spfrs"] = nc.dram_tensor("spfrs", [TSH, E], BF16, kind="Internal")
        share["xr"] = nc.dram_tensor("xr", [TSH, E], F32, kind="ExternalInput")
        share["cones"] = nc.dram_tensor("c_ones", [128, 128], F32, kind="ExternalInput")
        share["cident"] = nc.dram_tensor("c_ident", [128, 128], F32, kind="ExternalInput")
        seeds = {
            "p1": ("x", "xn"),
            "p4r": ("xn", "cident", "cfml"),
            "p2": ("xnag", "vsig", "usig"),
            "p2b": ("cones", "asig"),
            "p4e": ("xnag", "cfma"),
            "p3": ("vsig", "usig", "asig", "w"),
            "p5": ("w", "spart"),
            "p6": ("x", "cones"),
        }
        d["_phases"] = []
        pctx5 = None
        for pfx, body in [("p1", body_L1), ("p4r", body_L4r), ("p2", body_L2),
                          ("p2b", body_L2b), ("p4e", body_L4e), ("p3", body_L3),
                          ("p5", body_L5o), ("p6", body_L6n)]:
            dp = {k: share[k] for k in seeds[pfx]}
            if pfx == "p4e":
                dp["spf"] = share["spfg"]
            if pfx == "p6":
                dp["spfrs"] = share["spfrs"]
                dp["spfrg"] = share["spfrg"]
                dp["x"] = share["xr"]
            pctx = ExitStack()
            body(NCNames(nc, pfx + "_"), None, dp, pctx, True)
            if pfx == "p5":
                pctx5 = pctx
            elif pfx == "p6":
                pctx.close()
                pctx5.close()
            else:
                pctx.close()
            d["_phases"].append((pfx, body, dp))
        d["_share"] = share
        return

    share = d["_share"]
    state = {"lastwait": {}, "chanmap": {}, "chancnt": {"dmas": 0, "dmag": 0}}
    pev6 = None
    for pfx, body, dp in d["_phases"]:
        if pfx == "p5":
            pev6 = PEV(ev, "p6:", state)

            def rs_hook(st, ev=ev, share=share):
                if st not in (15, 31):
                    return
                c = st // 16
                ev.do("gpsimd", lambda e, c=c: e.collective_compute(
                    "ReduceScatter", OP.add, replica_groups=RG8,
                    ins=[share["spfs"].ap()[8192 * c:8192 * (c + 1), :].opt()],
                    outs=[share["spfrs"].ap()[1024 * c:1024 * (c + 1), :].opt()]),
                    sig=f"rss{c}", sem="tb", waits=(f"p5:sst{st}_3",))
            dp["_rs_hook"] = rs_hook
        with nc.named_scope(pfx):
            pev = pev6 if pfx == "p6" else PEV(ev, pfx + ":", state,
                                              exclude=("tb",) if pfx in ("p4r", "p3", "p5") else ())
            body(nc, pev, dp, None, False)
        if pfx == "p4e":
            with nc.named_scope("rsg"):
                for c in range(2):
                    ev.do("gpsimd", lambda e, c=c: e.collective_compute(
                        "ReduceScatter", OP.add, replica_groups=RG8,
                        ins=[share["spfg"].ap()[8192 * c:8192 * (c + 1), :].opt()],
                        outs=[share["spfrg"].ap()[1024 * c:1024 * (c + 1), :].opt()]),
                        sig=f"rsg{c}", sem="tb", waits=("p4e:fst31_3",))
        if pfx == "p5":
            for c in range(2):
                ev.alias(f"p6:rss{c}", f"rss{c}")
                ev.alias(f"p6:rsg{c}", f"rsg{c}")
        if pfx == "p1":
            with nc.named_scope("ag"):
                cev = PEV(ev, "ag:", state)
                cev.do("gpsimd", lambda e: e.collective_compute(
                    "AllGather", OP.bypass, replica_groups=RG8,
                    ins=[share["xn"].ap().opt()], outs=[share["xnag"].ap().opt()]),
                    sig="done", sem="tb")
        if pfx == "p4r":
            with nc.named_scope("agc"):
                cev = PEV(ev, "agc:", state)
                cev.do("gpsimd", lambda e: e.collective_compute(
                    "AllGather", OP.bypass, replica_groups=RG8,
                    ins=[share["cfml"].ap().opt()], outs=[share["cfma"].ap().opt()]),
                    sig="done", sem="tb")


def body_L6n(nc, ev, d, ctx, declare):
    if declare:
        ddecl(nc, d, "spfrs", "spfrs", [TSH, E], BF16, "ExternalInput")
        ddecl(nc, d, "spfrg", "spfrg", [TSH, E], BF16, "ExternalInput")
        ddecl(nc, d, "x", "x", [TSH, E], F32, "ExternalInput")
        ddecl(nc, d, "obT", "obT", [1, E], F32, "ExternalInput")
        ddecl(nc, d, "cones", "c_ones", [128, 128], F32, "ExternalInput")
        d["out"] = nc.dram_tensor("out", [TSH, E], F32, kind="ExternalOutput")
        sbuf_decl(nc, ctx, d, [
            ("sft0", [128, 2 * E], BF16), ("sft1", [128, 2 * E], BF16),
            ("xt0", [128, E], F32), ("xt1", [128, E], F32),
            ("ot0", [128, E], F32), ("ot1", [128, E], F32),
            ("gtf0", [128, E], F32), ("gtf1", [128, E], F32),
            ("sfw0", [128, E], F32), ("sfw1", [128, E], F32),
            ("obr", [1, E], F32), ("onesf", [128, 128], F32), ("obts", [128, E], F32)])
        d["bk"] = [ctx.enter_context(nc.psum_tensor(f"bk{i}", [128, 512], F32))
                   for i in range(2)]
        return
    bk = d["bk"]
    ev.dma("sync", "misc", lambda e: e.dma_start(out=d["obr"][:, :], in_=d["obT"].ap()),
           sig="ld_ob")
    ev.dma("sync", "misc", lambda e: e.dma_start(out=d["onesf"][:, :], in_=d["cones"].ap()),
           sig="ld_on")
    for hf in range(2):
        ev.do("tensor", lambda e, hf=hf: e.matmul(
            bk[hf][0:128, 0:512], d["onesf"][0:1, 0:128],
            d["obr"][0:1, 512 * hf:512 * (hf + 1)], start=True, stop=True),
            sig=f"obm{hf}", sem="tensor", waits=("ld_ob", "ld_on"))
        ev.do("scalar", lambda e, hf=hf: e.copy(
            d["obts"][:, 512 * hf:512 * (hf + 1)], bk[hf][0:128, 0:512]),
            sig=f"obe{hf}", sem="scalar", waits=(f"obm{hf}",))
    for tc in range(16):
        p = tc % 2
        sft, xt, ot, gtf, sfw = (d[f"sft{p}"], d[f"xt{p}"], d[f"ot{p}"],
                                 d[f"gtf{p}"], d[f"sfw{p}"])
        lw = [f"rss{tc // 8}", f"rsg{tc // 8}"] if tc % 8 == 0 else []
        if tc >= 2:
            lw += [f"sadd{tc-2}", f"gsil{tc-2}"]
        ev.dma("sync", "s", lambda e, tc=tc, sft=sft: e.dma_start(
            out=sft[:, 0:E], in_=d["spfrs"].ap()[128 * tc:128 * (tc + 1), :]),
            sig=f"lss{tc}", waits=tuple(lw))
        ev.dma("sync", "s", lambda e, tc=tc, sft=sft: e.dma_start(
            out=sft[:, E:2 * E], in_=d["spfrg"].ap()[128 * tc:128 * (tc + 1), :]),
            sig=f"ls{tc}")
        ev.dma("gpsimd", "x", lambda e, tc=tc, xt=xt: e.dma_start(
            out=xt[:, :], in_=d["x"].ap()[128 * tc:128 * (tc + 1), :]),
            sig=f"lx{tc}", waits=(f"cmb{tc-2}",) if tc >= 2 else ())
        ev.do("scalar", lambda e, sft=sft, gtf=gtf: e.activation(
            gtf[:, :], sft[:, E:2 * E], AF.Silu),
            sig=f"gsil{tc}", sem="scalar",
            waits=(f"ls{tc}",) + ((f"mul{tc-2}",) if tc >= 2 else ()))
        ev.do("vector", lambda e, sft=sft, sfw=sfw: e.tensor_tensor(
            sfw[:, :], sft[:, 0:E], d["obts"][:, :], OP.add),
            sig=f"sadd{tc}", sem="vector",
            waits=(f"ls{tc}", "obe1") + ((f"mul{tc-2}",) if tc >= 2 else ()))
        ev.do("vector", lambda e, sfw=sfw, gtf=gtf: e.tensor_tensor(
            sfw[:, :], sfw[:, :], gtf[:, :], OP.mult),
            sig=f"mul{tc}", sem="vector", waits=(f"sadd{tc}", f"gsil{tc}"))
        ev.do("vector", lambda e, sfw=sfw, xt=xt, ot=ot: e.tensor_tensor(
            ot[:, :], sfw[:, :], xt[:, :], OP.add),
            sig=f"cmb{tc}", sem="vector",
            waits=(f"mul{tc}", f"lx{tc}") + ((f"ost{tc-2}",) if tc >= 2 else ()))
        ev.dma("gpsimd", "o", lambda e, tc=tc, ot=ot: e.dma_start(
            out=d["out"].ap()[128 * tc:128 * (tc + 1), :], in_=ot[:, :]),
            sig=f"ost{tc}", waits=(f"cmb{tc}",))
    ev.do("gpsimd", waits=("ost15",))


def fused_inmaps(ins):
    kc = conv_consts()
    ones128 = np.ones((128, 128), np.float32)
    C8 = list(range(NCORE))
    x = ins["L0_attn_x"]
    rwT = np.ascontiguousarray(ins["L0_ffn_router_w"].T)
    corr = np.ascontiguousarray(ins["L0_ffn_correction_bias"][:, None])
    sgwT = np.ascontiguousarray(ins["L0_ffn_shared_gate_w"].T)
    suwT = np.ascontiguousarray(ins["L0_ffn_shared_up_w"].T)
    sdw = ins["L0_ffn_shared_down_w"].T
    obT = np.ascontiguousarray(ins["L0_attn_o_b"][None, :])

    def fsel(h):
        f = np.zeros((8, 2, 128, 128), np.float32)
        for j in range(8):
            for el in range(2):
                f[j, el, 16 * j + 2 * h + el, :] = 1.0
        return f.astype(ml_dtypes.bfloat16)
    xf = np.ascontiguousarray(x.reshape(B * N, E))
    im = []
    for k in C8:
        b, th, h = k // 2, k % 2, k
        im.append({
            "x": np.ascontiguousarray(x[b, 2048 * th:2048 * (th + 1), :]),
            "xr": np.ascontiguousarray(np.concatenate(
                [xf[8192 * c + 1024 * k:8192 * c + 1024 * k + 1024] for c in range(2)],
                axis=0)),
            "uw": np.ascontiguousarray(ins["L0_attn_u_w"][:, DH * h:DH * (h + 1)]),
            "vw": np.ascontiguousarray(ins["L0_attn_v_w"][:, DH * h:DH * (h + 1)]),
            "ub": np.ascontiguousarray(ins["L0_attn_u_b"][DH * h:DH * (h + 1), None]),
            "vb": np.ascontiguousarray(ins["L0_attn_v_b"][DH * h:DH * (h + 1), None]),
            "posw": ins["L0_attn_pos_w"],
            "posb": np.ascontiguousarray(ins["L0_attn_pos_b"][:, None]),
            "lw0": ins["L0_attn_lw0"],
            "lb0": np.ascontiguousarray(ins["L0_attn_lb0"][:, None]),
            "lw1": ins["L0_attn_lw1"],
            "lb1": np.ascontiguousarray(ins["L0_attn_lb1"][:, None]),
            "lw2": ins["L0_attn_lw2"],
            "lb2": np.ascontiguousarray(ins["L0_attn_lb2"][:, None]),
            "outw": np.ascontiguousarray(ins["L0_attn_out_w"][:, DH * h:DH * (h + 1)]),
            "outb": np.ascontiguousarray(ins["L0_attn_out_b"][DH * h:DH * (h + 1), None]),
            "c_ramp": kc["c_ramp"], "c_decay": kc["c_decay"], "c_ones": ones128,
            "c_d1mov": kc["c_d1mov"], "c_wf": kc["c_wf"], "c_e2": kc["c_e2"],
            "c_tinv": kc["c_tinv"], "c_cstk": kc["c_cstk"],
            "rwT": rwT, "corr": corr,
            "gwh": np.ascontiguousarray(ins["L0_ffn_gate_w"][2 * h:2 * h + 2]),
            "uwh": np.ascontiguousarray(ins["L0_ffn_up_w"][2 * h:2 * h + 2]),
            "dwh": np.ascontiguousarray(ins["L0_ffn_down_w"][2 * h:2 * h + 2]),
            "sgwh": np.ascontiguousarray(sgwT[:, HR * h:HR * (h + 1)]),
            "suwh": np.ascontiguousarray(suwT[:, HR * h:HR * (h + 1)]),
            "sdwh": np.ascontiguousarray(sdw[HR * h:HR * (h + 1), :]),
            "fselb": fsel(h),
            "c_ident": kc["c_ident"],
            "owh": np.ascontiguousarray(ins["L0_attn_o_w"][DH * h:DH * (h + 1), :]),
            "obT": obT,
        })
    return im


def kernel(**inputs):
    ins = {k: np.asarray(v) for k, v in inputs.items()}
    C8 = list(range(NCORE))
    nc = build_program(body_fused)
    im = fused_inmaps(ins)
    res = run_bass_kernel_spmd(nc, im, core_ids=C8)
    outs = [np.asarray(res.results[k]["out"], np.float32) for k in C8]
    outg = np.empty((B * N, E), np.float32)
    for k in C8:
        for c in range(2):
            outg[8192 * c + 1024 * k:8192 * c + 1024 * k + 1024] = \
                outs[k][1024 * c:1024 * (c + 1)]
    return outg.reshape(B, N, E)


def kernel_multi(**inputs):
    ins = {k: np.asarray(v) for k, v in inputs.items()}
    kc = conv_consts()
    ones128 = np.ones((128, 128), np.float32)
    C8 = list(range(NCORE))
    x = ins["L0_attn_x"]

    nc1 = build_program(body_L1)
    im = [{"x": np.ascontiguousarray(x[k // 2, 2048 * (k % 2):2048 * (k % 2 + 1)])}
          for k in C8]
    r1 = run_bass_kernel_spmd(nc1, im, core_ids=C8)
    xnag = np.concatenate([np.asarray(r1.results[k]["xn"]) for k in C8], axis=0)

    nc2 = build_program(body_L2)
    im = [{"xnag": xnag,
           "uw": np.ascontiguousarray(ins["L0_attn_u_w"][:, DH * h:DH * (h + 1)]),
           "vw": np.ascontiguousarray(ins["L0_attn_v_w"][:, DH * h:DH * (h + 1)]),
           "ub": np.ascontiguousarray(ins["L0_attn_u_b"][DH * h:DH * (h + 1), None]),
           "vb": np.ascontiguousarray(ins["L0_attn_v_b"][DH * h:DH * (h + 1), None])}
          for h in C8]
    r2 = run_bass_kernel_spmd(nc2, im, core_ids=C8)

    nc2b = build_program(body_L2b)
    im = [{"posw": ins["L0_attn_pos_w"],
           "posb": np.ascontiguousarray(ins["L0_attn_pos_b"][:, None]),
           "lw0": ins["L0_attn_lw0"], "lb0": np.ascontiguousarray(ins["L0_attn_lb0"][:, None]),
           "lw1": ins["L0_attn_lw1"], "lb1": np.ascontiguousarray(ins["L0_attn_lb1"][:, None]),
           "lw2": ins["L0_attn_lw2"], "lb2": np.ascontiguousarray(ins["L0_attn_lb2"][:, None]),
           "outw": np.ascontiguousarray(ins["L0_attn_out_w"][:, DH * h:DH * (h + 1)]),
           "outb": np.ascontiguousarray(ins["L0_attn_out_b"][DH * h:DH * (h + 1), None]),
           "c_ramp": kc["c_ramp"], "c_decay": kc["c_decay"], "c_ones": ones128}
          for h in C8]
    r2b = run_bass_kernel_spmd(nc2b, im, core_ids=C8)

    nc3 = build_program(body_L3)
    im = [{"vsig": np.asarray(r2.results[h]["vsig"]),
           "usig": np.asarray(r2.results[h]["usig"]),
           "asig": np.asarray(r2b.results[h]["asig"]),
           "c_d1mov": kc["c_d1mov"], "c_wf": kc["c_wf"], "c_e2": kc["c_e2"],
           "c_tinv": kc["c_tinv"], "c_cstk": kc["c_cstk"]}
          for h in C8]
    r3 = run_bass_kernel_spmd(nc3, im, core_ids=C8)

    nc4a = build_program(body_L4a)
    rwT = np.ascontiguousarray(ins["L0_ffn_router_w"].T)
    corr = np.ascontiguousarray(ins["L0_ffn_correction_bias"][:, None])
    sgwT = np.ascontiguousarray(ins["L0_ffn_shared_gate_w"].T)
    suwT = np.ascontiguousarray(ins["L0_ffn_shared_up_w"].T)
    sdwT = np.ascontiguousarray(ins["L0_ffn_shared_down_w"].T)
    im = [{"xn": np.ascontiguousarray(xnag[2048 * k:2048 * (k + 1)]),
           "rwT": rwT, "corr": corr,
           "gw": ins["L0_ffn_gate_w"], "uw": ins["L0_ffn_up_w"],
           "dw": ins["L0_ffn_down_w"],
           "sgwT": sgwT, "suwT": suwT, "sdwT": sdwT,
           "c_ident": kc["c_ident"], "c_ones": ones128, "c_sel": kc["c_sel"]}
          for k in C8]
    r4a = run_bass_kernel_spmd(nc4a, im, core_ids=C8)

    nc4b = build_program(body_L4b)
    wh = [np.asarray(r3.results[h]["w"]) for h in C8]
    ob = np.ascontiguousarray(ins["L0_attn_o_b"][:, None])
    im = []
    for k in C8:
        b, th = k // 2, k % 2
        wsl = np.stack([wh[h][b, :, 2048 * th:2048 * (th + 1), :] for h in C8], axis=0)
        im.append({"wsl": np.ascontiguousarray(wsl),
                   "g": np.asarray(r4a.results[k]["gout"]),
                   "x": np.ascontiguousarray(x[b, 2048 * th:2048 * (th + 1), :]),
                   "ow": ins["L0_attn_o_w"], "ob": ob, "c_ident": kc["c_ident"]})
    r4b = run_bass_kernel_spmd(nc4b, im, core_ids=C8)
    outs = [np.asarray(r4b.results[k]["out"], np.float32) for k in C8]
    return np.stack([np.concatenate([outs[2 * b], outs[2 * b + 1]], axis=0)
                     for b in range(4)], axis=0)


# version 19 (final)
# speedup vs baseline: 1.1333x; 1.0118x over previous
from contextlib import ExitStack
import numpy as np
import ml_dtypes
import concourse.bass as bass
import concourse.mybir as mybir
from concourse.bass_utils import run_bass_kernel_spmd

F32 = mybir.dt.float32
BF16 = mybir.dt.bfloat16
AF = mybir.ActivationFunctionType
OP = mybir.AluOpType
AX = mybir.AxisListType

NCORE = 8
B, N, E = 4, 4096, 1024
TSH = 2048
H, DH, D1 = 8, 384, 3072
RW = 512
GAMMA, EPS_SRMS, EPS_LN = 0.99, 1e-8, 1e-5
M = 8192
NF1 = 65
NEXP, NGRP, EPG, TOPK = 16, 4, 4, 4
HS, HR = 2048, 256

ENGS = ("sync", "tensor", "vector", "scalar", "gpsimd")
NDMACH = 12
SEMS = (("tensor", "vector", "scalar", "gpsimd_e", "vb", "sb", "tb")
        + tuple(f"dmas_{i}" for i in range(NDMACH))
        + tuple(f"dmag_{i}" for i in range(NDMACH)))


def conv_consts():
    i_ = np.arange(64, dtype=np.float64)
    j_ = np.arange(64, dtype=np.float64)
    f1 = np.arange(NF1, dtype=np.float64)
    f2 = np.arange(64, dtype=np.float64)
    out = {}
    Dm = np.exp(-2j * np.pi * np.outer(i_, f1) / 128.0)
    out["c_d1mov"] = np.concatenate([Dm.real, Dm.imag], axis=1).astype(np.float32)
    ph = (-2 * np.pi / M) * np.einsum(
        "j,f->jf", j_, (128 * f2[None, :] + f1[:, None]).reshape(-1)
    ).reshape(64, NF1, 64)
    wf = np.stack([np.cos(ph), np.sin(ph), -np.sin(ph)], axis=2)
    out["c_wf"] = wf.reshape(64, NF1 * 3 * 64).astype(np.float32)
    phe = 2 * np.pi * np.outer(f2, j_) / 64.0
    out["c_e2"] = np.concatenate(
        [np.cos(phe), np.sin(phe), -np.sin(phe)], axis=1).astype(np.float32)
    pht = 2 * np.pi * np.outer(f1, j_) / M
    out["c_tinv"] = np.concatenate([np.cos(pht), np.sin(pht)], axis=1).astype(np.float32)
    w = np.full(NF1, 2.0); w[0] = 1.0; w[64] = 1.0
    ang = 2 * np.pi * np.outer(f1, i_) / 128.0
    Crs = (w[:, None] / M * np.cos(ang))
    Cis = (-w[:, None] / M * np.sin(ang))
    out["c_cstk"] = np.concatenate([Crs[0:64], Crs[64:65], Cis[1:64]],
                                   axis=0).astype(np.float32)
    out["c_ramp"] = np.arange(N, dtype=np.float64)[None, :].astype(np.float32)
    out["c_decay"] = (GAMMA ** np.arange(N, dtype=np.float64))[None, :].astype(np.float32)
    out["c_ones"] = np.ones((128, 128), np.float32)
    out["c_ident"] = np.eye(128, dtype=np.float32)
    sel = np.zeros((NEXP, NEXP * 128), np.float32)
    for e in range(NEXP):
        sel[e, e * 128:(e + 1) * 128] = 1.0
    out["c_sel"] = sel
    return out


class EV:

    def __init__(self, active, engine, sems):
        self.active = active
        self.eng = engine
        self.sems = sems
        self.cnt = {k: 0 for k in sems}
        self.events = {}
        self._chanidx = {}
        self._chanlast = {}

    def do(self, engine, emit=None, sig=None, sem=None, step=None, waits=()):
        for w in waits:
            if w is None:
                continue
            if engine == self.active:
                s, v = self.events[w]
                self.eng.wait_ge(self.sems[s], v)
        last = None
        if engine == self.active and emit is not None:
            last = emit(self.eng)
        if sig is not None:
            semn = sem if sem is not None else engine
            stp = step if step is not None else (16 if semn.startswith("dma") else 1)
            self.cnt[semn] += stp
            self.events[sig] = (semn, self.cnt[semn])
            if engine == self.active:
                assert last is not None, f"signal {sig} without instruction"
                last.then_inc(self.sems[semn], stp)
        return last

    def alias(self, new, old):
        self.events[new] = self.events[old]

    def dma(self, engine, chan, emit=None, sig=None, waits=()):
        q = "dmas" if engine in ("sync", "scalar") else "dmag"
        key = (q, chan)
        if key not in self._chanidx:
            self._chanidx[key] = len([k for k in self._chanidx if k[0] == q])
        idx = self._chanidx[key]
        assert idx < NDMACH, f"too many DMA channels on {q}"
        semn = f"{q}_{idx}"
        wl = list(waits)
        if key in self._chanlast:
            wl.append(self._chanlast[key])
        nm = sig or f"_dma_{q}_{chan}_{self.cnt[semn] // 16}"
        self.do(engine, emit, sig=nm, sem=semn, step=16, waits=tuple(wl))
        self._chanlast[key] = nm
        return nm


def build_program(body):
    nc = bass.Bass("TRN2", target_bir_lowering=False)
    ctx = ExitStack()
    d = {}
    body(nc, None, d, ctx, True)
    sems = {s: ctx.enter_context(nc.semaphore(s)) for s in SEMS}
    with nc.Block() as block:
        for engname in ENGS:
            def section(eng, engname=engname):
                ev = EV(engname, eng, sems)
                body(nc, ev, d, ctx, False)
            getattr(block, engname)(section)
    ctx.close()
    return nc


def sbuf_decl(nc, ctx, d, items):
    for nm, shp, dt, *rest in items:
        kw = rest[0] if rest else {}
        d[nm] = ctx.enter_context(nc.sbuf_tensor(nm, shp, dt, **kw))


def ddecl(nc, d, key, name, shape, dt, kind):
    if key not in d:
        d[key] = nc.dram_tensor(name, shape, dt, kind=kind)


def body_L1(nc, ev, d, ctx, declare):
    nt = TSH // 128
    if declare:
        ddecl(nc, d, "x", "x", [TSH, E], F32, "ExternalInput")
        ddecl(nc, d, "xn", "xn", [TSH, E], BF16, "ExternalOutput")
        sbuf_decl(nc, ctx, d, [(f"xt{i}", [128, E], F32) for i in range(nt)])
        sbuf_decl(nc, ctx, d, [(f"xnb{i}", [128, E], BF16) for i in range(2)])
        sbuf_decl(nc, ctx, d, [("sq", [128, E], F32)])
        sbuf_decl(nc, ctx, d, [(nm, [128, nt], F32) for nm in
                               ["rs", "rs2", "mcol", "vcol", "stdc", "rstd", "nmr"]])
        return
    for i in range(nt):
        ev.dma("sync", "x", lambda e, i=i: e.dma_start(
            out=d[f"xt{i}"][:, :], in_=d["x"].ap()[128 * i:128 * (i + 1), :]),
            sig=f"ld{i}")
    for i in range(nt):
        ev.do("vector", lambda e, i=i: e.reduce_sum(
            d["rs"][:, i:i + 1], d[f"xt{i}"][:, :], axis=AX.X), waits=(f"ld{i}",))
        ev.do("vector", lambda e, i=i: e.tensor_tensor(
            d["sq"][:, :], d[f"xt{i}"][:, :], d[f"xt{i}"][:, :], OP.mult),
            sig=f"sq{i}", sem="vector")
        ev.do("vector", lambda e, i=i: e.reduce_sum(
            d["rs2"][:, i:i + 1], d["sq"][:, :], axis=AX.X),
            waits=(f"sq{i}",), sig=f"red{i}", sem="vector")
    ev.do("vector", lambda e: e.tensor_scalar_mul(d["mcol"][:, :], d["rs"][:, :], 1.0 / E),
          sig="m", sem="vector", waits=(f"red{nt-1}",))
    ev.do("vector", lambda e: e.tensor_scalar_mul(d["rs2"][:, :], d["rs2"][:, :], 1.0 / E),
          sig="e2", sem="vector")
    ev.do("vector", lambda e: e.tensor_tensor(d["vcol"][:, :], d["mcol"][:, :], d["mcol"][:, :], OP.mult),
          sig="m2", sem="vector", waits=("m",))
    ev.do("vector", lambda e: e.tensor_tensor(d["vcol"][:, :], d["rs2"][:, :], d["vcol"][:, :], OP.subtract),
          sig="var", sem="vector", waits=("m2", "e2"))
    ev.do("vector", lambda e: e.tensor_scalar_add(d["vcol"][:, :], d["vcol"][:, :], EPS_LN),
          sig="vare", sem="vector", waits=("var",))
    ev.do("scalar", lambda e: e.activation(d["stdc"][:, :], d["vcol"][:, :], AF.Sqrt),
          sig="std", sem="scalar", waits=("vare",))
    ev.do("vector", lambda e: e.reciprocal(d["rstd"][:, :], d["stdc"][:, :]),
          sig="rstd", sem="vector", waits=("std",))
    ev.do("vector", lambda e: e.tensor_tensor(d["nmr"][:, :], d["mcol"][:, :], d["rstd"][:, :], OP.mult),
          sig="nmr0", sem="vector", waits=("rstd",))
    ev.do("vector", lambda e: e.tensor_scalar_mul(d["nmr"][:, :], d["nmr"][:, :], -1.0),
          sig="nmr", sem="vector", waits=("nmr0",))
    for i in range(nt):
        w = ("nmr",) if i < 2 else ("nmr", f"st{i-2}")
        ev.do("vector", lambda e, i=i: e.tensor_scalar(
            d[f"xnb{i % 2}"][:, :], d[f"xt{i}"][:, :],
            d["rstd"][:, i:i + 1], d["nmr"][:, i:i + 1], OP.mult, OP.add),
            sig=f"app{i}", sem="vector", waits=w)
        ev.dma("gpsimd", "xn", lambda e, i=i: e.dma_start(
            out=d["xn"].ap()[128 * i:128 * (i + 1), :], in_=d[f"xnb{i % 2}"][:, :]),
            sig=f"st{i}", waits=(f"app{i}",))
    ev.do("gpsimd", waits=(f"st{nt-1}",))


def body_L2(nc, ev, d, ctx, declare):
    if declare:
        ddecl(nc, d, "xnag", "xnag", [B * N, E], BF16, "ExternalInput")
        d["uw"] = nc.dram_tensor("uw", [E, DH], F32, kind="ExternalInput")
        d["vw"] = nc.dram_tensor("vw", [E, DH], F32, kind="ExternalInput")
        d["ub"] = nc.dram_tensor("ub", [DH, 1], F32, kind="ExternalInput")
        d["vb"] = nc.dram_tensor("vb", [DH, 1], F32, kind="ExternalInput")
        ddecl(nc, d, "vsig", "vsig", [B, 3, 128, M], BF16, "ExternalOutput")
        ddecl(nc, d, "usig", "usig", [B, 3, 128, M], BF16, "ExternalOutput")
        sbuf_decl(nc, ctx, d, [
            ("uwf", [128, 8 * DH], F32), ("uwb", [128, 8 * DH], BF16),
            ("vwf", [128, 8 * DH], F32), ("vwb", [128, 8 * DH], BF16),
            ("ubf", [128, 3], F32), ("vbf", [128, 3], F32)])
        sbuf_decl(nc, ctx, d, [(f"xk{q}_{i}", [128, 512], BF16, dict(is_transpose_dest=True))
                               for q in range(8) for i in range(2)])
        sbuf_decl(nc, ctx, d, [(f"sig{s}{oc}", [128, M], BF16)
                               for s in ("v", "u") for oc in range(3)])
        d["psA"] = ctx.enter_context(nc.psum_tensor("psA", [128, 512], F32))
        d["psB"] = ctx.enter_context(nc.psum_tensor("psB", [128, 512], F32))
        return

    d["_unit"] = [0]
    for nm, src in [("uwf", d["uw"]), ("vwf", d["vw"])]:
        ev.dma("sync", "wl", lambda e, nm=nm, src=src: e.dma_start(
            out=d[nm][:, :].rearrange("p (k c) -> p k c", k=8),
            in_=src.ap().rearrange("(k p) c -> p k c", p=128)),
            sig=f"ld_{nm}")
    ev.dma("sync", "wl", lambda e: e.dma_start(out=d["ubf"][:, :].rearrange("p (o x) -> p o x", x=1),
           in_=d["ub"].ap().rearrange("(o p) x -> p o x", p=128),
           allow_slow_non_contiguous=True), sig="ld_ub")
    ev.dma("sync", "wl", lambda e: e.dma_start(out=d["vbf"][:, :].rearrange("p (o x) -> p o x", x=1),
           in_=d["vb"].ap().rearrange("(o p) x -> p o x", p=128),
           allow_slow_non_contiguous=True), sig="ld_vb")
    ev.do("vector", lambda e: e.tensor_copy(d["uwb"][:, :], d["uwf"][:, :]),
          sig="cst_uw", sem="vector", waits=("ld_uwf",))
    ev.do("vector", lambda e: e.tensor_copy(d["vwb"][:, :], d["vwf"][:, :]),
          sig="cst_vw", sem="vector", waits=("ld_vwf",))
    last = None
    for s in ("v", "u"):
        for oc in range(3):
            last = ev.do("vector", lambda e, s=s, oc=oc: e.memset(
                d[f"sig{s}{oc}"][:, :].rearrange("p (j two i) -> p j two i", two=2, i=64)[:, :, 1, :], 0.0),
                sig=f"pad{s}{oc}", sem="vector")

    for b in range(B):
        for tt in range(8):
            buf = tt % 2
            for q in range(8):
                wl = []
                if b > 0 or tt > 1:
                    pb, ptt = (b, tt - 2) if tt > 1 else (b - 1, 6 + tt)
                    wl = [f"mmdone{pb}_{ptt}"]
                ev.dma("sync", f"xk{q}",
                       lambda e, b=b, tt=tt, q=q, buf=buf: e.dma_start(
                    out=d[f"xk{q}_{buf}"][:, :],
                    in_=d["xnag"].ap()[N * b + 512 * tt:N * b + 512 * (tt + 1),
                                       128 * q:128 * (q + 1)], transpose=True),
                    sig=f"xl{b}_{tt}_{q}", waits=tuple(wl))
            for si, (s, W, bias) in enumerate((("v", "vwb", "vbf"), ("u", "uwb", "ubf"))):
                for oc in range(3):
                    un = d["_unit"]
                    n = un[0]; un[0] += 1
                    ps = d["psA"] if n % 2 == 0 else d["psB"]
                    def mm(e, W=W, oc=oc, buf=buf, ps=ps):
                        lastl = None
                        for q in range(8):
                            lhs = d[W][:, q * DH + oc * 128:q * DH + oc * 128 + 128]
                            lastl = e.matmul(ps[:, :], lhs, d[f"xk{q}_{buf}"][:, :],
                                             start=(q == 0), stop=(q == 7))
                        return lastl
                    wl = [f"xl{b}_{tt}_3", f"xl{b}_{tt}_7", "cst_uw", "cst_vw"]
                    if n >= 2:
                        wl.append(f"uev{n - 2}")
                    ev.do("tensor", mm, sig=f"umm{n}", sem="tensor", waits=tuple(wl))
                    def evac(e, s=s, oc=oc, tt=tt, ps=ps, bias=bias):
                        dst = d[f"sig{s}{oc}"][:, :].rearrange(
                            "p (j i) -> p j i", i=128)[:, :, tt * 8:tt * 8 + 8]
                        src = ps[:, :].rearrange("p (i j) -> p i j", i=8)
                        return e.activation(dst.rearrange("p j i -> p i j"), src,
                                            AF.Silu, bias=d[bias][:, oc:oc + 1], scale=1.0)
                    ev.do("scalar", evac, sig=f"uev{n}", sem="scalar",
                          waits=(f"umm{n}", f"pad{s}{oc}",
                                 None if b == 0 else f"sw{b-1}_{s}{oc}"))
                    ev.alias(f"ev{b}_{tt}_{si * 3 + oc}", f"uev{n}")
            ev.alias(f"mmdone{b}_{tt}", f"umm{un[0] - 1}")
        for s in ("v", "u"):
            for oc in range(3):
                tgt = d["vsig"] if s == "v" else d["usig"]
                ev.dma("gpsimd", f"sw{s}{oc}", lambda e, tgt=tgt, b=b, oc=oc, s=s: e.dma_start(
                    out=tgt.ap()[b, oc, :, :], in_=d[f"sig{s}{oc}"][:, :]),
                    sig=f"sw{b}_{s}{oc}",
                    waits=(f"ev{b}_7_{(0 if s == 'v' else 1) * 3 + oc}",))
    ev.do("gpsimd", waits=tuple(f"sw{B-1}_{s}{oc}" for s in ("v", "u") for oc in range(3)))


def body_L2b(nc, ev, d, ctx, declare):
    if declare:
        d["posw"] = nc.dram_tensor("posw", [1, RW], F32, kind="ExternalInput")
        d["posb"] = nc.dram_tensor("posb", [RW, 1], F32, kind="ExternalInput")
        for l in range(3):
            d[f"lw{l}"] = nc.dram_tensor(f"lw{l}", [RW, RW], F32, kind="ExternalInput")
            d[f"lb{l}"] = nc.dram_tensor(f"lb{l}", [RW, 1], F32, kind="ExternalInput")
        d["outw"] = nc.dram_tensor("outw", [RW, DH], F32, kind="ExternalInput")
        d["outb"] = nc.dram_tensor("outb", [DH, 1], F32, kind="ExternalInput")
        d["cramp"] = nc.dram_tensor("c_ramp", [1, N], F32, kind="ExternalInput")
        d["cdecay"] = nc.dram_tensor("c_decay", [1, N], F32, kind="ExternalInput")
        ddecl(nc, d, "cones", "c_ones", [128, 128], F32, "ExternalInput")
        ddecl(nc, d, "asig", "asig", [3, 128, M], BF16, "ExternalOutput")
        sbuf_decl(nc, ctx, d, [
            ("poswf", [1, RW], F32), ("posbf", [128, 4], F32),
            ("rampf", [1, N], F32), ("decayf", [1, N], F32),
            ("decayt", [128, N], BF16), ("onesf", [128, 128], F32),
            ("onesb", [128, 1], BF16),
            ("hcur", [128, 4 * N], BF16), ("hnxt", [128, 4 * N], BF16),
            ("hsq", [128, 4 * N], BF16),
            ("s2sl", [1, 512], F32), ("rmssl", [1, 512], F32), ("rinvsl", [1, 512], F32),
            ("rbt", [128, N], BF16), ("ctmp", [128, 512], BF16),
            ("ast", [128, M], BF16)])
        for l in range(3):
            sbuf_decl(nc, ctx, d, [(f"lwb{l}", [128, 4 * RW], BF16),
                                   (f"lbf{l}", [128, 4], F32)])
        sbuf_decl(nc, ctx, d, [("wstage", [128, 4 * RW], F32),
                               ("outwb", [128, 4 * DH], BF16),
                               ("outbf", [128, 3], F32)])
        d["psA"] = ctx.enter_context(nc.psum_tensor("psA", [128, 512], F32))
        d["psB"] = ctx.enter_context(nc.psum_tensor("psB", [128, 512], F32))
        d["psD"] = ctx.enter_context(nc.psum_tensor("psD", [128, 512], F32))
        d["psE"] = ctx.enter_context(nc.psum_tensor("psE", [1, 512], F32))
        return

    ev.dma("sync", "wl", lambda e: e.dma_start(out=d["poswf"][:, :], in_=d["posw"].ap()),
           sig="ld_pw")
    ev.dma("sync", "wl", lambda e: e.dma_start(
        out=d["posbf"][:, :].rearrange("p (o x) -> p o x", x=1),
        in_=d["posb"].ap().rearrange("(o p) x -> p o x", p=128),
        allow_slow_non_contiguous=True), sig="ld_pb")
    ev.dma("sync", "wl", lambda e: e.dma_start(out=d["rampf"][:, :], in_=d["cramp"].ap()),
           sig="ld_rmp")
    ev.dma("sync", "wl", lambda e: e.dma_start(out=d["decayf"][:, :], in_=d["cdecay"].ap()),
           sig="ld_dc")
    ev.dma("sync", "wl", lambda e: e.dma_start(out=d["onesf"][:, :], in_=d["cones"].ap()),
           sig="ld_on")
    for l in range(3):
        ev.dma("sync", "wl", lambda e, l=l: e.dma_start(
            out=d["wstage"][:, :].rearrange("p (k c) -> p k c", k=4),
            in_=d[f"lw{l}"].ap().rearrange("(k p) c -> p k c", p=128)),
            sig=f"ld_lw{l}",
            waits=(f"cst_lw{l-1}",) if l > 0 else ())
        ev.dma("sync", "wl", lambda e, l=l: e.dma_start(
            out=d[f"lbf{l}"][:, :].rearrange("p (o x) -> p o x", x=1),
            in_=d[f"lb{l}"].ap().rearrange("(o p) x -> p o x", p=128),
            allow_slow_non_contiguous=True), sig=f"ld_lb{l}")
        ev.do("vector", lambda e, l=l: e.tensor_copy(d[f"lwb{l}"][:, :], d["wstage"][:, :]),
              sig=f"cst_lw{l}", sem="vector", waits=(f"ld_lw{l}",))
    ev.dma("sync", "wl", lambda e: e.dma_start(
        out=d["wstage"][:, 0:4 * DH].rearrange("p (k c) -> p k c", k=4),
        in_=d["outw"].ap().rearrange("(k p) c -> p k c", p=128)),
        sig="ld_ow", waits=("cst_lw2",))
    ev.dma("sync", "wl", lambda e: e.dma_start(
        out=d["outbf"][:, :].rearrange("p (o x) -> p o x", x=1),
        in_=d["outb"].ap().rearrange("(o p) x -> p o x", p=128),
        allow_slow_non_contiguous=True), sig="ld_ob")
    ev.do("vector", lambda e: e.tensor_copy(d["outwb"][:, :], d["wstage"][:, 0:4 * DH]),
          sig="cst_ow", sem="vector", waits=("ld_ow",))
    ev.do("vector", lambda e: e.memset(
        d["ast"][:, :].rearrange("p (j two i) -> p j two i", two=2, i=64)[:, :, 1, :], 0.0),
        sig="pad_ast", sem="vector")
    ev.do("vector", lambda e: e.tensor_copy(d["onesb"][:, :], d["onesf"][:, 0:1]),
          sig="cst_on", sem="vector", waits=("ld_on",))

    for t in range(8):
        ev.do("tensor", lambda e, t=t: e.matmul(
            d["psD"][:, :], d["onesf"][0:1, :], d["decayf"][0:1, 512 * t:512 * (t + 1)],
            start=True, stop=True), sig=f"dcmm{t}", sem="tensor",
            waits=("ld_on", "ld_dc") if t == 0 else (f"dcev{t-1}",))
        ev.do("vector", lambda e, t=t: e.tensor_copy(
            d["decayt"][:, 512 * t:512 * (t + 1)], d["psD"][:, :]),
            sig=f"dcev{t}", sem="vector", waits=(f"dcmm{t}",))

    un = [0]
    for c4 in range(4):
        for tt in range(8):
            n = un[0]; un[0] += 1
            ps = d["psA"] if n % 2 == 0 else d["psB"]
            wl = ["ld_pw", "ld_rmp"]
            if n >= 2:
                wl.append(f"h0ev{n-2}")
            ev.do("tensor", lambda e, c4=c4, tt=tt, ps=ps: e.matmul(
                ps[:, :], d["poswf"][0:1, 128 * c4:128 * (c4 + 1)],
                d["rampf"][0:1, 512 * tt:512 * (tt + 1)], start=True, stop=True),
                sig=f"h0mm{n}", sem="tensor", waits=tuple(wl))
            ev.do("vector", lambda e, c4=c4, tt=tt, ps=ps: e.tensor_scalar(
                d["hcur"][:, :].rearrange("p (k t) -> p k t", k=4)[:, c4, 512 * tt:512 * (tt + 1)],
                ps[:, :], d["posbf"][:, c4:c4 + 1], None, OP.add),
                sig=f"h0ev{n}", sem="vector", waits=(f"h0mm{n}", "ld_pb"))
    ev.alias("hlay-1", f"h0ev{un[0]-1}")

    for l in range(4):
        hin = d["hcur"] if l % 2 == 0 else d["hnxt"]
        hout = d["hnxt"] if l % 2 == 0 else d["hcur"]
        ev.do("vector", lambda e, hin=hin: e.tensor_tensor(
            d["hsq"][:, :], hin[:, :], hin[:, :], OP.mult),
            sig=f"sq{l}", sem="vector", waits=(f"hlay{l-1}",))
        for tt in range(8):
            def ssum(e, tt=tt):
                lastl = None
                for k in range(4):
                    lastl = e.matmul(
                        d["psE"][:, :], d["onesb"][:, :],
                        d["hsq"][:, :].rearrange("p (k t) -> p k t", k=4)[:, k, 512 * tt:512 * (tt + 1)],
                        start=(k == 0), stop=(k == 3))
                return lastl
            ev.do("tensor", ssum, sig=f"ss{l}_{tt}", sem="tensor",
                  waits=(f"sq{l}", "cst_on") if tt == 0 else (f"ssev{l}_{tt-1}",))
            ev.do("vector", lambda e: e.tensor_copy(d["s2sl"][:, :], d["psE"][:, :]),
                  sig=f"ssev{l}_{tt}", sem="vector",
                  waits=(f"ss{l}_{tt}",) + ((f"rms{l}_{tt-1}",) if tt > 0 else ()))
            ev.do("scalar", lambda e: e.activation(d["rmssl"][:, :], d["s2sl"][:, :],
                                                   AF.Sqrt, scale=1.0 / RW),
                  sig=f"rms{l}_{tt}", sem="scalar",
                  waits=(f"ssev{l}_{tt}",) + ((f"rinv{l}_{tt-1}",) if tt > 0 else ()))
            ev.do("vector", lambda e: e.tensor_scalar_add(d["rmssl"][:, :], d["rmssl"][:, :], EPS_SRMS),
                  sig=f"rmse{l}_{tt}", sem="vector", waits=(f"rms{l}_{tt}",))
            ev.do("vector", lambda e: e.reciprocal(d["rinvsl"][:, :], d["rmssl"][:, :]),
                  sig=f"rinv{l}_{tt}", sem="vector",
                  waits=(f"rmse{l}_{tt}",) + ((f"rb{l}_{tt-1}",) if tt > 0 else ()))
            ev.do("tensor", lambda e, tt=tt: e.matmul(
                d["psD"][:, :], d["onesf"][0:1, :], d["rinvsl"][0:1, :],
                start=True, stop=True), sig=f"rb{l}_{tt}", sem="tensor",
                waits=(f"rinv{l}_{tt}",) + ((f"rbev{l}_{tt-1}",) if tt > 0 else ()))
            ev.do("vector", lambda e, tt=tt: e.tensor_copy(
                d["rbt"][:, 512 * tt:512 * (tt + 1)], d["psD"][:, :]),
                sig=f"rbev{l}_{tt}", sem="vector", waits=(f"rb{l}_{tt}",))
        ev.do("vector", lambda e, hin=hin: e.tensor_tensor(
            d["hsq"][:, :].rearrange("p (k t) -> p k t", k=4), hin[:, :].rearrange("p (k t) -> p k t", k=4),
            d["rbt"][:, :].rearrange("p (o t) -> p o t", o=1).broadcast_to([128, 4, N]),
            OP.mult), sig=f"hm{l}", sem="vector", waits=(f"rbev{l}_7", f"ssev{l}_7"))
        ev.do("vector", lambda e: e.tensor_scalar_max(d["hsq"][:, :], d["hsq"][:, :], 0.0),
              sig=f"hr{l}", sem="vector", waits=(f"hm{l}",))
        nco = 4 if l < 3 else 3
        ncols = RW if l < 3 else DH
        Wb = d[f"lwb{l}"] if l < 3 else d["outwb"]
        un2 = [0]
        for oc in range(nco):
            for tt in range(8):
                n = un2[0]; un2[0] += 1
                ps = d["psA"] if n % 2 == 0 else d["psB"]
                def mmh(e, Wb=Wb, oc=oc, tt=tt, ps=ps, ncols=ncols):
                    lastl = None
                    for k in range(4):
                        lhs = Wb[:, k * ncols + oc * 128:k * ncols + oc * 128 + 128]
                        mov = d["hsq"][:, :].rearrange("p (k t) -> p k t", k=4)[:, k, 512 * tt:512 * (tt + 1)]
                        lastl = e.matmul(ps[:, :], lhs, mov, start=(k == 0), stop=(k == 3))
                    return lastl
                wl = [f"hr{l}", "cst_ow" if l == 3 else f"cst_lw{l}"]
                if n >= 2:
                    wl.append(f"lev{l}_{n-2}")
                ev.do("tensor", mmh, sig=f"lmm{l}_{n}", sem="tensor", waits=tuple(wl))
                if l < 3:
                    ev.do("vector", lambda e, hout=hout, oc=oc, tt=tt, ps=ps, l=l: e.tensor_scalar(
                        hout[:, :].rearrange("p (k t) -> p k t", k=4)[:, oc, 512 * tt:512 * (tt + 1)],
                        ps[:, :], d[f"lbf{l}"][:, oc:oc + 1], None, OP.add),
                        sig=f"lev{l}_{n}", sem="vector", waits=(f"lmm{l}_{n}", f"ld_lb{l}"))
                else:
                    ev.do("vector", lambda e, oc=oc, tt=tt, ps=ps: e.tensor_scalar(
                        d["ctmp"][:, :], ps[:, :], d["outbf"][:, oc:oc + 1], None, OP.add),
                        sig=f"cb{n}", sem="vector", waits=(f"lmm{l}_{n}", "ld_ob",
                                                           f"dmul{n-1}" if n > 0 else None))
                    def dmul(e, oc=oc, tt=tt):
                        dst = d["ast"][:, :].rearrange("p (j i) -> p j i", i=128)[:, :, tt * 8:tt * 8 + 8]
                        return e.tensor_tensor(
                            dst.rearrange("p j i -> p i j"),
                            d["ctmp"][:, :].rearrange("p (i j) -> p i j", i=8),
                            d["decayt"][:, 512 * tt:512 * (tt + 1)].rearrange("p (i j) -> p i j", i=8),
                            OP.mult)
                    ev.do("vector", dmul, sig=f"dmul{n}", sem="vector",
                          waits=(f"cb{n}", "dcev7", "pad_ast",
                                 f"awr{oc-1}" if (tt == 0 and oc > 0) else None))
                    ev.alias(f"lev{l}_{n}", f"dmul{n}")
            if l == 3:
                ev.dma("gpsimd", "aw", lambda e, oc=oc: e.dma_start(
                    out=d["asig"].ap()[oc, :, :], in_=d["ast"][:, :]),
                    sig=f"awr{oc}", waits=(f"dmul{oc * 8 + 7}",))
        if l < 3:
            ev.alias(f"hlay{l}", f"lev{l}_{un2[0]-1}")
    ev.do("gpsimd", waits=("awr0", "awr1", "awr2"))


def body_L3(nc, ev, d, ctx, declare):
    CC = 128
    if declare:
        ddecl(nc, d, "vsig", "vsig", [B, 3, 128, M], BF16, "ExternalInput")
        ddecl(nc, d, "usig", "usig", [B, 3, 128, M], BF16, "ExternalInput")
        ddecl(nc, d, "asig", "asig", [3, 128, M], BF16, "ExternalInput")
        for knm, shp in [("c_d1mov", [64, 130]), ("c_wf", [64, NF1 * 3 * 64]),
                         ("c_e2", [64, 192]), ("c_tinv", [NF1, 128]),
                         ("c_cstk", [128, 64])]:
            d[knm] = nc.dram_tensor(knm, shp, F32, kind="ExternalInput")
        ddecl(nc, d, "w", "w", [B, 3, N, CC], BF16, "ExternalOutput")
        sbuf_decl(nc, ctx, d, [
            ("Xv", [128, CC * 64], BF16, dict(is_transpose_dest=True)),
            ("Xu", [128, CC * 64], BF16, dict(is_transpose_dest=True)),
            ("sA", [NF1, CC * 130], BF16),
            ("sB", [128, CC * 130], BF16),
            ("Ah", [64, NF1 * 2 * CC], BF16),
            ("Pd", [64, NF1 * 2 * CC], BF16),
            ("t1", [NF1, NF1 * CC], BF16),
            ("kd1", [64, 130], BF16), ("kwf", [64, NF1 * 3 * 64], BF16),
            ("ke2", [64, 192], BF16), ("ktinv", [NF1, 128], BF16),
            ("kcstk", [128, 64], BF16)])
        d["bk"] = [ctx.enter_context(nc.psum_tensor(f"bk{i}", [128, 512], F32))
                   for i in range(8)]
        return

    for knm, dst in [("c_d1mov", "kd1"), ("c_wf", "kwf"), ("c_e2", "ke2"),
                     ("c_tinv", "ktinv"), ("c_cstk", "kcstk")]:
        ev.dma("gpsimd", "kc", lambda e, knm=knm, dst=dst: e.dma_start(
            out=d[dst][:, :], in_=d[knm].ap()), sig=f"ld_{dst}")
    KW = ("ld_kd1", "ld_kwf", "ld_ke2", "ld_ktinv", "ld_kcstk")

    bk = d["bk"]
    d["_Vh"] = None
    S1 = d["sA"][0:64, :]
    Qe = d["sA"][0:65, 0:CC * 2 * 64]
    Vh = d["sB"][0:64, 0:NF1 * 2 * CC]
    Rst = d["sB"][0:128, 0:CC * 64]
    Wt = d["sB"][0:64, CC * 64:CC * 64 + 64 * CC]

    def emit_f1(pfx, X, waits_extra):
        for wv in range(16):
            def f1(e, wv=wv, X=X):
                lastl = None
                for s in range(4):
                    c = wv * 8 + 2 * s
                    lastl = e.matmul(bk[s][0:128, 0:130], X[0:64, c * 64:(c + 2) * 64],
                                     d["kd1"][:, :], start=True, stop=True)
                return lastl
            wl = list(waits_extra) + ["ld_kd1"]
            if wv > 0:
                wl.append(f"{pfx}s1e{wv-1}")
            ev.do("tensor", f1, sig=f"{pfx}f1m{wv}", sem="tensor", waits=tuple(wl))
            def s1e(e, wv=wv):
                lastl = None
                for s in range(4):
                    c = wv * 8 + 2 * s
                    e.copy(S1[:, c * 130:(c + 1) * 130], bk[s][0:64, 0:130])
                    lastl = e.copy(S1[:, (c + 1) * 130:(c + 2) * 130],
                                   bk[s][64:128, 0:130])
                return lastl
            ev.do("scalar", s1e, sig=f"{pfx}s1e{wv}", sem="scalar",
                  waits=(f"{pfx}f1m{wv}",))

    def emit_f2(pfx, Dst, first_evac_waits=()):
        S1v = S1[:, :].rearrange("p (c x) -> p c x", x=130)
        for g in range(17):
            lo, hi = g * 4, min((g + 1) * 4, NF1)
            def f2(e, lo=lo, hi=hi):
                lastl = None
                for f1 in range(lo, hi):
                    o = (f1 - lo) * CC
                    wr = d["kwf"][:, (f1 * 3 + 0) * 64:(f1 * 3 + 1) * 64]
                    wi = d["kwf"][:, (f1 * 3 + 1) * 64:(f1 * 3 + 2) * 64]
                    wni = d["kwf"][:, (f1 * 3 + 2) * 64:(f1 * 3 + 3) * 64]
                    s1r = S1v[:, :, 0 * 65 + f1]
                    s1i = S1v[:, :, 1 * 65 + f1]
                    e.matmul(bk[0][0:64, o:o + CC], wr, s1r, start=True, stop=False)
                    e.matmul(bk[1][0:64, o:o + CC], wr, s1i, start=True, stop=False)
                    e.matmul(bk[1][0:64, o:o + CC], wi, s1r, start=False, stop=True)
                    lastl = e.matmul(bk[0][0:64, o:o + CC], wni, s1i, start=False, stop=True)
                return lastl
            wl = [f"{pfx}s1e15", "ld_kwf"]
            if g > 0:
                wl.append(f"{pfx}f2e{g-1}")
            ev.do("tensor", f2, sig=f"{pfx}f2m{g}", sem="tensor", waits=tuple(wl))
            def f2e(e, lo=lo, hi=hi, Dst=Dst):
                n = hi - lo
                dv = Dst[:, :].rearrange("p (f pl c) -> p f pl c", f=NF1, pl=2)
                e.tensor_copy(dv[:, lo:hi, 0, :],
                              bk[0][0:64, 0:n * CC].rearrange("p (f c) -> p f c", c=CC))
                return e.tensor_copy(dv[:, lo:hi, 1, :],
                                     bk[1][0:64, 0:n * CC].rearrange("p (f c) -> p f c", c=CC))
            def f2e_s(e, lo=lo, hi=hi, Dst=Dst):
                n = hi - lo
                dv = Dst[:, :].rearrange("p (f pl c) -> p f pl c", f=NF1, pl=2)
                e.copy(dv[:, lo:hi, 0, :],
                       bk[0][0:64, 0:n * CC].rearrange("p (f c) -> p f c", c=CC))
                return e.copy(dv[:, lo:hi, 1, :],
                              bk[1][0:64, 0:n * CC].rearrange("p (f c) -> p f c", c=CC))
            few = (f"{pfx}f2m{g}",) + (tuple(first_evac_waits) if g == 0 else ())
            if g % 2 == 0:
                ev.do("vector", f2e, sig=f"{pfx}f2e{g}", sem="vector", waits=few)
            else:
                ev.do("scalar", f2e_s, sig=f"{pfx}f2e{g}", sem="scalar", waits=few)

    for oc in range(3):
        prevlast = None if oc == 0 else f"{oc-1}_{B-1}"
        ev.dma("sync", "xv", lambda e, oc=oc: e.dma_start(
            out=d["Xv"][:, :], in_=d["asig"].ap()[oc, :, :].rearrange("c (j i) -> (c j) i", i=128),
            transpose=True), sig=f"axl{oc}",
            waits=() if oc == 0 else (f"vf1_done{prevlast}",))
        emit_f1(f"a{oc}", d["Xv"],
                [f"axl{oc}"] + ([] if oc == 0 else [f"wcomb{prevlast}_15"]))
        ev.alias(f"af1_done{oc}", f"a{oc}f1m15")
        emit_f2(f"a{oc}", d["Ah"],
                first_evac_waits=() if oc == 0 else (f"prod{prevlast}", f"prodG{prevlast}"))
        for b in range(B):
            pb = f"{oc}_{b}"
            prev = None if (oc == 0 and b == 0) else (f"{oc}_{b-1}" if b > 0 else f"{oc-1}_{B-1}")
            xw = [f"af1_done{oc}"] if b == 0 else [f"vf1_done{prev}"]
            uw_ = [] if prev is None else [f"wcomb{prev}_15"]
            ev.dma("sync", "xv", lambda e, oc=oc, b=b: e.dma_start(
                out=d["Xv"][:, :], in_=d["vsig"].ap()[b, oc, :, :].rearrange("c (j i) -> (c j) i", i=128),
                transpose=True), sig=f"vxl{pb}", waits=tuple(xw))
            ev.dma("sync", "xu", lambda e, oc=oc, b=b: e.dma_start(
                out=d["Xu"][:, :], in_=d["usig"].ap()[b, oc, :, :].rearrange("c (j i) -> (c j) i", i=128),
                transpose=True), sig=f"uxl{pb}", waits=tuple(uw_))
            vw = [f"vxl{pb}", f"a{oc}f2e16"]
            if prev is not None:
                vw.append(f"itw{prev}")
                vw.append(f"itwG{prev}")
                vw.append(f"wcomb{prev}_15")
            emit_f1(f"v{pb}", d["Xv"], vw)
            ev.alias(f"vf1_done{pb}", f"v{pb}f1m15")
            fe = () if prev is None else (f"i2m{prev}_15", f"wwr{prev}")
            emit_f2(f"v{pb}", Vh, first_evac_waits=fe)
            def PV(w):
                return d[w][:, :].rearrange("p (f pl c) -> p f pl c", f=NF1, pl=2)
            def pstage(which, c0, c1):
                def fa(e):
                    V, A, P = PV("_Vh"), PV("Ah"), PV("Pd")
                    T1 = d["t1"][0:64, :].rearrange("p (f c) -> p f c", f=NF1)
                    if which == 0:
                        e.tensor_tensor(P[:, :, 0, c0:c1], V[:, :, 0, c0:c1],
                                        A[:, :, 0, c0:c1], OP.mult)
                        return e.tensor_tensor(T1[:, :, c0:c1], V[:, :, 1, c0:c1],
                                               A[:, :, 1, c0:c1], OP.mult)
                    if which == 1:
                        e.tensor_tensor(P[:, :, 0, c0:c1], P[:, :, 0, c0:c1],
                                        T1[:, :, c0:c1], OP.subtract)
                        return e.tensor_tensor(P[:, :, 1, c0:c1], V[:, :, 0, c0:c1],
                                               A[:, :, 1, c0:c1], OP.mult)
                    if which == 2:
                        return e.tensor_tensor(T1[:, :, c0:c1], V[:, :, 1, c0:c1],
                                               A[:, :, 0, c0:c1], OP.mult)
                    return e.tensor_tensor(P[:, :, 1, c0:c1], P[:, :, 1, c0:c1],
                                           T1[:, :, c0:c1], OP.add)
                return fa
            pw = [f"v{pb}f2e16", f"a{oc}f2e16"]
            if prev is not None:
                pw.append(f"i1m_last{prev}")
                pw.append(f"itw{prev}")
                pw.append(f"itwG{prev}")
            d["_Vh"] = Vh
            ev.do("vector", pstage(0, 0, CC), sig=f"prodA{pb}", sem="vector",
                  waits=tuple(pw))
            ev.do("vector", pstage(1, 0, CC), sig=f"prodB{pb}", sem="vector",
                  waits=(f"prodA{pb}",))
            ev.do("vector", pstage(2, 0, CC), sig=f"prodC{pb}", sem="vector",
                  waits=(f"prodB{pb}",))
            ev.do("vector", pstage(3, 0, CC), sig=f"prod{pb}", sem="vector",
                  waits=(f"prodC{pb}",))
            ev.alias(f"prodG{pb}", f"prod{pb}")
            Pv = d["Pd"][:, :].rearrange("p (f pl c) -> p f pl c", f=NF1, pl=2)
            for g in range(32):
                pair = ((bk[2][0:65, 0:256], bk[3][0:65, 0:256]),
                        (bk[4][0:65, 0:256], bk[5][0:65, 0:256]))[g % 2]
                def i1(e, g=g, pair=pair):
                    psr, psi = pair
                    e2r = d["ke2"][:, 0:64]
                    e2i = d["ke2"][:, 64:128]
                    e2ni = d["ke2"][:, 128:192]
                    lastl = None
                    for s in range(4):
                        c = g * 4 + s
                        pr = Pv[:, :, 0, c]
                        pi = Pv[:, :, 1, c]
                        qr = psr[:, s * 64:(s + 1) * 64]
                        qi = psi[:, s * 64:(s + 1) * 64]
                        e.matmul(qr, pr, e2r, start=True, stop=False)
                        e.matmul(qi, pr, e2i, start=True, stop=False)
                        e.matmul(qr, pi, e2ni, start=False, stop=True)
                        lastl = e.matmul(qi, pi, e2r, start=False, stop=True)
                    return lastl
                wl = [f"prod{pb}", f"prodG{pb}", "ld_ke2"]
                if g > 1:
                    wl.append(f"qev{pb}_{g-2}")
                if g == 0 and prev is not None:
                    wl.append(f"itw{prev}")
                    wl.append(f"itwG{prev}")
                ev.do("tensor", i1, sig=f"i1m{pb}_{g}", sem="tensor", waits=tuple(wl))
                def qev(e, g=g, pair=pair):
                    psr, psi = pair
                    dst = Qe[:, :].rearrange("p (c pl j) -> p c pl j", pl=2, j=64)
                    e.copy(dst[:, g * 4:(g + 1) * 4, 0, :],
                           psr[:, :].rearrange("p (c j) -> p c j", j=64))
                    return e.copy(dst[:, g * 4:(g + 1) * 4, 1, :],
                                  psi[:, :].rearrange("p (c j) -> p c j", j=64))
                ev.do("scalar", qev, sig=f"qev{pb}_{g}", sem="scalar",
                      waits=(f"i1m{pb}_{g}",))
            ev.alias(f"i1m_last{pb}", f"i1m{pb}_31")
            def qv():
                return Qe[:, :].rearrange("p (c pl j) -> p c pl j", pl=2, j=64)
            def tib(rows, plane, c0, c1):
                lo, hi = rows
                return d["ktinv"][lo:hi, 64 * plane:64 * plane + 64].rearrange(
                    "p (o j) -> p o j", o=1).broadcast_to([hi - lo, c1 - c0, 64])
            T1A = d["t1"][0:64, 0:CC * 64].rearrange("p (c j) -> p c j", j=64)
            T1N = d["t1"][64:65, 0:CC * 64].rearrange("p (c j) -> p c j", j=64)
            RrV = Rst[0:64, :].rearrange("p (c j) -> p c j", j=64)
            RiV = Rst[64:128, :].rearrange("p (c j) -> p c j", j=64)
            RnV = Rst[64:65, :].rearrange("p (c j) -> p c j", j=64)
            TB = d["Pd"][0:64, 0:CC * 64].rearrange("p (c j) -> p c j", j=64)
            def istage(which, c0, c1):
                def fa(e):
                    Q = qv()
                    if which == 0:
                        e.tensor_tensor(RrV[:, c0:c1], Q[0:64, c0:c1, 0, :],
                                        tib((0, 64), 0, c0, c1), OP.mult)
                        e.tensor_tensor(T1A[:, c0:c1], Q[0:64, c0:c1, 1, :],
                                        tib((0, 64), 1, c0, c1), OP.mult)
                        return e.tensor_tensor(TB[:, c0:c1], Q[0:64, c0:c1, 0, :],
                                               tib((0, 64), 1, c0, c1), OP.mult)
                    if which == 1:
                        return e.tensor_tensor(RrV[:, c0:c1], RrV[:, c0:c1],
                                               T1A[:, c0:c1], OP.subtract)
                    if which == 2:
                        return e.tensor_tensor(T1A[:, c0:c1], Q[0:64, c0:c1, 1, :],
                                               tib((0, 64), 0, c0, c1), OP.mult)
                    return e.tensor_tensor(RiV[:, c0:c1], TB[:, c0:c1],
                                           T1A[:, c0:c1], OP.add)
                return fa
            iw = [f"qev{pb}_31", "ld_ktinv"]
            if prev is not None:
                iw.append(f"i2m{prev}_15")
                iw.append(f"wwr{prev}")
            def nyq_a(e):
                return e.tensor_tensor(T1N, qv()[64:65, :, 1, :],
                                       tib((64, 65), 1, 0, CC), OP.mult)
            def nyq_e(e):
                return e.tensor_tensor(RnV, qv()[64:65, :, 0, :],
                                       tib((64, 65), 0, 0, CC), OP.mult)
            def nyq_f(e):
                return e.tensor_tensor(RnV, RnV, T1N, OP.subtract)
            ev.do("vector", istage(0, 0, CC), sig=f"itwA{pb}", sem="vector",
                  waits=tuple(iw))
            ev.do("vector", istage(1, 0, CC), sig=f"itwB{pb}", sem="vector",
                  waits=(f"itwA{pb}",))
            ev.do("vector", istage(2, 0, CC), sig=f"itwC{pb}", sem="vector",
                  waits=(f"itwB{pb}",))
            ev.do("vector", istage(3, 0, CC), sig=f"itwD{pb}", sem="vector",
                  waits=(f"itwC{pb}",))
            ev.do("vector", nyq_a, sig=f"nyqA{pb}", sem="vector",
                  waits=(f"itwD{pb}",))
            ev.do("vector", nyq_e, sig=f"nyqE{pb}", sem="vector",
                  waits=(f"nyqA{pb}",))
            ev.do("vector", nyq_f, sig=f"itw{pb}", sem="vector",
                  waits=(f"nyqE{pb}",))
            ev.alias(f"itwG{pb}", f"itw{pb}")
            for g in range(16):
                ps = bk[6 + g % 2][0:64, 0:512]
                def i2(e, g=g, ps=ps):
                    return e.matmul(ps, d["kcstk"][:, :],
                                    Rst[:, g * 512:(g + 1) * 512],
                                    start=True, stop=True)
                wl = [f"itw{pb}", f"itwG{pb}", "ld_kcstk"]
                if g > 1:
                    wl.append(f"wcomb{pb}_{g-2}")
                ev.do("tensor", i2, sig=f"i2m{pb}_{g}", sem="tensor", waits=tuple(wl))
                def wcomb(e, g=g, ps=ps):
                    dst = Wt[:, :].rearrange("p (j c) -> p c j", c=CC)[:, g * 8:(g + 1) * 8, :]
                    src = ps[:, :].rearrange("p (c j) -> p c j", j=64)
                    uop = d["Xu"][0:64, :].rearrange("p (c j) -> p c j", j=64)[:, g * 8:(g + 1) * 8, :]
                    return e.tensor_tensor(dst, src, uop, OP.mult)
                ev.do("vector", wcomb, sig=f"wcomb{pb}_{g}", sem="vector",
                      waits=(f"i2m{pb}_{g}", f"uxl{pb}") +
                            ((f"wwr{prev}",) if (g == 0 and prev is not None) else ()))
            ev.dma("gpsimd", "ww", lambda e, oc=oc, b=b: e.dma_start(
                out=d["w"].ap()[b, oc, :, :].rearrange("(i j) c -> i j c", i=64),
                in_=Wt[:, :].rearrange("p (j c) -> p j c", c=CC)),
                sig=f"wwr{pb}", waits=(f"wcomb{pb}_15",))
    ev.do("gpsimd", waits=tuple(f"wwr{oc}_{b}" for oc in range(3) for b in range(B)))


def body_L4a(nc, ev, d, ctx, declare):
    import os
    TRUNC = int(os.environ.get("L4A_TRUNC", "99"))
    NT = TSH // 512
    NCK = TSH // 128
    if declare:
        ddecl(nc, d, "xn", "xn", [TSH, E], BF16, "ExternalInput")
        d["rwT"] = nc.dram_tensor("rwT", [E, NEXP], F32, kind="ExternalInput")
        d["corr"] = nc.dram_tensor("corr", [NEXP, 1], F32, kind="ExternalInput")
        ddecl(nc, d, "gw", "gw", [NEXP, E, HR], F32, "ExternalInput")
        ddecl(nc, d, "uw", "uw", [NEXP, E, HR], F32, "ExternalInput")
        ddecl(nc, d, "dw", "dw", [NEXP, HR, E], F32, "ExternalInput")
        d["sgwT"] = nc.dram_tensor("sgwT", [E, HS], F32, kind="ExternalInput")
        d["suwT"] = nc.dram_tensor("suwT", [E, HS], F32, kind="ExternalInput")
        d["sdwT"] = nc.dram_tensor("sdwT", [HS, E], F32, kind="ExternalInput")
        ddecl(nc, d, "cident", "c_ident", [128, 128], F32, "ExternalInput")
        ddecl(nc, d, "cones", "c_ones", [128, 128], F32, "ExternalInput")
        d["csel"] = nc.dram_tensor("c_sel", [NEXP, NEXP * 128], F32, kind="ExternalInput")
        ddecl(nc, d, "gout", "gout", [E, TSH], BF16, "ExternalOutput")
        sbuf_decl(nc, ctx, d, [(f"xnf{q}", [128, TSH], BF16, dict(is_transpose_dest=True))
                               for q in range(8)])
        sbuf_decl(nc, ctx, d, [
            ("rwb", [128, 8 * NEXP], BF16), ("corrf", [NEXP, 1], F32),
            ("self_", [NEXP, NEXP * 128], F32),
            ("identf", [128, 128], F32), ("onesf", [128, 128], F32),
            ("scf", [NEXP, TSH], F32), ("sbf", [NEXP, TSH], F32),
            ("sctok", [128, NCK * NEXP], F32), ("sbtok", [128, NCK * NEXP], F32),
            ("ps6", [128, NCK * 6 * NGRP], F32),
            ("mA", [128, NCK * NGRP], F32), ("mB", [128, NCK * NGRP], F32),
            ("ggt", [128, NCK * NGRP * NGRP], F32),
            ("gmask", [128, NCK * NGRP], F32),
            ("emask", [128, NCK * NEXP], F32),
            ("msk", [128, NCK * NEXP], F32), ("msk2", [128, NCK * NEXP], F32),
            ("egt", [128, NCK * NEXP * NEXP], F32),
            ("erank", [128, NCK * NEXP], F32),
            ("wts", [128, NCK * NEXP], F32), ("wsum", [128, NCK], F32),
            ("cfm", [NEXP, TSH], F32),
            ("cbt", [128, NEXP * 512], BF16),
            ("gwe", [128, 8 * HR], BF16), ("uwe", [128, 8 * HR], BF16),
            ("dwe", [128, 2 * 512], BF16),
            ("sgw", [128, 8 * 512], BF16), ("suw", [128, 8 * 512], BF16),
            ("sdw", [128, 512], BF16),
            ("sgt", [128, 512], BF16), ("sut", [128, 512], BF16),
            ("hall", [128, NEXP * 2 * 512], BF16),
            ("shh", [128, 16 * 512], BF16),
            ("gst", [128, 512], BF16)])
        d["bk"] = [ctx.enter_context(nc.psum_tensor(f"bk{i}", [128, 512], F32))
                   for i in range(8)]
        return

    bk = d["bk"]
    for q in range(8):
        ev.dma("sync", "xn", lambda e, q=q: e.dma_start(
            out=d[f"xnf{q}"][:, :],
            in_=d["xn"].ap()[:, 128 * q:128 * (q + 1)], transpose=True),
            sig=f"xl{q}")
    XL = tuple(f"xl{q}" for q in range(8))
    ev.dma("gpsimd", "rw", lambda e: e.dma_start(
        out=d["rwb"][:, :].rearrange("p (k x) -> p k x", k=8),
        in_=d["rwT"].ap().rearrange("(k p) x -> p k x", p=128)), sig="ld_rw")
    ev.dma("sync", "kc", lambda e: e.dma_start(out=d["corrf"][:, :], in_=d["corr"].ap()),
           sig="ld_corr")
    ev.dma("sync", "kc", lambda e: e.dma_start(out=d["identf"][:, :], in_=d["cident"].ap()),
           sig="ld_id")
    ev.dma("sync", "kc", lambda e: e.dma_start(out=d["onesf"][:, :], in_=d["cones"].ap()),
           sig="ld_on")
    ev.dma("sync", "kc", lambda e: e.dma_start(out=d["self_"][:, :], in_=d["csel"].ap()),
           sig="ld_sel")

    for t4 in range(NT):
        def rmm(e, t4=t4):
            lastl = None
            for q in range(8):
                lastl = e.matmul(bk[6][0:NEXP, 0:512],
                                 d["rwb"][:, :].rearrange("p (k x) -> p k x", k=8)[:, q, :],
                                 d[f"xnf{q}"][:, 512 * t4:512 * (t4 + 1)],
                                 start=(q == 0), stop=(q == 7))
            return lastl
        ev.do("tensor", rmm, sig=f"rmm{t4}", sem="tensor",
              waits=XL + ("ld_rw",) + ((f"rsev{t4-1}",) if t4 > 0 else ()))
        ev.do("scalar", lambda e, t4=t4: e.activation(
            d["scf"][:, 512 * t4:512 * (t4 + 1)], bk[6][0:NEXP, 0:512], AF.Sigmoid),
            sig=f"rsev{t4}", sem="scalar", waits=(f"rmm{t4}",))
    ev.do("vector", lambda e: e.tensor_scalar(
        d["sbf"][:, :], d["scf"][:, :], d["corrf"][:, :], None, OP.add),
        sig="sbf", sem="vector", waits=("rsev3", "ld_corr"))

    def trunc_exit(lastev):
        for nm in ["gw", "uw", "dw", "sgwT", "suwT", "sdwT", "cident", "csel"]:
            if ("d_" + nm) not in ev.events:
                ev.dma("sync", "dummy", lambda e, nm=nm: e.dma_start(
                    out=d["identf"][0:1, 0:2], in_=d[nm].ap().rearrange(
                        *(("a b c -> (a b) c",) if nm in ("gw", "uw", "dw") else ("a b -> a b",))
                    )[0:1, 0:2]), sig="d_" + nm)
        ev.do("vector", lambda e: e.memset(d["gst"][:, :], 0.0), sig="z_gst", sem="vector",
              waits=(lastev,))
        for oc in range(8):
            for t4 in range(NT):
                ev.dma("gpsimd", "go", lambda e, oc=oc, t4=t4: e.dma_start(
                    out=d["gout"].ap()[128 * oc:128 * (oc + 1), 512 * t4:512 * (t4 + 1)],
                    in_=d["gst"][:, :]), sig=f"zw{oc}_{t4}", waits=("z_gst",))
        ev.do("gpsimd", waits=("d_csel", f"zw7_{NT-1}"))

    if TRUNC <= 1:
        trunc_exit("rsev3")
        return
    for src, dst, sig0 in (("scf", "sctok", "tsc"), ("sbf", "sbtok", "tsb")):
        for ck in range(NCK):
            ev.do("tensor", lambda e, src=src, ck=ck: e.transpose(
                bk[7][0:128, 0:NEXP], d[src][:, 128 * ck:128 * (ck + 1)],
                d["identf"][0:NEXP, 0:NEXP]),
                sig=f"{sig0}m{ck}", sem="tensor",
                waits=(("sbf", "ld_id") if (ck == 0 and src == "scf") else
                       ((f"{sig0}e{ck-1}",) if ck > 0 else ("tsce15",))))
            ev.do("vector", lambda e, dst=dst, ck=ck: e.tensor_copy(
                d[dst][:, NEXP * ck:NEXP * (ck + 1)], bk[7][0:128, 0:NEXP]),
                sig=f"{sig0}e{ck}", sem="vector", waits=(f"{sig0}m{ck}",))

    if TRUNC <= 2:
        trunc_exit("tsbe15")
        return
    def V4(nm):
        return d[nm][:, :].rearrange("p (c g) -> p c g", g=NGRP)
    def V16(nm):
        return d[nm][:, :].rearrange("p (c x) -> p c x", x=NEXP)
    sbg = d["sbtok"][:, :].rearrange("p (c g e) -> p c g e", g=NGRP, e=EPG)
    P6 = d["ps6"][:, :].rearrange("p (c s g) -> p c s g", s=6, g=NGRP)
    pairs = [(0, 1), (0, 2), (0, 3), (1, 2), (1, 3), (2, 3)]
    def step_ps(e):
        lastl = None
        for pi, (i1, i2) in enumerate(pairs):
            lastl = e.tensor_tensor(P6[:, :, pi, :], sbg[:, :, :, i1], sbg[:, :, :, i2], OP.add)
        return lastl
    ev.do("vector", step_ps, sig="ps6", sem="vector", waits=("tsbe15",))
    ev.do("vector", lambda e: (e.tensor_tensor(V4("mA"), P6[:, :, 0, :], P6[:, :, 1, :], OP.max),
                               e.tensor_tensor(V4("mB"), P6[:, :, 2, :], P6[:, :, 3, :], OP.max))[-1],
          sig="mx1", sem="vector", waits=("ps6",))
    ev.do("vector", lambda e: e.tensor_tensor(V4("mA"), V4("mA"), V4("mB"), OP.max),
          sig="mx2", sem="vector", waits=("mx1",))
    ev.do("vector", lambda e: e.tensor_tensor(V4("mB"), P6[:, :, 4, :], P6[:, :, 5, :], OP.max),
          sig="mx3", sem="vector", waits=("mx2",))
    ev.do("vector", lambda e: e.tensor_tensor(V4("mA"), V4("mA"), V4("mB"), OP.max),
          sig="gsc", sem="vector", waits=("mx3",))
    ev.do("vector", lambda e: e.memset(d["ggt"][:, :], 0.0), sig="ggt0", sem="vector")
    def step_ggt(e):
        gsc = V4("mA")
        gf = d["ggt"][:, :].rearrange("p (c x) -> p c x", x=NGRP * NGRP)
        lastl = None
        for dl in range(1, NGRP):
            n = NGRP - dl
            lastl = e.tensor_tensor(gf[:, :, dl:dl + 5 * (n - 1) + 1:5],
                                    gsc[:, :, dl:], gsc[:, :, 0:n], OP.is_gt)
            lastl = e.tensor_tensor(gf[:, :, 4 * dl:4 * dl + 5 * (n - 1) + 1:5],
                                    gsc[:, :, 0:n], gsc[:, :, dl:], OP.is_gt)
        return lastl
    ev.do("vector", step_ggt, sig="ggt", sem="vector", waits=("gsc", "ggt0"))
    ev.do("vector", lambda e: e.tensor_reduce(
        V4("gmask"), d["ggt"][:, :].rearrange("p (c g x) -> p c g x", g=NGRP, x=NGRP),
        AX.X, OP.add), sig="grank", sem="vector", waits=("ggt",))
    ev.do("vector", lambda e: e.tensor_scalar(V4("gmask"), V4("gmask"), 2.0, None, OP.is_lt),
          sig="gmask", sem="vector", waits=("grank",))
    ev.do("vector", lambda e: e.tensor_copy(
        d["emask"][:, :].rearrange("p (c g e) -> p c g e", g=NGRP, e=EPG),
        V4("gmask").rearrange("p c g -> p c g ()").broadcast_to([128, NCK, NGRP, EPG])
        if False else
        V4("gmask").rearrange("p c (g o) -> p c g o", o=1).broadcast_to([128, NCK, NGRP, EPG])),
        sig="emask", sem="vector", waits=("gmask",))
    ev.do("vector", lambda e: e.tensor_tensor(
        d["msk"][:, :], d["sbtok"][:, :], d["emask"][:, :], OP.mult),
        sig="mska", sem="vector", waits=("emask",))
    ev.do("vector", lambda e: e.tensor_scalar(
        d["msk2"][:, :], d["emask"][:, :], 1.0, 1e30, OP.subtract, OP.mult),
        sig="mskb", sem="vector", waits=("emask",))
    ev.do("vector", lambda e: e.tensor_tensor(
        d["msk"][:, :], d["msk"][:, :], d["msk2"][:, :], OP.add),
        sig="msk", sem="vector", waits=("mska", "mskb"))
    ev.do("vector", lambda e: e.memset(d["egt"][:, :], 0.0), sig="egt0", sem="vector")
    def step_egt(e):
        mk = V16("msk")
        ef = d["egt"][:, :].rearrange("p (c x) -> p c x", x=NEXP * NEXP)
        lastl = None
        for dl in range(1, NEXP):
            n = NEXP - dl
            lastl = e.tensor_tensor(ef[:, :, dl:dl + 17 * (n - 1) + 1:17],
                                    mk[:, :, dl:], mk[:, :, 0:n], OP.is_gt)
            lastl = e.tensor_tensor(ef[:, :, 16 * dl:16 * dl + 17 * (n - 1) + 1:17],
                                    mk[:, :, 0:n], mk[:, :, dl:], OP.is_gt)
        return lastl
    ev.do("vector", step_egt, sig="egt", sem="vector", waits=("msk", "egt0"))
    ev.do("vector", lambda e: e.tensor_reduce(
        V16("erank"), d["egt"][:, :].rearrange("p (c a x) -> p c a x", a=NEXP, x=NEXP),
        AX.X, OP.add), sig="erank", sem="vector", waits=("egt",))
    ev.do("vector", lambda e: e.tensor_scalar(V16("erank"), V16("erank"), float(TOPK), None, OP.is_lt),
          sig="kmask", sem="vector", waits=("erank",))
    ev.do("vector", lambda e: e.tensor_tensor(
        d["wts"][:, :], d["sctok"][:, :], d["erank"][:, :], OP.mult),
        sig="wts", sem="vector", waits=("kmask",))
    ev.do("vector", lambda e: e.tensor_reduce(
        d["wsum"][:, :].rearrange("p (c o) -> p c o", o=1),
        V16("wts"), AX.X, OP.add), sig="wsum", sem="vector", waits=("wts",))
    ev.do("vector", lambda e: e.tensor_scalar_add(d["wsum"][:, :], d["wsum"][:, :], 1e-20),
          sig="wsum2", sem="vector", waits=("wsum",))
    ev.do("vector", lambda e: e.reciprocal(d["wsum"][:, :], d["wsum"][:, :]),
          sig="winv", sem="vector", waits=("wsum2",))
    ev.do("vector", lambda e: e.tensor_tensor(
        V16("wts"), V16("wts"),
        d["wsum"][:, :].rearrange("p (c o) -> p c o", o=1).broadcast_to([128, NCK, NEXP]),
        OP.mult), sig="comb", sem="vector", waits=("winv",))
    if TRUNC <= 3:
        trunc_exit("comb")
        return
    for ck in range(NCK):
        ev.do("tensor", lambda e, ck=ck: e.transpose(
            bk[7][0:NEXP, 0:128], d["wts"][:, NEXP * ck:NEXP * (ck + 1)], d["identf"][:, :]),
            sig=f"cbm{ck}", sem="tensor",
            waits=("comb", "ld_id") if ck == 0 else (f"cbe{ck-1}",))
        ev.do("vector", lambda e, ck=ck: e.tensor_copy(
            d["cfm"][:, 128 * ck:128 * (ck + 1)], bk[7][0:NEXP, 0:128]),
            sig=f"cbe{ck}", sem="vector", waits=(f"cbm{ck}",))

    if TRUNC <= 4:
        trunc_exit("cbe15")
        return
    NTeff = 1 if TRUNC == 5 else NT
    for t4 in range(NTeff):
        tsl = slice(512 * t4, 512 * (t4 + 1))
        for e_ in range(NEXP):
            cw = (("cbe15", "ld_sel") if (t4 == 0 and e_ == 0) else
                  ((f"cbte{t4}_{e_-1}",) if e_ > 0 else
                   (f"cbte{t4-1}_{NEXP-1}", f"gse{t4-1}_1_3")))
            ev.do("tensor", lambda e, e_=e_, t4=t4: e.matmul(
                bk[6][0:128, 0:512], d["self_"][:, 128 * e_:128 * (e_ + 1)],
                d["cfm"][:, tsl], start=True, stop=True),
                sig=f"cbtm{t4}_{e_}", sem="tensor", waits=cw)
            ev.do("vector", lambda e, e_=e_: e.tensor_copy(
                d["cbt"][:, 512 * e_:512 * (e_ + 1)], bk[6][0:128, 0:512]),
                sig=f"cbte{t4}_{e_}", sem="vector",
                waits=(f"cbtm{t4}_{e_}",) + ((f"hw{t4-1}_{e_}_1",) if t4 > 0 else ()))
        for e_ in range(NEXP):
            ev.dma("gpsimd", "gw", lambda e, e_=e_: e.dma_start(
                out=d["gwe"][:, :].rearrange("p (k x) -> p k x", k=8),
                in_=d["gw"].ap()[e_].rearrange("(k p) x -> p k x", p=128)),
                sig=f"gwl{t4}_{e_}",
                waits=() if (t4 == 0 and e_ == 0) else (f"gmm{t4}_{e_-1}" if e_ > 0 else f"gmm{t4-1}_{NEXP-1}",))
            ev.dma("gpsimd", "uw", lambda e, e_=e_: e.dma_start(
                out=d["uwe"][:, :].rearrange("p (k x) -> p k x", k=8),
                in_=d["uw"].ap()[e_].rearrange("(k p) x -> p k x", p=128)),
                sig=f"uwl{t4}_{e_}",
                waits=() if (t4 == 0 and e_ == 0) else (f"umm{t4}_{e_-1}" if e_ > 0 else f"umm{t4-1}_{NEXP-1}",))
            for hc in range(2):
                def gmm(e, e_=e_, hc=hc, t4=t4):
                    lastl = None
                    for q in range(8):
                        lhs = d["gwe"][:, :].rearrange("p (k x) -> p k x", k=8)[:, q, 128 * hc:128 * (hc + 1)]
                        lastl = e.matmul(bk[0][0:128, 0:512], lhs,
                                         d[f"xnf{q}"][:, tsl], start=(q == 0), stop=(q == 7))
                    return lastl
                def umm(e, e_=e_, hc=hc, t4=t4):
                    lastl = None
                    for q in range(8):
                        lhs = d["uwe"][:, :].rearrange("p (k x) -> p k x", k=8)[:, q, 128 * hc:128 * (hc + 1)]
                        lastl = e.matmul(bk[1][0:128, 0:512], lhs,
                                         d[f"xnf{q}"][:, tsl], start=(q == 0), stop=(q == 7))
                    return lastl
                prevh = None
                if not (t4 == 0 and e_ == 0 and hc == 0):
                    if hc == 1:
                        prevh = f"hw{t4}_{e_}_0"
                    elif e_ > 0:
                        prevh = f"hw{t4}_{e_-1}_1"
                    else:
                        prevh = f"hw{t4-1}_{NEXP-1}_1"
                ev.do("tensor", gmm, sig=f"gmm{t4}_{e_}" if hc == 1 else f"gmmh{t4}_{e_}_{hc}",
                      sem="tensor", waits=(f"gwl{t4}_{e_}",) + ((prevh,) if prevh else ()))
                ev.do("tensor", umm, sig=f"umm{t4}_{e_}" if hc == 1 else f"ummh{t4}_{e_}_{hc}",
                      sem="tensor", waits=(f"uwl{t4}_{e_}",))
                gev = f"gmm{t4}_{e_}" if hc == 1 else f"gmmh{t4}_{e_}_{hc}"
                uev = f"umm{t4}_{e_}" if hc == 1 else f"ummh{t4}_{e_}_{hc}"
                ev.do("scalar", lambda e: e.activation(d["sgt"][:, :], bk[0][0:128, 0:512], AF.Silu),
                      sig=f"sge{t4}_{e_}_{hc}", sem="scalar",
                      waits=(gev,) + ((prevh,) if prevh else ()))
                ev.do("scalar", lambda e: e.copy(d["sut"][:, :], bk[1][0:128, 0:512]),
                      sig=f"sue{t4}_{e_}_{hc}", sem="scalar",
                      waits=(uev,) + ((f"hm1{t4}_{e_}_{hc-1}" if hc > 0 else
                                       (f"hm1{t4}_{e_-1}_1" if e_ > 0 else
                                        (f"hm1{t4-1}_{NEXP-1}_1" if t4 > 0 else None)),)))
                def hmul1(e):
                    return e.tensor_tensor(d["sgt"][:, :], d["sgt"][:, :], d["sut"][:, :], OP.mult)
                def hmul2(e, e_=e_, hc=hc):
                    hv = d["hall"][:, :].rearrange("p (x s) -> p x s", s=512)
                    return e.tensor_tensor(hv[:, e_ * 2 + hc, :], d["sgt"][:, :],
                                           d["cbt"][:, 512 * e_:512 * (e_ + 1)], OP.mult)
                ev.do("vector", hmul1, sig=f"hm1{t4}_{e_}_{hc}", sem="vector",
                      waits=(f"sge{t4}_{e_}_{hc}", f"sue{t4}_{e_}_{hc}"))
                ev.do("vector", hmul2, sig=f"hw{t4}_{e_}_{hc}", sem="vector",
                      waits=(f"hm1{t4}_{e_}_{hc}", f"cbte{t4}_{e_}") +
                            ((f"dmm{t4-1}_1_{e_}",) if t4 > 0 else ()))
        for sh in range(4):
            ev.dma("gpsimd", "sg", lambda e, sh=sh: e.dma_start(
                out=d["sgw"][:, :].rearrange("p (k x) -> p k x", k=8),
                in_=d["sgwT"].ap()[:, 512 * sh:512 * (sh + 1)].rearrange("(k p) x -> p k x", p=128)),
                sig=f"sgl{t4}_{sh}",
                waits=() if (t4 == 0 and sh == 0) else ((f"shmm{t4}_{sh-1}_3g",) if sh > 0 else (f"shmm{t4-1}_3_3g",)))
            ev.dma("gpsimd", "su", lambda e, sh=sh: e.dma_start(
                out=d["suw"][:, :].rearrange("p (k x) -> p k x", k=8),
                in_=d["suwT"].ap()[:, 512 * sh:512 * (sh + 1)].rearrange("(k p) x -> p k x", p=128)),
                sig=f"sul{t4}_{sh}",
                waits=() if (t4 == 0 and sh == 0) else ((f"shmm{t4}_{sh-1}_3u",) if sh > 0 else (f"shmm{t4-1}_3_3u",)))
            for hc in range(4):
                def sgmm(e, sh=sh, hc=hc, t4=t4):
                    lastl = None
                    for q in range(8):
                        lhs = d["sgw"][:, :].rearrange("p (k x) -> p k x", k=8)[:, q, 128 * hc:128 * (hc + 1)]
                        lastl = e.matmul(bk[2][0:128, 0:512], lhs, d[f"xnf{q}"][:, tsl],
                                         start=(q == 0), stop=(q == 7))
                    return lastl
                def summ(e, sh=sh, hc=hc, t4=t4):
                    lastl = None
                    for q in range(8):
                        lhs = d["suw"][:, :].rearrange("p (k x) -> p k x", k=8)[:, q, 128 * hc:128 * (hc + 1)]
                        lastl = e.matmul(bk[3][0:128, 0:512], lhs, d[f"xnf{q}"][:, tsl],
                                         start=(q == 0), stop=(q == 7))
                    return lastl
                prevs = None
                if not (t4 == 0 and sh == 0 and hc == 0):
                    if hc > 0:
                        prevs = f"shw{t4}_{sh}_{hc-1}"
                    elif sh > 0:
                        prevs = f"shw{t4}_{sh-1}_3"
                    else:
                        prevs = f"shw{t4-1}_3_3"
                gs = f"shmm{t4}_{sh}_{hc}g" if hc < 3 else f"shmm{t4}_{sh}_3g"
                us = f"shmm{t4}_{sh}_{hc}u" if hc < 3 else f"shmm{t4}_{sh}_3u"
                ev.do("tensor", sgmm, sig=gs, sem="tensor",
                      waits=(f"sgl{t4}_{sh}",) + ((prevs,) if prevs else ()))
                ev.do("tensor", summ, sig=us, sem="tensor", waits=(f"sul{t4}_{sh}",))
                pssg = (f"shw{t4}_{sh}_{hc-1}" if hc > 0 else
                        (f"shw{t4}_{sh-1}_3" if sh > 0 else f"hw{t4}_{NEXP-1}_1"))
                pssu = (f"shm1{t4}_{sh}_{hc-1}" if hc > 0 else
                        (f"shm1{t4}_{sh-1}_3" if sh > 0 else f"hm1{t4}_{NEXP-1}_1"))
                ev.do("scalar", lambda e: e.activation(d["sgt"][:, :], bk[2][0:128, 0:512], AF.Silu),
                      sig=f"ssge{t4}_{sh}_{hc}", sem="scalar", waits=(gs, pssg))
                ev.do("scalar", lambda e: e.copy(d["sut"][:, :], bk[3][0:128, 0:512]),
                      sig=f"ssue{t4}_{sh}_{hc}", sem="scalar", waits=(us, pssu))
                ev.do("vector", lambda e: e.tensor_tensor(d["sgt"][:, :], d["sgt"][:, :], d["sut"][:, :], OP.mult),
                      sig=f"shm1{t4}_{sh}_{hc}", sem="vector",
                      waits=(f"ssge{t4}_{sh}_{hc}", f"ssue{t4}_{sh}_{hc}"))
                ev.do("vector", lambda e, sh=sh, hc=hc: e.tensor_copy(
                    d["shh"][:, :].rearrange("p (x s) -> p x s", s=512)[:, sh * 4 + hc, :],
                    d["sgt"][:, :]), sig=f"shw{t4}_{sh}_{hc}", sem="vector",
                    waits=(f"shm1{t4}_{sh}_{hc}",) +
                          ((f"sdmm{t4-1}_1_{sh}",) if t4 > 0 else ()))
        for half in range(2):
            for e_ in range(NEXP):
                ev.dma("gpsimd", "dw", lambda e, e_=e_, half=half: e.dma_start(
                    out=d["dwe"][:, :].rearrange("p (k x) -> p k x", k=2),
                    in_=d["dw"].ap()[e_][:, 512 * half:512 * (half + 1)]
                        .rearrange("(k p) x -> p k x", p=128)),
                    sig=f"dwl{t4}_{half}_{e_}",
                    waits=() if (t4 == 0 and half == 0 and e_ == 0) else
                          ((f"dmm{t4}_{half}_{e_-1}",) if e_ > 0 else
                           ((f"dmm{t4}_{half-1}_{NEXP-1}",) if half > 0 else (f"sdmm{t4-1}_1_15",))))
                def dmm(e, e_=e_, half=half):
                    hv = d["hall"][:, :].rearrange("p (x s) -> p x s", s=512)
                    lastl = None
                    for hc in range(2):
                        for ocq in range(4):
                            lhs = d["dwe"][:, :].rearrange("p (k x) -> p k x", k=2)[:, hc, 128 * ocq:128 * (ocq + 1)]
                            lastl = e.matmul(bk[4 + ocq][0:128, 0:512], lhs,
                                             hv[:, e_ * 2 + hc, :],
                                             start=(e_ == 0 and hc == 0), stop=False)
                    return lastl
                ev.do("tensor", dmm, sig=f"dmm{t4}_{half}_{e_}", sem="tensor",
                      waits=(f"dwl{t4}_{half}_{e_}", f"hw{t4}_{e_}_1") +
                            ((f"gde{t4}_{half-1}_3",) if (e_ == 0 and half == 1) else ()) +
                            ((f"gde{t4-1}_1_3",) if (e_ == 0 and half == 0 and t4 > 0) else ()))
            for sh in range(16):
                ev.dma("gpsimd", "sd", lambda e, sh=sh, half=half: e.dma_start(
                    out=d["sdw"][:, :],
                    in_=d["sdwT"].ap()[128 * sh:128 * (sh + 1), 512 * half:512 * (half + 1)]),
                    sig=f"sdl{t4}_{half}_{sh}",
                    waits=(f"sdmm{t4}_{half}_{sh-1}",) if sh > 0 else (f"dmm{t4}_{half}_{NEXP-1}",))
                def sdmm(e, sh=sh, half=half, t4=t4):
                    lastl = None
                    for ocq in range(4):
                        lhs = d["sdw"][:, 128 * ocq:128 * (ocq + 1)]
                        lastl = e.matmul(bk[4 + ocq][0:128, 0:512], lhs,
                                         d["shh"][:, :].rearrange("p (x s) -> p x s", s=512)[:, sh, :],
                                         start=False, stop=(sh == 15))
                    return lastl
                ev.do("tensor", sdmm, sig=f"sdmm{t4}_{half}_{sh}", sem="tensor",
                      waits=(f"sdl{t4}_{half}_{sh}", f"shw{t4}_3_3"))
            for ocq in range(4):
                oc = half * 4 + ocq
                pgd = (f"gde{t4}_{half}_{ocq-1}" if ocq > 0 else
                       (f"gde{t4}_{half-1}_3" if half > 0 else
                        (f"gde{t4-1}_1_3" if t4 > 0 else None)))
                ev.do("scalar", lambda e, ocq=ocq: e.activation(
                    d["gst"][:, :], bk[4 + ocq][0:128, 0:512], AF.Silu),
                    sig=f"gse{t4}_{half}_{ocq}", sem="scalar",
                    waits=(f"sdmm{t4}_{half}_15",) + ((pgd,) if pgd else ()))
                ev.dma("gpsimd", "go", lambda e, oc=oc, t4=t4: e.dma_start(
                    out=d["gout"].ap()[128 * oc:128 * (oc + 1), tsl], in_=d["gst"][:, :]),
                    sig=f"gde{t4}_{half}_{ocq}", waits=(f"gse{t4}_{half}_{ocq}",))
    if TRUNC == 5:
        trunc_exit(f"gde0_1_3")
        return
    ev.do("gpsimd", waits=tuple(f"gde{NT-1}_{h}_{q}" for h in range(2) for q in range(4)))


def body_L4r(nc, ev, d, ctx, declare):
    NCK = TSH // 128
    if declare:
        ddecl(nc, d, "xn", "xn", [TSH, E], BF16, "ExternalInput")
        d["rwT"] = nc.dram_tensor("rwT", [E, NEXP], F32, kind="ExternalInput")
        d["corr"] = nc.dram_tensor("corr", [NEXP, 1], F32, kind="ExternalInput")
        ddecl(nc, d, "cident", "c_ident", [128, 128], F32, "ExternalInput")
        ddecl(nc, d, "cfml", "cfml", [NEXP, TSH], BF16, "ExternalOutput")
        sbuf_decl(nc, ctx, d, [(f"xnf{q}", [128, TSH], BF16, dict(is_transpose_dest=True))
                               for q in range(8)])
        sbuf_decl(nc, ctx, d, [
            ("rwb", [128, 8 * NEXP], BF16), ("corrf", [NEXP, 1], F32),
            ("identf", [128, 128], F32),
            ("scf", [NEXP, TSH], F32), ("sbf", [NEXP, TSH], F32),
            ("sctok", [128, NCK * NEXP], F32), ("sbtok", [128, NCK * NEXP], F32),
            ("ps6", [128, NCK * 6 * NGRP], F32),
            ("mA", [128, NCK * NGRP], F32), ("mB", [128, NCK * NGRP], F32),
            ("ggt", [128, NCK * NGRP * NGRP], F32),
            ("gmask", [128, NCK * NGRP], F32),
            ("emask", [128, NCK * NEXP], F32),
            ("msk", [128, NCK * NEXP], F32), ("msk2", [128, NCK * NEXP], F32),
            ("egt", [128, NCK * NEXP * NEXP], F32),
            ("erank", [128, NCK * NEXP], F32),
            ("wts", [128, NCK * NEXP], F32), ("wsum", [128, NCK], F32),
            ("cfm", [NEXP, TSH], F32), ("cfmb", [NEXP, TSH], BF16)])
        d["bk"] = [ctx.enter_context(nc.psum_tensor(f"bk{i}", [128, 512], F32))
                   for i in range(8)]
        return

    bk = d["bk"]
    for q in range(8):
        ev.dma("sync", "xn", lambda e, q=q: e.dma_start(
            out=d[f"xnf{q}"][:, :],
            in_=d["xn"].ap()[:, 128 * q:128 * (q + 1)], transpose=True),
            sig=f"xl{q}")
    XL = tuple(f"xl{q}" for q in range(8))
    ev.dma("gpsimd", "rw", lambda e: e.dma_start(
        out=d["rwb"][:, :].rearrange("p (k x) -> p k x", k=8),
        in_=d["rwT"].ap().rearrange("(k p) x -> p k x", p=128)), sig="ld_rw")
    ev.dma("sync", "kc", lambda e: e.dma_start(out=d["corrf"][:, :], in_=d["corr"].ap()),
           sig="ld_corr")
    ev.dma("sync", "kc", lambda e: e.dma_start(out=d["identf"][:, :], in_=d["cident"].ap()),
           sig="ld_id")
    for t4 in range(4):
        def rmm(e, t4=t4):
            lastl = None
            for q in range(8):
                lastl = e.matmul(bk[6][0:NEXP, 0:512],
                                 d["rwb"][:, :].rearrange("p (k x) -> p k x", k=8)[:, q, :],
                                 d[f"xnf{q}"][:, 512 * t4:512 * (t4 + 1)],
                                 start=(q == 0), stop=(q == 7))
            return lastl
        ev.do("tensor", rmm, sig=f"rmm{t4}", sem="tensor",
              waits=XL + ("ld_rw",) + ((f"rsev{t4-1}",) if t4 > 0 else ()))
        ev.do("scalar", lambda e, t4=t4: e.activation(
            d["scf"][:, 512 * t4:512 * (t4 + 1)], bk[6][0:NEXP, 0:512], AF.Sigmoid),
            sig=f"rsev{t4}", sem="scalar", waits=(f"rmm{t4}",))
    ev.do("vector", lambda e: e.tensor_scalar(
        d["sbf"][:, :], d["scf"][:, :], d["corrf"][:, :], None, OP.add),
        sig="sbf", sem="vector", waits=("rsev3", "ld_corr"))
    for src, dst, sig0 in (("scf", "sctok", "tsc"), ("sbf", "sbtok", "tsb")):
        for ck in range(NCK):
            ev.do("tensor", lambda e, src=src, ck=ck: e.transpose(
                bk[7][0:128, 0:NEXP], d[src][:, 128 * ck:128 * (ck + 1)],
                d["identf"][0:NEXP, 0:NEXP]),
                sig=f"{sig0}m{ck}", sem="tensor",
                waits=(("sbf", "ld_id") if (ck == 0 and src == "scf") else
                       ((f"{sig0}e{ck-1}",) if ck > 0 else ("tsce15",))))
            ev.do("vector", lambda e, dst=dst, ck=ck: e.tensor_copy(
                d[dst][:, NEXP * ck:NEXP * (ck + 1)], bk[7][0:128, 0:NEXP]),
                sig=f"{sig0}e{ck}", sem="vector", waits=(f"{sig0}m{ck}",))
    def V4(nm):
        return d[nm][:, :].rearrange("p (c g) -> p c g", g=NGRP)
    def V16(nm):
        return d[nm][:, :].rearrange("p (c x) -> p c x", x=NEXP)
    sbg = d["sbtok"][:, :].rearrange("p (c g e) -> p c g e", g=NGRP, e=EPG)
    P6 = d["ps6"][:, :].rearrange("p (c s g) -> p c s g", s=6, g=NGRP)
    pairs = [(0, 1), (0, 2), (0, 3), (1, 2), (1, 3), (2, 3)]
    def step_ps(e):
        lastl = None
        for pi, (i1, i2) in enumerate(pairs):
            lastl = e.tensor_tensor(P6[:, :, pi, :], sbg[:, :, :, i1], sbg[:, :, :, i2], OP.add)
        return lastl
    ev.do("vector", step_ps, sig="ps6", sem="vector", waits=("tsbe15",))
    ev.do("vector", lambda e: (e.tensor_tensor(V4("mA"), P6[:, :, 0, :], P6[:, :, 1, :], OP.max),
                               e.tensor_tensor(V4("mB"), P6[:, :, 2, :], P6[:, :, 3, :], OP.max))[-1],
          sig="mx1", sem="vector", waits=("ps6",))
    ev.do("vector", lambda e: e.tensor_tensor(V4("mA"), V4("mA"), V4("mB"), OP.max),
          sig="mx2", sem="vector", waits=("mx1",))
    ev.do("vector", lambda e: e.tensor_tensor(V4("mB"), P6[:, :, 4, :], P6[:, :, 5, :], OP.max),
          sig="mx3", sem="vector", waits=("mx2",))
    ev.do("vector", lambda e: e.tensor_tensor(V4("mA"), V4("mA"), V4("mB"), OP.max),
          sig="gsc", sem="vector", waits=("mx3",))
    ev.do("vector", lambda e: e.memset(d["ggt"][:, :], 0.0), sig="ggt0", sem="vector")
    def step_ggt(e):
        gsc = V4("mA")
        gf = d["ggt"][:, :].rearrange("p (c x) -> p c x", x=NGRP * NGRP)
        lastl = None
        for dl in range(1, NGRP):
            n = NGRP - dl
            lastl = e.tensor_tensor(gf[:, :, dl:dl + 5 * (n - 1) + 1:5],
                                    gsc[:, :, dl:], gsc[:, :, 0:n], OP.is_gt)
            lastl = e.tensor_tensor(gf[:, :, 4 * dl:4 * dl + 5 * (n - 1) + 1:5],
                                    gsc[:, :, 0:n], gsc[:, :, dl:], OP.is_gt)
        return lastl
    ev.do("vector", step_ggt, sig="ggt", sem="vector", waits=("gsc", "ggt0"))
    ev.do("vector", lambda e: e.tensor_reduce(
        V4("gmask"), d["ggt"][:, :].rearrange("p (c g x) -> p c g x", g=NGRP, x=NGRP),
        AX.X, OP.add), sig="grank", sem="vector", waits=("ggt",))
    ev.do("vector", lambda e: e.tensor_scalar(V4("gmask"), V4("gmask"), 2.0, None, OP.is_lt),
          sig="gmask", sem="vector", waits=("grank",))
    ev.do("vector", lambda e: e.tensor_copy(
        d["emask"][:, :].rearrange("p (c g e) -> p c g e", g=NGRP, e=EPG),
        V4("gmask").rearrange("p c (g o) -> p c g o", o=1).broadcast_to([128, NCK, NGRP, EPG])),
        sig="emask", sem="vector", waits=("gmask",))
    ev.do("vector", lambda e: e.tensor_tensor(
        d["msk"][:, :], d["sbtok"][:, :], d["emask"][:, :], OP.mult),
        sig="mska", sem="vector", waits=("emask",))
    ev.do("vector", lambda e: e.tensor_scalar(
        d["msk2"][:, :], d["emask"][:, :], 1.0, 1e30, OP.subtract, OP.mult),
        sig="mskb", sem="vector", waits=("emask",))
    ev.do("vector", lambda e: e.tensor_tensor(
        d["msk"][:, :], d["msk"][:, :], d["msk2"][:, :], OP.add),
        sig="msk", sem="vector", waits=("mska", "mskb"))
    ev.do("vector", lambda e: e.memset(d["egt"][:, :], 0.0), sig="egt0", sem="vector")
    def step_egt(e):
        mk = V16("msk")
        ef = d["egt"][:, :].rearrange("p (c x) -> p c x", x=NEXP * NEXP)
        lastl = None
        for dl in range(1, NEXP):
            n = NEXP - dl
            lastl = e.tensor_tensor(ef[:, :, dl:dl + 17 * (n - 1) + 1:17],
                                    mk[:, :, dl:], mk[:, :, 0:n], OP.is_gt)
            lastl = e.tensor_tensor(ef[:, :, 16 * dl:16 * dl + 17 * (n - 1) + 1:17],
                                    mk[:, :, 0:n], mk[:, :, dl:], OP.is_gt)
        return lastl
    ev.do("vector", step_egt, sig="egt", sem="vector", waits=("msk", "egt0"))
    ev.do("vector", lambda e: e.tensor_reduce(
        V16("erank"), d["egt"][:, :].rearrange("p (c a x) -> p c a x", a=NEXP, x=NEXP),
        AX.X, OP.add), sig="erank", sem="vector", waits=("egt",))
    ev.do("vector", lambda e: e.tensor_scalar(V16("erank"), V16("erank"), float(TOPK), None, OP.is_lt),
          sig="kmask", sem="vector", waits=("erank",))
    ev.do("vector", lambda e: e.tensor_tensor(
        d["wts"][:, :], d["sctok"][:, :], d["erank"][:, :], OP.mult),
        sig="wts", sem="vector", waits=("kmask",))
    ev.do("vector", lambda e: e.tensor_reduce(
        d["wsum"][:, :].rearrange("p (c o) -> p c o", o=1),
        V16("wts"), AX.X, OP.add), sig="wsum", sem="vector", waits=("wts",))
    ev.do("vector", lambda e: e.tensor_scalar_add(d["wsum"][:, :], d["wsum"][:, :], 1e-20),
          sig="wsum2", sem="vector", waits=("wsum",))
    ev.do("vector", lambda e: e.reciprocal(d["wsum"][:, :], d["wsum"][:, :]),
          sig="winv", sem="vector", waits=("wsum2",))
    ev.do("vector", lambda e: e.tensor_tensor(
        V16("wts"), V16("wts"),
        d["wsum"][:, :].rearrange("p (c o) -> p c o", o=1).broadcast_to([128, NCK, NEXP]),
        OP.mult), sig="comb", sem="vector", waits=("winv",))
    for ck in range(NCK):
        ev.do("tensor", lambda e, ck=ck: e.transpose(
            bk[7][0:NEXP, 0:128], d["wts"][:, NEXP * ck:NEXP * (ck + 1)], d["identf"][:, :]),
            sig=f"cbm{ck}", sem="tensor",
            waits=("comb", "ld_id") if ck == 0 else (f"cbe{ck-1}",))
        ev.do("vector", lambda e, ck=ck: e.tensor_copy(
            d["cfm"][:, 128 * ck:128 * (ck + 1)], bk[7][0:NEXP, 0:128]),
            sig=f"cbe{ck}", sem="vector", waits=(f"cbm{ck}",))
    ev.do("vector", lambda e: e.tensor_copy(d["cfmb"][:, :], d["cfm"][:, :]),
          sig="cfmc", sem="vector", waits=("cbe15",))
    ev.dma("gpsimd", "cf", lambda e: e.dma_start(
        out=d["cfml"].ap(), in_=d["cfmb"][:, :]), sig="cfst", waits=("cfmc",))
    ev.do("gpsimd", waits=("cfst",))


def body_L4e(nc, ev, d, ctx, declare):
    NST = (B * N) // 512
    if declare:
        ddecl(nc, d, "xnag", "xnag", [B * N, E], BF16, "ExternalInput")
        ddecl(nc, d, "cfma", "cfma", [128, TSH], BF16, "ExternalInput")
        ddecl(nc, d, "spf", "spf", [B * N, E], BF16, "ExternalOutput")
        d["gwh"] = nc.dram_tensor("gwh", [2, E, HR], F32, kind="ExternalInput")
        d["uwh"] = nc.dram_tensor("uwh", [2, E, HR], F32, kind="ExternalInput")
        d["dwh"] = nc.dram_tensor("dwh", [2, HR, E], F32, kind="ExternalInput")
        d["sgwh"] = nc.dram_tensor("sgwh", [E, HR], F32, kind="ExternalInput")
        d["suwh"] = nc.dram_tensor("suwh", [E, HR], F32, kind="ExternalInput")
        d["sdwh"] = nc.dram_tensor("sdwh", [HR, E], F32, kind="ExternalInput")
        d["fselb"] = nc.dram_tensor("fselb", [8, 2, 128, 128], BF16, kind="ExternalInput")
        sbuf_decl(nc, ctx, d, [(f"xnt{p}", [128, 8 * 512], BF16,
                                dict(is_transpose_dest=True)) for p in range(2)])
        sbuf_decl(nc, ctx, d, [
            ("cfms", [128, TSH], BF16), ("fsb", [128, 16 * 128], BF16),
            ("gwb", [128, 2 * 8 * HR], BF16), ("uwb", [128, 2 * 8 * HR], BF16),
            ("dwb", [128, 2 * 2 * E], BF16),
            ("sgwb", [128, 8 * HR], BF16), ("suwb", [128, 8 * HR], BF16),
            ("sdwb", [128, 2 * E], BF16),
            ("cbts0", [128, 2 * 512], BF16), ("cbts1", [128, 2 * 512], BF16),
            ("sgt0", [128, 512], BF16), ("sgt1", [128, 512], BF16),
            ("sut0", [128, 512], BF16), ("sut1", [128, 512], BF16),
            ("hv0", [128, 6 * 512], BF16), ("hv1", [128, 6 * 512], BF16),
            ("fstg0", [128, 4 * E], BF16), ("fstg1", [128, 4 * E], BF16)])
        d["bk"] = [ctx.enter_context(nc.psum_tensor(f"bk{i}", [128, 512], F32))
                   for i in range(8)]
        return

    bk = d["bk"]
    ev.dma("sync", "cf", lambda e: e.dma_start(
        out=d["cfms"][:, :], in_=d["cfma"].ap()), sig="ld_cf")
    ev.dma("sync", "cf", lambda e: e.dma_start(
        out=d["fsb"][:, :].rearrange("p (a b) -> p a b", a=16),
        in_=d["fselb"].ap().rearrange("j l p c -> p (j l) c")), sig="ld_fs")
    for nm, src, k in (("gwb", "gwh", 2), ("uwb", "uwh", 2)):
        ev.dma("gpsimd", "wl", lambda e, nm=nm, src=src: e.dma_start(
            out=d[nm][:, :].rearrange("p (l k h) -> p (l k) h", l=2, k=8),
            in_=d[src].ap().rearrange("l (k p) h -> p (l k) h", p=128)),
            sig=f"ld_{nm}")
    ev.dma("gpsimd", "wl", lambda e: e.dma_start(
        out=d["dwb"][:, :].rearrange("p (l k h) -> p (l k) h", l=2, k=2),
        in_=d["dwh"].ap().rearrange("l (k p) h -> p (l k) h", p=128)),
        sig="ld_dwb")
    for nm, src in (("sgwb", "sgwh"), ("suwb", "suwh")):
        ev.dma("gpsimd", "wl", lambda e, nm=nm, src=src: e.dma_start(
            out=d[nm][:, :].rearrange("p (k h) -> p k h", k=8),
            in_=d[src].ap().rearrange("(k p) h -> p k h", p=128)),
            sig=f"ld_{nm}")
    ev.dma("gpsimd", "wl", lambda e: e.dma_start(
        out=d["sdwb"][:, :].rearrange("p (k h) -> p k h", k=2),
        in_=d["sdwh"].ap().rearrange("(k p) h -> p k h", p=128)),
        sig="ld_sdwb")

    GWB = lambda: d["gwb"][:, :].rearrange("p (l k h) -> p l k h", l=2, k=8)
    UWB = lambda: d["uwb"][:, :].rearrange("p (l k h) -> p l k h", l=2, k=8)
    DWB = lambda: d["dwb"][:, :].rearrange("p (l k h) -> p l k h", l=2, k=2)
    SGWB = lambda: d["sgwb"][:, :].rearrange("p (k h) -> p k h", k=8)
    SUWB = lambda: d["suwb"][:, :].rearrange("p (k h) -> p k h", k=8)
    SDWB = lambda: d["sdwb"][:, :].rearrange("p (k h) -> p k h", k=2)

    for st in range(NST):
        p = st % 2
        j = st // 4
        xnt, cbts = d[f"xnt{p}"], d[f"cbts{p}"]
        hv = d[f"hv{p}"][:, :].rearrange("p (u s) -> p u s", s=512)
        fstg = d[f"fstg{p}"]
        for q in range(8):
            ev.dma("sync", "xn", lambda e, st=st, q=q, xnt=xnt: e.dma_start(
                out=xnt[:, 512 * q:512 * (q + 1)],
                in_=d["xnag"].ap()[512 * st:512 * (st + 1), 128 * q:128 * (q + 1)],
                transpose=True), sig=f"xl{st}_{q}",
                waits=(f"gu{st-2}_5u",) if (st >= 2 and q == 0) else ())
        for el in range(2):
            sw = ["ld_cf", "ld_fs"] if st == 0 and el == 0 else []
            if st >= 1 and el == 0:
                sw.append(f"cbe{st-1}_1")
            ev.do("tensor", lambda e, j=j, el=el, st=st: e.matmul(
                bk[6 + el][0:128, 0:512],
                d["fsb"][:, :].rearrange("p (a b) -> p a b", a=16)[:, 2 * j + el, :],
                d["cfms"][:, 512 * (st % 4):512 * (st % 4 + 1)],
                start=True, stop=True),
                sig=f"cbm{st}_{el}", sem="tensor", waits=tuple(sw))
            cw = [f"cbm{st}_{el}"]
            if st >= 2:
                cw.append(f"hm2{st-2}_{2*el+1}")
            ev.do("vector", lambda e, el=el, cbts=cbts: e.tensor_copy(
                cbts[:, 512 * el:512 * (el + 1)], bk[6 + el][0:128, 0:512]),
                sig=f"cbe{st}_{el}", sem="vector", waits=tuple(cw))
        for u in range(6):
            up_ = u % 2
            bkg, bku = bk[2 * up_], bk[2 * up_ + 1]
            sgt, sut = d[f"sgt{up_}"], d[f"sut{up_}"]
            if u < 4:
                el, hc = u // 2, u % 2
                glhs = lambda q, el=el, hc=hc: GWB()[:, el, q, 128 * hc:128 * (hc + 1)]
                ulhs = lambda q, el=el, hc=hc: UWB()[:, el, q, 128 * hc:128 * (hc + 1)]
            else:
                hc = u - 4
                glhs = lambda q, hc=hc: SGWB()[:, q, 128 * hc:128 * (hc + 1)]
                ulhs = lambda q, hc=hc: SUWB()[:, q, 128 * hc:128 * (hc + 1)]
            def gmm(e, glhs=glhs, st=st, xnt=xnt, bkg=bkg):
                lastl = None
                for q in range(8):
                    lastl = e.matmul(bkg[0:128, 0:512], glhs(q),
                                     xnt[:, 512 * q:512 * (q + 1)],
                                     start=(q == 0), stop=(q == 7))
                return lastl
            def umm(e, ulhs=ulhs, st=st, xnt=xnt, bku=bku):
                lastl = None
                for q in range(8):
                    lastl = e.matmul(bku[0:128, 0:512], ulhs(q),
                                     xnt[:, 512 * q:512 * (q + 1)],
                                     start=(q == 0), stop=(q == 7))
                return lastl
            mw = []
            if u == 0:
                mw = [f"xl{st}_7"] + (["ld_gwb", "ld_uwb", "ld_sgwb", "ld_suwb"]
                                      if st == 0 else [])
            if u >= 2:
                mw.append(f"sge{st}_{u-2}")
                mw.append(f"sue{st}_{u-2}")
            elif st >= 1:
                mw.append(f"sge{st-1}_{u+4}")
                mw.append(f"sue{st-1}_{u+4}")
            ev.do("tensor", gmm, sig=f"gu{st}_{u}g", sem="tensor", waits=tuple(mw))
            ev.do("tensor", umm, sig=f"gu{st}_{u}u" if u < 5 else f"gu{st}_5u",
                  sem="tensor")
            usig = f"gu{st}_{u}u" if u < 5 else f"gu{st}_5u"
            ew = [f"gu{st}_{u}g"]
            uw_ = [usig]
            if u >= 2:
                ew.append(f"hm2{st}_{u-2}")
                uw_.append(f"hm1{st}_{u-2}")
            elif st >= 1:
                ew.append(f"hm2{st-1}_{u+4}")
                uw_.append(f"hm1{st-1}_{u+4}")
            ev.do("scalar", lambda e, bkg=bkg, sgt=sgt: e.activation(
                sgt[:, :], bkg[0:128, 0:512], AF.Silu),
                sig=f"sge{st}_{u}", sem="scalar", waits=tuple(ew))
            ev.do("scalar", lambda e, bku=bku, sut=sut: e.copy(
                sut[:, :], bku[0:128, 0:512]),
                sig=f"sue{st}_{u}", sem="scalar", waits=tuple(uw_))
            if u < 4:
                ev.do("vector", lambda e, sgt=sgt, sut=sut: e.tensor_tensor(
                    sgt[:, :], sgt[:, :], sut[:, :], OP.mult),
                    sig=f"hm1{st}_{u}", sem="vector",
                    waits=(f"sge{st}_{u}", f"sue{st}_{u}"))
                hw = [f"hm1{st}_{u}", f"cbe{st}_{u//2}"]
                if st >= 2:
                    hw.append(f"dmm{st-2}_7")
                ev.do("vector", lambda e, u=u, sgt=sgt, cbts=cbts, hv=hv:
                      e.tensor_tensor(hv[:, u, :], sgt[:, :],
                                      cbts[:, 512 * (u // 2):512 * (u // 2 + 1)],
                                      OP.mult),
                      sig=f"hm2{st}_{u}", sem="vector", waits=tuple(hw))
            else:
                hw = [f"sge{st}_{u}", f"sue{st}_{u}"]
                if st >= 2:
                    hw.append(f"dmm{st-2}_7")
                ev.do("vector", lambda e, u=u, sgt=sgt, sut=sut, hv=hv:
                      e.tensor_tensor(hv[:, u, :], sgt[:, :], sut[:, :], OP.mult),
                      sig=f"hm1{st}_{u}", sem="vector", waits=tuple(hw))
                ev.alias(f"hm2{st}_{u}", f"hm1{st}_{u}")
        HUNITS = [f"hm2{st}_{u}" for u in range(4)] + [f"hm1{st}_{u}" for u in (4, 5)]
        for g in range(8):
            tc, half = g // 2, g % 2
            bkd = bk[4 + g % 2]
            def dmm(e, tc=tc, half=half, bkd=bkd, hv=hv):
                lastl = None
                for u in range(6):
                    if u < 4:
                        el, hc = u // 2, u % 2
                        mov = DWB()[:, el, hc, 512 * half:512 * (half + 1)]
                    else:
                        mov = SDWB()[:, u - 4, 512 * half:512 * (half + 1)]
                    lastl = e.matmul(bkd[0:128, 0:512],
                                     hv[:, u, 128 * tc:128 * (tc + 1)], mov,
                                     start=(u == 0), stop=(u == 5))
                return lastl
            dw_ = list(HUNITS) if g == 0 else []
            if g == 0 and st == 0:
                dw_ += ["ld_dwb", "ld_sdwb"]
            if g >= 2:
                dw_.append(f"fev{st}_{g-2}")
            elif st >= 1:
                dw_.append(f"fev{st-1}_{g+6}")
            ev.do("tensor", dmm, sig=f"dmm{st}_{g}", sem="tensor", waits=tuple(dw_))
            fw = [f"dmm{st}_{g}"]
            if st >= 2 and g == 0:
                fw.append(f"fst{st-2}_3")
            ev.do("scalar", lambda e, tc=tc, half=half, bkd=bkd, fstg=fstg: e.copy(
                fstg[:, E * tc + 512 * half:E * tc + 512 * (half + 1)],
                bkd[0:128, 0:512]),
                sig=f"fev{st}_{g}", sem="scalar", waits=tuple(fw))
        for tc in range(4):
            ev.dma("gpsimd", "fo", lambda e, st=st, tc=tc, fstg=fstg: e.dma_start(
                out=d["spf"].ap()[512 * st + 128 * tc:512 * st + 128 * (tc + 1), 0:E],
                in_=fstg[:, E * tc:E * (tc + 1)]),
                sig=f"fst{st}_{tc}", waits=(f"fev{st}_{2*tc+1}",))
    ev.do("gpsimd", waits=(f"fst{NST-1}_3",))


def body_L4b(nc, ev, d, ctx, declare):
    NK = 24
    NE = 8
    if declare:
        d["wsl"] = nc.dram_tensor("wsl", [H, 3, TSH, 128], BF16, kind="ExternalInput")
        d["g"] = nc.dram_tensor("g", [E, TSH], BF16, kind="ExternalInput")
        d["x"] = nc.dram_tensor("x", [TSH, E], F32, kind="ExternalInput")
        d["ow"] = nc.dram_tensor("ow", [D1, E], F32, kind="ExternalInput")
        d["ob"] = nc.dram_tensor("ob", [E, 1], F32, kind="ExternalInput")
        d["cident"] = nc.dram_tensor("c_ident", [128, 128], F32, kind="ExternalInput")
        d["out"] = nc.dram_tensor("out", [TSH, E], F32, kind="ExternalOutput")
        sbuf_decl(nc, ctx, d, [(f"Xw{p}", [128, NK * 512], BF16,
                                dict(is_transpose_dest=True)) for p in range(2)])
        sbuf_decl(nc, ctx, d, [
            ("gb0", [128, NE * 512], BF16), ("gb1", [128, NE * 512], BF16),
            ("xb0", [128, 4 * E], F32), ("xb1", [128, 4 * E], F32),
            ("outb0", [128, 4 * E], F32), ("outb1", [128, 4 * E], F32),
            ("owb", [128, NK * E], BF16),
            ("obf", [128, NE], F32), ("identf", [128, 128], F32),
            ("Pt", [128, NE * 512], F32)])
        d["bk"] = [ctx.enter_context(nc.psum_tensor(f"bk{i}", [128, 512], F32))
                   for i in range(4)]
        return

    bk = d["bk"]
    ev.dma("gpsimd", "ow", lambda e: e.dma_start(
        out=d["owb"][:, :].rearrange("p (j c) -> p j c", j=NK),
        in_=d["ow"].ap().rearrange("(j p) c -> p j c", p=128)), sig="ld_ow")
    ev.dma("sync", "misc", lambda e: e.dma_start(
        out=d["obf"][:, :].rearrange("p (o x) -> p o x", x=1),
        in_=d["ob"].ap().rearrange("(o p) x -> p o x", p=128),
        allow_slow_non_contiguous=True), sig="ld_ob")
    ev.dma("sync", "misc", lambda e: e.dma_start(
        out=d["identf"][:, :], in_=d["cident"].ap()), sig="ld_id")

    for t4 in range(4):
        p = t4 % 2
        Xw, gb, xb, outb = d[f"Xw{p}"], d[f"gb{p}"], d[f"xb{p}"], d[f"outb{p}"]
        for h in range(H):
            for oc in range(3):
                kk = 3 * h + oc
                ev.dma("sync", "xw", lambda e, h=h, oc=oc, kk=kk, t4=t4: e.dma_start(
                    out=Xw[:, 512 * kk:512 * (kk + 1)],
                    in_=d["wsl"].ap()[h, oc, 512 * t4:512 * (t4 + 1), :],
                    transpose=True), sig=f"xw{t4}_{kk}",
                    waits=(f"pmm{t4-2}_{NE-1}",) if (t4 >= 2 and kk == 0) else ())
        for e_ in range(NE):
            ev.dma("sync", "g", lambda e, e_=e_, t4=t4: e.dma_start(
                out=gb[:, 512 * e_:512 * (e_ + 1)],
                in_=d["g"].ap()[128 * e_:128 * (e_ + 1), 512 * t4:512 * (t4 + 1)]),
                sig=f"g{t4}_{e_}",
                waits=(f"prod{t4-2}_{NE-1}",) if (t4 >= 2 and e_ == 0) else ())
        for tc in range(4):
            ev.dma("gpsimd", "x", lambda e, tc=tc, t4=t4: e.dma_start(
                out=xb[:, E * tc:E * (tc + 1)],
                in_=d["x"].ap()[512 * t4 + 128 * tc:512 * t4 + 128 * (tc + 1), :]),
                sig=f"x{t4}_{tc}",
                waits=(f"radd{t4-2}_31",) if (t4 >= 2 and tc == 0) else ())
        for e_ in range(NE):
            def pmm(e, e_=e_, t4=t4, Xw=Xw):
                lastl = None
                for kk in range(NK):
                    lastl = e.matmul(
                        bk[e_ % 2][0:128, 0:512],
                        d["owb"][:, :].rearrange("p (j c) -> p j c", j=NK)
                        [:, kk, 128 * e_:128 * (e_ + 1)],
                        Xw[:, 512 * kk:512 * (kk + 1)],
                        start=(kk == 0), stop=(kk == NK - 1))
                return lastl
            bw = [f"xw{t4}_{NK-1}", "ld_ow"]
            if e_ >= 2:
                bw.append(f"sadd{t4}_{e_-2}")
            elif t4 >= 1:
                bw.append(f"sadd{t4-1}_{e_+6}")
            ev.do("tensor", pmm, sig=f"pmm{t4}_{e_}", sem="tensor", waits=tuple(bw))
            sw = [f"pmm{t4}_{e_}", "ld_ob"]
            if t4 >= 1:
                sw.append(f"tmm{t4-1}_{24+e_}")
            ev.do("vector", lambda e, e_=e_: e.tensor_scalar(
                d["Pt"][:, 512 * e_:512 * (e_ + 1)], bk[e_ % 2][0:128, 0:512],
                d["obf"][:, e_:e_ + 1], None, OP.add),
                sig=f"sadd{t4}_{e_}", sem="vector", waits=tuple(sw))
            ev.do("vector", lambda e, e_=e_, gb=gb: e.tensor_tensor(
                d["Pt"][:, 512 * e_:512 * (e_ + 1)], d["Pt"][:, 512 * e_:512 * (e_ + 1)],
                gb[:, 512 * e_:512 * (e_ + 1)], OP.mult),
                sig=f"prod{t4}_{e_}", sem="vector",
                waits=(f"sadd{t4}_{e_}", f"g{t4}_{NE-1}"))
        for tc in range(4):
            for e_ in range(NE):
                idx = tc * NE + e_
                tw = [f"prod{t4}_{NE-1}"] if idx == 0 else []
                if t4 == 0 and idx == 0:
                    tw.append("ld_id")
                if idx >= 2:
                    tw.append(f"radd{t4}_{idx-2}")
                elif t4 >= 1:
                    tw.append(f"radd{t4-1}_{idx+30}")
                ev.do("tensor", lambda e, tc=tc, e_=e_, idx=idx: e.transpose(
                    bk[2 + idx % 2][0:128, 0:128],
                    d["Pt"][:, 512 * e_ + 128 * tc:512 * e_ + 128 * (tc + 1)],
                    d["identf"][:, :]),
                    sig=f"tmm{t4}_{idx}", sem="tensor", waits=tuple(tw))
                rw = [f"tmm{t4}_{idx}", f"x{t4}_3"]
                if t4 >= 2:
                    rw.append(f"ost{t4-2}_{tc}")
                ev.do("vector", lambda e, tc=tc, e_=e_, idx=idx, outb=outb, xb=xb:
                      e.tensor_tensor(
                          outb[:, E * tc + 128 * e_:E * tc + 128 * (e_ + 1)],
                          bk[2 + idx % 2][0:128, 0:128],
                          xb[:, E * tc + 128 * e_:E * tc + 128 * (e_ + 1)], OP.add),
                      sig=f"radd{t4}_{idx}", sem="vector", waits=tuple(rw))
            ev.dma("gpsimd", "out", lambda e, tc=tc, t4=t4, outb=outb: e.dma_start(
                out=d["out"].ap()[512 * t4 + 128 * tc:512 * t4 + 128 * (tc + 1), :],
                in_=outb[:, E * tc:E * (tc + 1)]),
                sig=f"ost{t4}_{tc}", waits=(f"radd{t4}_{tc * NE + NE - 1}",))
    ev.do("gpsimd", waits=("ost3_3",))


def body_L5o(nc, ev, d, ctx, declare):
    if declare:
        if "w" not in d:
            d["w"] = nc.dram_tensor("w", [B, 3, N, 128], BF16, kind="ExternalInput")
        if "owh" not in d:
            d["owh"] = nc.dram_tensor("owh", [DH, E], F32, kind="ExternalInput")
        if "spart" not in d:
            d["spart"] = nc.dram_tensor("spart", [B * N, E], BF16, kind="ExternalOutput")
        sbuf_decl(nc, ctx, d, [(f"Xo{p}", [128, 3 * 512], BF16,
                                dict(is_transpose_dest=True)) for p in range(2)])
        sbuf_decl(nc, ctx, d, [("owb", [128, 3 * E], BF16),
                               ("stg0", [128, 4 * E], BF16), ("stg1", [128, 4 * E], BF16)])
        d["bk"] = [ctx.enter_context(nc.psum_tensor(f"bk{i}", [128, 512], F32))
                   for i in range(4)]
        return
    bk = d["bk"]
    ev.dma("gpsimd", "ow", lambda e: e.dma_start(
        out=d["owb"][:, :].rearrange("p (o c) -> p o c", o=3),
        in_=d["owh"].ap().rearrange("(o p) c -> p o c", p=128)), sig="ld_ow")
    for st in range(32):
        b, s = st // 8, st % 8
        p = st % 2
        Xo, stg = d[f"Xo{p}"], d[f"stg{p}"]
        for oc in range(3):
            ev.dma("sync", "xw", lambda e, b=b, s=s, oc=oc, Xo=Xo: e.dma_start(
                out=Xo[:, 512 * oc:512 * (oc + 1)],
                in_=d["w"].ap()[b, oc, 512 * s:512 * (s + 1), :], transpose=True),
                sig=f"xw{st}_{oc}",
                waits=(f"omm{st-2}_3_1",) if (st >= 2 and oc == 0) else ())
        for tc in range(4):
            for hf in range(2):
                bki = 2 * (tc % 2) + hf
                def omm(e, tc=tc, hf=hf, bki=bki, Xo=Xo):
                    lastl = None
                    for oc in range(3):
                        lastl = e.matmul(
                            bk[bki][0:128, 0:512],
                            Xo[:, 512 * oc + 128 * tc:512 * oc + 128 * (tc + 1)],
                            d["owb"][:, :].rearrange("p (o c) -> p o c", o=3)
                            [:, oc, 512 * hf:512 * (hf + 1)],
                            start=(oc == 0), stop=(oc == 2))
                    return lastl
                mw = [f"xw{st}_2", "ld_ow"]
                if tc >= 2:
                    mw.append(f"oev{st}_{tc-2}_{hf}")
                elif st >= 1:
                    mw.append(f"oev{st-1}_{tc+2}_{hf}")
                ev.do("tensor", omm, sig=f"omm{st}_{tc}_{hf}", sem="tensor",
                      waits=tuple(mw))
                sw = [f"omm{st}_{tc}_{hf}"]
                if st >= 2 and tc == 0 and hf == 0:
                    sw.append(f"sst{st-2}_3")
                ev.do("scalar", lambda e, tc=tc, hf=hf, bki=bki, stg=stg: e.copy(
                    stg[:, E * tc + 512 * hf:E * tc + 512 * (hf + 1)],
                    bk[bki][0:128, 0:512]),
                    sig=f"oev{st}_{tc}_{hf}", sem="scalar", waits=tuple(sw))
            ev.dma("gpsimd", "sp", lambda e, st=st, tc=tc, stg=stg: e.dma_start(
                out=d["spart"].ap()[512 * st + 128 * tc:512 * st + 128 * (tc + 1), 0:E],
                in_=stg[:, E * tc:E * (tc + 1)]),
                sig=f"sst{st}_{tc}", waits=(f"oev{st}_{tc}_1",))
        if st % 4 == 3 and d.get("_rs_hook") is not None:
            d["_rs_hook"](st)
    ev.do("gpsimd", waits=(f"sst31_3",))


def body_L6(nc, ev, d, ctx, declare):
    if declare:
        if "sred" not in d:
            d["sred"] = nc.dram_tensor("sred", [TSH, E], BF16, kind="ExternalInput")
        if "gout" not in d:
            d["gout"] = nc.dram_tensor("gout", [E, TSH], BF16, kind="ExternalInput")
        if "x" not in d:
            d["x"] = nc.dram_tensor("x", [TSH, E], F32, kind="ExternalInput")
        if "obT" not in d:
            d["obT"] = nc.dram_tensor("obT", [1, E], F32, kind="ExternalInput")
        if "cones" not in d:
            d["cones"] = nc.dram_tensor("c_ones", [128, 128], F32, kind="ExternalInput")
        d["out"] = nc.dram_tensor("out", [TSH, E], F32, kind="ExternalOutput")
        sbuf_decl(nc, ctx, d, [(f"gt{p}", [128, E], BF16,
                                dict(is_transpose_dest=True)) for p in range(2)])
        sbuf_decl(nc, ctx, d, [
            ("st0", [128, E], BF16), ("st1", [128, E], BF16),
            ("xt0", [128, E], F32), ("xt1", [128, E], F32),
            ("ot0", [128, E], F32), ("ot1", [128, E], F32),
            ("sf0", [128, E], F32), ("sf1", [128, E], F32),
            ("obr", [1, E], F32), ("onesf", [128, 128], F32), ("obts", [128, E], F32)])
        d["bk"] = [ctx.enter_context(nc.psum_tensor(f"bk{i}", [128, 512], F32))
                   for i in range(2)]
        return
    bk = d["bk"]
    ev.dma("sync", "misc", lambda e: e.dma_start(out=d["obr"][:, :], in_=d["obT"].ap()),
           sig="ld_ob")
    ev.dma("sync", "misc", lambda e: e.dma_start(out=d["onesf"][:, :], in_=d["cones"].ap()),
           sig="ld_on")
    for hf in range(2):
        ev.do("tensor", lambda e, hf=hf: e.matmul(
            bk[hf][0:128, 0:512], d["onesf"][0:1, 0:128],
            d["obr"][0:1, 512 * hf:512 * (hf + 1)], start=True, stop=True),
            sig=f"obm{hf}", sem="tensor", waits=("ld_ob", "ld_on"))
        ev.do("scalar", lambda e, hf=hf: e.copy(
            d["obts"][:, 512 * hf:512 * (hf + 1)], bk[hf][0:128, 0:512]),
            sig=f"obe{hf}", sem="scalar", waits=(f"obm{hf}",))
    for tc in range(16):
        p = tc % 2
        gt, stt, xt, ot, sf = (d[f"gt{p}"], d[f"st{p}"], d[f"xt{p}"],
                               d[f"ot{p}"], d[f"sf{p}"])
        lw = (f"cmb{tc-2}",) if tc >= 2 else ()
        ev.dma("sync", "s", lambda e, tc=tc, stt=stt: e.dma_start(
            out=stt[:, :], in_=d["sred"].ap()[128 * tc:128 * (tc + 1), :]),
            sig=f"ls{tc}", waits=(f"sadd{tc-2}",) if tc >= 2 else ())
        for hf in range(2):
            ev.dma("sync", "g", lambda e, tc=tc, hf=hf, gt=gt: e.dma_start(
                out=gt[:, 512 * hf:512 * (hf + 1)],
                in_=d["gout"].ap()[512 * hf:512 * (hf + 1),
                                   128 * tc:128 * (tc + 1)], transpose=True),
                sig=f"lg{tc}_{hf}", waits=lw if hf == 0 else ())
        ev.dma("gpsimd", "x", lambda e, tc=tc, xt=xt: e.dma_start(
            out=xt[:, :], in_=d["x"].ap()[128 * tc:128 * (tc + 1), :]),
            sig=f"lx{tc}", waits=(f"cmb{tc-2}",) if tc >= 2 else ())
        ev.do("vector", lambda e, stt=stt, sf=sf: e.tensor_tensor(
            sf[:, :], stt[:, :], d["obts"][:, :], OP.add),
            sig=f"sadd{tc}", sem="vector",
            waits=(f"ls{tc}", "obe1") + ((f"mul{tc-2}",) if tc >= 2 else ()))
        ev.do("vector", lambda e, sf=sf, gt=gt: e.tensor_tensor(
            sf[:, :], sf[:, :], gt[:, :], OP.mult),
            sig=f"mul{tc}", sem="vector", waits=(f"sadd{tc}", f"lg{tc}_1"))
        ev.do("vector", lambda e, sf=sf, xt=xt, ot=ot: e.tensor_tensor(
            ot[:, :], sf[:, :], xt[:, :], OP.add),
            sig=f"cmb{tc}", sem="vector",
            waits=(f"mul{tc}", f"lx{tc}") + ((f"ost{tc-2}",) if tc >= 2 else ()))
        ev.dma("gpsimd", "o", lambda e, tc=tc, ot=ot: e.dma_start(
            out=d["out"].ap()[128 * tc:128 * (tc + 1), :], in_=ot[:, :]),
            sig=f"ost{tc}", waits=(f"cmb{tc}",))
    ev.do("gpsimd", waits=("ost15",))


class PEV:

    def __init__(self, ev, pfx, state, exclude=()):
        self.ev = ev
        self.pfx = pfx
        self.state = state
        self.pend = dict(ev.cnt)
        self.exclude = exclude
        self.fired = False

    @property
    def events(self):
        return self.ev.events

    def _pw(self, waits):
        return tuple((w if w is None else self.pfx + w) for w in waits)

    def _barrier(self, engine):
        if self.fired or engine != self.ev.active:
            return
        self.fired = True
        own = {"tensor": "tensor", "vector": "vector", "scalar": "scalar"}.get(engine)
        lw = self.state["lastwait"]
        for s, v in self.pend.items():
            if v > lw.get(s, 0) and s != own and s not in self.exclude:
                self.ev.eng.wait_ge(self.ev.sems[s], v)
                lw[s] = v

    def do(self, engine, emit=None, sig=None, sem=None, step=None, waits=()):
        self._barrier(engine)
        return self.ev.do(engine, emit,
                          sig=None if sig is None else self.pfx + sig,
                          sem=sem, step=step, waits=self._pw(waits))

    def dma(self, engine, chan, emit=None, sig=None, waits=()):
        self._barrier(engine)
        q = "dmas" if engine in ("sync", "scalar") else "dmag"
        key = (q, self.pfx + chan)
        cm = self.state["chanmap"]
        if key not in cm:
            cm[key] = str(self.state["chancnt"][q] % NDMACH)
            self.state["chancnt"][q] += 1
        return self.ev.dma(engine, cm[key], emit,
                           sig=None if sig is None else self.pfx + sig,
                           waits=self._pw(waits))

    def alias(self, new, old):
        self.ev.alias(self.pfx + new, self.pfx + old)


RG8 = [list(range(8))]


class NCNames:

    def __init__(self, nc, pfx):
        self._nc = nc
        self._pfx = pfx

    def sbuf_tensor(self, name, shape, dt, **kw):
        return self._nc.sbuf_tensor(self._pfx + name, shape, dt, **kw)

    def psum_tensor(self, name, shape, dt, **kw):
        return self._nc.psum_tensor(self._pfx + name, shape, dt, **kw)

    def __getattr__(self, a):
        return getattr(self._nc, a)


def body_fused(nc, ev, d, ctx, declare):
    if declare:
        share = {}
        share["x"] = nc.dram_tensor("x", [TSH, E], F32, kind="ExternalInput")
        share["xn"] = nc.dram_tensor("xn", [TSH, E], BF16, kind="Internal")
        share["xnag"] = nc.dram_tensor("xnag", [B * N, E], BF16, kind="Internal")
        share["vsig"] = nc.dram_tensor("vsig", [B, 3, 128, M], BF16, kind="Internal")
        share["usig"] = nc.dram_tensor("usig", [B, 3, 128, M], BF16, kind="Internal")
        share["asig"] = nc.dram_tensor("asig", [3, 128, M], BF16, kind="Internal")
        share["w"] = nc.dram_tensor("w", [B, 3, N, 128], BF16, kind="Internal")
        share["cfml"] = nc.dram_tensor("cfml", [NEXP, TSH], BF16, kind="Internal")
        share["cfma"] = nc.dram_tensor("cfma", [128, TSH], BF16, kind="Internal")
        share["spfg"] = nc.dram_tensor("spfg", [B * N, E], BF16, kind="Internal")
        share["spfs"] = nc.dram_tensor("spfs", [B * N, E], BF16, kind="Internal")
        share["spart"] = share["spfs"]
        share["spfrg"] = nc.dram_tensor("spfrg", [TSH, E], BF16, kind="Internal")
        share["spfrs"] = nc.dram_tensor("spfrs", [TSH, E], BF16, kind="Internal")
        share["xr"] = nc.dram_tensor("xr", [TSH, E], F32, kind="ExternalInput")
        share["cones"] = nc.dram_tensor("c_ones", [128, 128], F32, kind="ExternalInput")
        share["cident"] = nc.dram_tensor("c_ident", [128, 128], F32, kind="ExternalInput")
        seeds = {
            "p1": ("x", "xn"),
            "p4r": ("xn", "cident", "cfml"),
            "p2": ("xnag", "vsig", "usig"),
            "p2b": ("cones", "asig"),
            "p4e": ("xnag", "cfma"),
            "p3": ("vsig", "usig", "asig", "w"),
            "p5": ("w", "spart"),
            "p6": ("x", "cones"),
        }
        d["_phases"] = []
        pctx5 = None
        for pfx, body in [("p1", body_L1), ("p4r", body_L4r), ("p2", body_L2),
                          ("p2b", body_L2b), ("p4e", body_L4e), ("p3", body_L3),
                          ("p5", body_L5o), ("p6", body_L6n)]:
            dp = {k: share[k] for k in seeds[pfx]}
            if pfx == "p4e":
                dp["spf"] = share["spfg"]
            if pfx == "p6":
                dp["spfrs"] = share["spfrs"]
                dp["spfrg"] = share["spfrg"]
                dp["x"] = share["xr"]
            pctx = ExitStack()
            body(NCNames(nc, pfx + "_"), None, dp, pctx, True)
            if pfx == "p5":
                pctx5 = pctx
            elif pfx == "p6":
                pctx.close()
                pctx5.close()
            else:
                pctx.close()
            d["_phases"].append((pfx, body, dp))
        d["_share"] = share
        return

    share = d["_share"]
    state = {"lastwait": {}, "chanmap": {}, "chancnt": {"dmas": 0, "dmag": 0}}
    pev6 = None
    for pfx, body, dp in d["_phases"]:
        if pfx == "p5":
            pev6 = PEV(ev, "p6:", state)

            def rs_hook(st, ev=ev, share=share):
                if st % 8 != 7:
                    return
                c = st // 8
                ev.do("gpsimd", lambda e, c=c: e.collective_compute(
                    "ReduceScatter", OP.add, replica_groups=RG8,
                    ins=[share["spfs"].ap()[4096 * c:4096 * (c + 1), :].opt()],
                    outs=[share["spfrs"].ap()[512 * c:512 * (c + 1), :].opt()]),
                    sig=f"rss{c}", sem="tb", waits=(f"p5:sst{st}_3",))
            dp["_rs_hook"] = rs_hook
        with nc.named_scope(pfx):
            pev = pev6 if pfx == "p6" else PEV(ev, pfx + ":", state,
                                              exclude=("tb",) if pfx in ("p4r", "p3", "p5") else ())
            body(nc, pev, dp, None, False)
        if pfx == "p4e":
            with nc.named_scope("rsg"):
                for c in range(4):
                    ev.do("gpsimd", lambda e, c=c: e.collective_compute(
                        "ReduceScatter", OP.add, replica_groups=RG8,
                        ins=[share["spfg"].ap()[4096 * c:4096 * (c + 1), :].opt()],
                        outs=[share["spfrg"].ap()[512 * c:512 * (c + 1), :].opt()]),
                        sig=f"rsg{c}", sem="tb", waits=("p4e:fst31_3",))
        if pfx == "p5":
            for c in range(4):
                ev.alias(f"p6:rss{c}", f"rss{c}")
                ev.alias(f"p6:rsg{c}", f"rsg{c}")
        if pfx == "p1":
            with nc.named_scope("ag"):
                cev = PEV(ev, "ag:", state)
                cev.do("gpsimd", lambda e: e.collective_compute(
                    "AllGather", OP.bypass, replica_groups=RG8,
                    ins=[share["xn"].ap().opt()], outs=[share["xnag"].ap().opt()]),
                    sig="done", sem="tb")
        if pfx == "p4r":
            with nc.named_scope("agc"):
                cev = PEV(ev, "agc:", state)
                cev.do("gpsimd", lambda e: e.collective_compute(
                    "AllGather", OP.bypass, replica_groups=RG8,
                    ins=[share["cfml"].ap().opt()], outs=[share["cfma"].ap().opt()]),
                    sig="done", sem="tb")


def body_L6n(nc, ev, d, ctx, declare):
    if declare:
        ddecl(nc, d, "spfrs", "spfrs", [TSH, E], BF16, "ExternalInput")
        ddecl(nc, d, "spfrg", "spfrg", [TSH, E], BF16, "ExternalInput")
        ddecl(nc, d, "x", "x", [TSH, E], F32, "ExternalInput")
        ddecl(nc, d, "obT", "obT", [1, E], F32, "ExternalInput")
        ddecl(nc, d, "cones", "c_ones", [128, 128], F32, "ExternalInput")
        d["out"] = nc.dram_tensor("out", [TSH, E], F32, kind="ExternalOutput")
        sbuf_decl(nc, ctx, d, [
            ("sft0", [128, 2 * E], BF16), ("sft1", [128, 2 * E], BF16),
            ("xt0", [128, E], F32), ("xt1", [128, E], F32),
            ("ot0", [128, E], F32), ("ot1", [128, E], F32),
            ("gtf0", [128, E], F32), ("gtf1", [128, E], F32),
            ("sfw0", [128, E], F32), ("sfw1", [128, E], F32),
            ("obr", [1, E], F32), ("onesf", [128, 128], F32), ("obts", [128, E], F32)])
        d["bk"] = [ctx.enter_context(nc.psum_tensor(f"bk{i}", [128, 512], F32))
                   for i in range(2)]
        return
    bk = d["bk"]
    ev.dma("sync", "misc", lambda e: e.dma_start(out=d["obr"][:, :], in_=d["obT"].ap()),
           sig="ld_ob")
    ev.dma("sync", "misc", lambda e: e.dma_start(out=d["onesf"][:, :], in_=d["cones"].ap()),
           sig="ld_on")
    for hf in range(2):
        ev.do("tensor", lambda e, hf=hf: e.matmul(
            bk[hf][0:128, 0:512], d["onesf"][0:1, 0:128],
            d["obr"][0:1, 512 * hf:512 * (hf + 1)], start=True, stop=True),
            sig=f"obm{hf}", sem="tensor", waits=("ld_ob", "ld_on"))
        ev.do("scalar", lambda e, hf=hf: e.copy(
            d["obts"][:, 512 * hf:512 * (hf + 1)], bk[hf][0:128, 0:512]),
            sig=f"obe{hf}", sem="scalar", waits=(f"obm{hf}",))
    for tc in range(16):
        p = tc % 2
        sft, xt, ot, gtf, sfw = (d[f"sft{p}"], d[f"xt{p}"], d[f"ot{p}"],
                                 d[f"gtf{p}"], d[f"sfw{p}"])
        lw = [f"rss{tc // 4}", f"rsg{tc // 4}"] if tc % 4 == 0 else []
        if tc >= 2:
            lw += [f"sadd{tc-2}", f"gsil{tc-2}"]
        ev.dma("sync", "s", lambda e, tc=tc, sft=sft: e.dma_start(
            out=sft[:, 0:E], in_=d["spfrs"].ap()[128 * tc:128 * (tc + 1), :]),
            sig=f"lss{tc}", waits=tuple(lw))
        ev.dma("sync", "s", lambda e, tc=tc, sft=sft: e.dma_start(
            out=sft[:, E:2 * E], in_=d["spfrg"].ap()[128 * tc:128 * (tc + 1), :]),
            sig=f"ls{tc}")
        ev.dma("gpsimd", "x", lambda e, tc=tc, xt=xt: e.dma_start(
            out=xt[:, :], in_=d["x"].ap()[128 * tc:128 * (tc + 1), :]),
            sig=f"lx{tc}", waits=(f"cmb{tc-2}",) if tc >= 2 else ())
        ev.do("scalar", lambda e, sft=sft, gtf=gtf: e.activation(
            gtf[:, :], sft[:, E:2 * E], AF.Silu),
            sig=f"gsil{tc}", sem="scalar",
            waits=(f"ls{tc}",) + ((f"mul{tc-2}",) if tc >= 2 else ()))
        ev.do("vector", lambda e, sft=sft, sfw=sfw: e.tensor_tensor(
            sfw[:, :], sft[:, 0:E], d["obts"][:, :], OP.add),
            sig=f"sadd{tc}", sem="vector",
            waits=(f"ls{tc}", "obe1") + ((f"mul{tc-2}",) if tc >= 2 else ()))
        ev.do("vector", lambda e, sfw=sfw, gtf=gtf: e.tensor_tensor(
            sfw[:, :], sfw[:, :], gtf[:, :], OP.mult),
            sig=f"mul{tc}", sem="vector", waits=(f"sadd{tc}", f"gsil{tc}"))
        ev.do("vector", lambda e, sfw=sfw, xt=xt, ot=ot: e.tensor_tensor(
            ot[:, :], sfw[:, :], xt[:, :], OP.add),
            sig=f"cmb{tc}", sem="vector",
            waits=(f"mul{tc}", f"lx{tc}") + ((f"ost{tc-2}",) if tc >= 2 else ()))
        ev.dma("gpsimd", "o", lambda e, tc=tc, ot=ot: e.dma_start(
            out=d["out"].ap()[128 * tc:128 * (tc + 1), :], in_=ot[:, :]),
            sig=f"ost{tc}", waits=(f"cmb{tc}",))
    ev.do("gpsimd", waits=("ost15",))


def fused_inmaps(ins):
    kc = conv_consts()
    ones128 = np.ones((128, 128), np.float32)
    C8 = list(range(NCORE))
    x = ins["L0_attn_x"]
    rwT = np.ascontiguousarray(ins["L0_ffn_router_w"].T)
    corr = np.ascontiguousarray(ins["L0_ffn_correction_bias"][:, None])
    sgwT = np.ascontiguousarray(ins["L0_ffn_shared_gate_w"].T)
    suwT = np.ascontiguousarray(ins["L0_ffn_shared_up_w"].T)
    sdw = ins["L0_ffn_shared_down_w"].T
    obT = np.ascontiguousarray(ins["L0_attn_o_b"][None, :])

    def fsel(h):
        f = np.zeros((8, 2, 128, 128), np.float32)
        for j in range(8):
            for el in range(2):
                f[j, el, 16 * j + 2 * h + el, :] = 1.0
        return f.astype(ml_dtypes.bfloat16)
    xf = np.ascontiguousarray(x.reshape(B * N, E))
    im = []
    for k in C8:
        b, th, h = k // 2, k % 2, k
        im.append({
            "x": np.ascontiguousarray(x[b, 2048 * th:2048 * (th + 1), :]),
            "xr": np.ascontiguousarray(np.concatenate(
                [xf[4096 * c + 512 * k:4096 * c + 512 * k + 512] for c in range(4)],
                axis=0)),
            "uw": np.ascontiguousarray(ins["L0_attn_u_w"][:, DH * h:DH * (h + 1)]),
            "vw": np.ascontiguousarray(ins["L0_attn_v_w"][:, DH * h:DH * (h + 1)]),
            "ub": np.ascontiguousarray(ins["L0_attn_u_b"][DH * h:DH * (h + 1), None]),
            "vb": np.ascontiguousarray(ins["L0_attn_v_b"][DH * h:DH * (h + 1), None]),
            "posw": ins["L0_attn_pos_w"],
            "posb": np.ascontiguousarray(ins["L0_attn_pos_b"][:, None]),
            "lw0": ins["L0_attn_lw0"],
            "lb0": np.ascontiguousarray(ins["L0_attn_lb0"][:, None]),
            "lw1": ins["L0_attn_lw1"],
            "lb1": np.ascontiguousarray(ins["L0_attn_lb1"][:, None]),
            "lw2": ins["L0_attn_lw2"],
            "lb2": np.ascontiguousarray(ins["L0_attn_lb2"][:, None]),
            "outw": np.ascontiguousarray(ins["L0_attn_out_w"][:, DH * h:DH * (h + 1)]),
            "outb": np.ascontiguousarray(ins["L0_attn_out_b"][DH * h:DH * (h + 1), None]),
            "c_ramp": kc["c_ramp"], "c_decay": kc["c_decay"], "c_ones": ones128,
            "c_d1mov": kc["c_d1mov"], "c_wf": kc["c_wf"], "c_e2": kc["c_e2"],
            "c_tinv": kc["c_tinv"], "c_cstk": kc["c_cstk"],
            "rwT": rwT, "corr": corr,
            "gwh": np.ascontiguousarray(ins["L0_ffn_gate_w"][2 * h:2 * h + 2]),
            "uwh": np.ascontiguousarray(ins["L0_ffn_up_w"][2 * h:2 * h + 2]),
            "dwh": np.ascontiguousarray(ins["L0_ffn_down_w"][2 * h:2 * h + 2]),
            "sgwh": np.ascontiguousarray(sgwT[:, HR * h:HR * (h + 1)]),
            "suwh": np.ascontiguousarray(suwT[:, HR * h:HR * (h + 1)]),
            "sdwh": np.ascontiguousarray(sdw[HR * h:HR * (h + 1), :]),
            "fselb": fsel(h),
            "c_ident": kc["c_ident"],
            "owh": np.ascontiguousarray(ins["L0_attn_o_w"][DH * h:DH * (h + 1), :]),
            "obT": obT,
        })
    return im


def kernel(**inputs):
    ins = {k: np.asarray(v) for k, v in inputs.items()}
    C8 = list(range(NCORE))
    nc = build_program(body_fused)
    im = fused_inmaps(ins)
    res = run_bass_kernel_spmd(nc, im, core_ids=C8)
    outs = [np.asarray(res.results[k]["out"], np.float32) for k in C8]
    outg = np.empty((B * N, E), np.float32)
    for k in C8:
        for c in range(4):
            outg[4096 * c + 512 * k:4096 * c + 512 * k + 512] = \
                outs[k][512 * c:512 * (c + 1)]
    return outg.reshape(B, N, E)


def kernel_multi(**inputs):
    ins = {k: np.asarray(v) for k, v in inputs.items()}
    kc = conv_consts()
    ones128 = np.ones((128, 128), np.float32)
    C8 = list(range(NCORE))
    x = ins["L0_attn_x"]

    nc1 = build_program(body_L1)
    im = [{"x": np.ascontiguousarray(x[k // 2, 2048 * (k % 2):2048 * (k % 2 + 1)])}
          for k in C8]
    r1 = run_bass_kernel_spmd(nc1, im, core_ids=C8)
    xnag = np.concatenate([np.asarray(r1.results[k]["xn"]) for k in C8], axis=0)

    nc2 = build_program(body_L2)
    im = [{"xnag": xnag,
           "uw": np.ascontiguousarray(ins["L0_attn_u_w"][:, DH * h:DH * (h + 1)]),
           "vw": np.ascontiguousarray(ins["L0_attn_v_w"][:, DH * h:DH * (h + 1)]),
           "ub": np.ascontiguousarray(ins["L0_attn_u_b"][DH * h:DH * (h + 1), None]),
           "vb": np.ascontiguousarray(ins["L0_attn_v_b"][DH * h:DH * (h + 1), None])}
          for h in C8]
    r2 = run_bass_kernel_spmd(nc2, im, core_ids=C8)

    nc2b = build_program(body_L2b)
    im = [{"posw": ins["L0_attn_pos_w"],
           "posb": np.ascontiguousarray(ins["L0_attn_pos_b"][:, None]),
           "lw0": ins["L0_attn_lw0"], "lb0": np.ascontiguousarray(ins["L0_attn_lb0"][:, None]),
           "lw1": ins["L0_attn_lw1"], "lb1": np.ascontiguousarray(ins["L0_attn_lb1"][:, None]),
           "lw2": ins["L0_attn_lw2"], "lb2": np.ascontiguousarray(ins["L0_attn_lb2"][:, None]),
           "outw": np.ascontiguousarray(ins["L0_attn_out_w"][:, DH * h:DH * (h + 1)]),
           "outb": np.ascontiguousarray(ins["L0_attn_out_b"][DH * h:DH * (h + 1), None]),
           "c_ramp": kc["c_ramp"], "c_decay": kc["c_decay"], "c_ones": ones128}
          for h in C8]
    r2b = run_bass_kernel_spmd(nc2b, im, core_ids=C8)

    nc3 = build_program(body_L3)
    im = [{"vsig": np.asarray(r2.results[h]["vsig"]),
           "usig": np.asarray(r2.results[h]["usig"]),
           "asig": np.asarray(r2b.results[h]["asig"]),
           "c_d1mov": kc["c_d1mov"], "c_wf": kc["c_wf"], "c_e2": kc["c_e2"],
           "c_tinv": kc["c_tinv"], "c_cstk": kc["c_cstk"]}
          for h in C8]
    r3 = run_bass_kernel_spmd(nc3, im, core_ids=C8)

    nc4a = build_program(body_L4a)
    rwT = np.ascontiguousarray(ins["L0_ffn_router_w"].T)
    corr = np.ascontiguousarray(ins["L0_ffn_correction_bias"][:, None])
    sgwT = np.ascontiguousarray(ins["L0_ffn_shared_gate_w"].T)
    suwT = np.ascontiguousarray(ins["L0_ffn_shared_up_w"].T)
    sdwT = np.ascontiguousarray(ins["L0_ffn_shared_down_w"].T)
    im = [{"xn": np.ascontiguousarray(xnag[2048 * k:2048 * (k + 1)]),
           "rwT": rwT, "corr": corr,
           "gw": ins["L0_ffn_gate_w"], "uw": ins["L0_ffn_up_w"],
           "dw": ins["L0_ffn_down_w"],
           "sgwT": sgwT, "suwT": suwT, "sdwT": sdwT,
           "c_ident": kc["c_ident"], "c_ones": ones128, "c_sel": kc["c_sel"]}
          for k in C8]
    r4a = run_bass_kernel_spmd(nc4a, im, core_ids=C8)

    nc4b = build_program(body_L4b)
    wh = [np.asarray(r3.results[h]["w"]) for h in C8]
    ob = np.ascontiguousarray(ins["L0_attn_o_b"][:, None])
    im = []
    for k in C8:
        b, th = k // 2, k % 2
        wsl = np.stack([wh[h][b, :, 2048 * th:2048 * (th + 1), :] for h in C8], axis=0)
        im.append({"wsl": np.ascontiguousarray(wsl),
                   "g": np.asarray(r4a.results[k]["gout"]),
                   "x": np.ascontiguousarray(x[b, 2048 * th:2048 * (th + 1), :]),
                   "ow": ins["L0_attn_o_w"], "ob": ob, "c_ident": kc["c_ident"]})
    r4b = run_bass_kernel_spmd(nc4b, im, core_ids=C8)
    outs = [np.asarray(r4b.results[k]["out"], np.float32) for k in C8]
    return np.stack([np.concatenate([outs[2 * b], outs[2 * b + 1]], axis=0)
                     for b in range(4)], axis=0)
